# version 01
# baseline (speedup 1.0000x reference)
import numpy as np
from contextlib import ExitStack

import concourse.bass as bass
import concourse.mybir as mybir
from concourse.bass_utils import run_bass_kernel_spmd

F32 = mybir.dt.float32
BF16 = mybir.dt.bfloat16
AF = mybir.ActivationFunctionType
ALU = mybir.AluOpType
AXL = mybir.AxisListType

T = 2048
C = 2048
NT = 16
KB = 16
NG = 16
L = 64
NCH = 32
WTOK = 1024
WT = 8
EPS = 1e-5
CLAMP = 43.0
NE, NR = 8, 16
ALPHA = 16.0

S_ZW, S_LAM, S_U, S_T1, S_T2 = 0, 2048, 4096, 6144, 8192
B_K, B_EK, B_RA, B_RI, B_V, B_VT, B_KT, B_KAP = (
    0, 2048, 4096, 6144, 8192, 10240, 12288, 14336)


class KBd:

    ENGINES = ("sync", "gpsimd", "tensor", "vector", "scalar")

    def __init__(self, sems):
        self.sems = sems
        self.cnt = {k: 0 for k in sems}
        self.ops = {e: [] for e in self.ENGINES}

    def emit(self, eng, fn):
        self.ops[eng].append(fn)

    def wait(self, eng, ch, val=None):
        v = self.cnt[ch] if val is None else val
        if v <= 0:
            return
        h = self.sems[ch]
        self.emit(eng, lambda e: e.wait_ge(h, v))

    def evt(self, ch):
        return (ch, self.cnt[ch])

    def wait_evt(self, eng, evt):
        if evt is None:
            return
        ch, v = evt
        if ch in ("LD", "LG"):
            v = self.cnt[ch]
        if v <= 0:
            return
        h = self.sems[ch]
        self.emit(eng, lambda e: e.wait_ge(h, v))

    def dma(self, qeng, out, in_, ch, **kw):
        h = self.sems[ch]
        self.cnt[ch] += 16
        self.emit(qeng, lambda e: e.dma_start(out=out, in_=in_, **kw)
                  .then_inc(h, 16))
        return (ch, self.cnt[ch])


def build_program(dbg=None, upto=99,
                  p5=("proj", "prep", "tp", "chunks", "cy", "cu", "cs")):
    nc = bass.Bass("TRN2", target_bir_lowering=False,
                   detect_race_conditions=False)

    dt_in = lambda n, s: nc.dram_tensor(n, list(s), F32, kind="ExternalInput").ap()
    x_in = dt_in("x", (T, C))
    wkT = dt_in("WkT", (C, C))
    wvT = dt_in("WvT", (C, C))
    wrT = dt_in("WrT", (C, C))
    woT = dt_in("WoT", (C, C))
    wfT = dt_in("WfT", (C, C))
    wwT = dt_in("WwT", (C, C))
    wx = dt_in("Wx", (C, C))
    atg = dt_in("ATg", (C, NE * NR + NE))
    btv = dt_in("BT", (NE * NR, C))
    lns_i = dt_in("lns", (128, 16))
    lnb_i = dt_in("lnb", (128, 16))
    bw_i = dt_in("bw", (128, 16))
    m0v = dt_in("m0", (128, 1))
    m1v = dt_in("m1", (128, 1))
    eyev = dt_in("eye", (128, 128))
    amaskv = dt_in("amask", (128, 512))

    out_t = nc.dram_tensor("out", [WTOK, C], F32, kind="ExternalOutput").ap()
    dbg_t = None
    if dbg is not None:
        dbg_t = nc.dram_tensor("dbg", list(dbg[1]), F32, kind="ExternalOutput").ap()

    x1s = nc.dram_tensor("x1s", [T, C], F32, kind="Internal").ap()
    mts = nc.dram_tensor("mts", [C, C], BF16, kind="Internal").ap()
    outss = nc.dram_tensor("outss", [C, T], BF16, kind="Internal").ap()

    with ExitStack() as st:
        sb = lambda n, s, d: st.enter_context(nc.sbuf_tensor(n, list(s), d))
        psm = lambda n, s, d: st.enter_context(nc.psum_tensor(n, list(s), d))
        actf = sb("actf", (128, 32768), BF16)
        f32p = sb("f32p", (128, 10240), F32)
        bbuf = sb("bbuf", (128, 18432), BF16)
        ttf = sb("ttf", (128, 2048), BF16)
        btb = sb("btb", (128, 2048), BF16)
        atgb = sb("atgb", (128, 2176), BF16)
        eyeb = sb("eyeb", (128, 128), BF16)
        amask = sb("amask_s", (128, 512), F32)
        ztok = sb("ztok", (128, 2048), BF16)
        wslc = sb("wslc", (128, 4096), BF16)
        asb = sb("asb", (128, 512), BF16)
        ogb = sb("ogb", (128, 512), BF16)
        lncol = sb("lncol", (128, 48), F32)
        mcols = sb("mcols", (128, 2), F32)
        stat = sb("stat", (128, 160), F32)
        ends = sb("ends", (128, 33), F32)
        mcb = sb("mcb", (128, 32), F32)
        eksb = sb("eksb", (128, 32), F32)
        edec = sb("edec", (128, 32), F32)
        s0b = sb("s0b", (128, 128), BF16)
        s1b = sb("s1b", (128, 128), BF16)
        tpt = sb("tpt", (128, 64), F32)
        mstg = sb("mstg", (128, 1024), BF16)
        xstg = sb("xstg", (128, 4096), F32)

        pp = psm("pp", (128, 2048), F32)
        pa = psm("pa", (128, 512), F32)
        py = psm("py", (128, 512), F32)
        ptb = psm("ptb", (128, 1024), BF16)
        pu = psm("pu", (128, 512), F32)

        sems = {}
        for name in ("LD", "LG", "PE", "VE", "SE"):
            sems[name] = st.enter_context(nc.semaphore(name))
        b = KBd(sems)
        S = sems

        def mm(outap, lhs, rhs, start, stop, inc=False, tp=None, sgc=False):
            h = S["PE"]
            if inc:
                b.cnt["PE"] += 1
            def f(e, outap=outap, lhs=lhs, rhs=rhs, start=start, stop=stop,
                  inc=inc, tp=tp, sgc=sgc):
                i = e.matmul(outap, lhs, rhs, start=start, stop=stop,
                             tile_position=tp, skip_group_check=sgc)
                if inc:
                    i.then_inc(h, 1)
            b.emit("tensor", f)

        def tpose(outap, inap, inc=False, tp=None):
            h = S["PE"]
            if inc:
                b.cnt["PE"] += 1
            def f(e, outap=outap, inap=inap, inc=inc, tp=tp):
                i = e.transpose(outap, inap, eyeb[:, :], tile_position=tp)
                if inc:
                    i.then_inc(h, 1)
            b.emit("tensor", f)

        def vop(fn, inc=True):
            h = S["VE"]
            prev = b.cnt["VE"]
            b.cnt["VE"] += 1
            def f(e, fn=fn, prev=prev):
                if prev > 0:
                    e.wait_ge(h, prev)
                fn(e).then_inc(h, 1)
            b.emit("vector", f)

        def sop(fn, inc=False):
            h = S["SE"]
            if inc:
                b.cnt["SE"] += 1
            def f(e, fn=fn, inc=inc):
                i = fn(e)
                if inc:
                    i.then_inc(h, 1)
            b.emit("scalar", f)

        def bcast_ap(t2d, ncols, rep, coloff=0):
            a = t2d[:, coloff:coloff + ncols]
            return bass.AP(tensor=a.tensor, offset=a.offset,
                           ap=[a.ap[0], [a.ap[1][0], ncols], [0, rep]])

        def colap(t2d, j):
            return t2d[:, j:j + 1]

        dbg_rows = [0]
        def dbg_store(name, sbap, nrows=128):
            if dbg is None or dbg[0] != name:
                return
            r = dbg_rows[0]
            b.wait("gpsimd", "VE")
            b.wait("gpsimd", "SE")
            b.wait("gpsimd", "PE")
            b.dma("gpsimd", dbg_t[r:r + nrows, :], sbap, "LG")
            b.wait("gpsimd", "LG")
            dbg_rows[0] += nrows

        def top2_gate(src_logits_ap):
            vop(lambda e: e.reduce_max(tpt[:, 0:1], tpt[:, 8:16], axis=AXL.X))
            vop(lambda e: e.tensor_scalar(
                out=tpt[:, 16:24], in0=tpt[:, 8:16], scalar1=tpt[:, 0:1],
                scalar2=None, op0=ALU.is_equal))
            vop(lambda e: e.scalar_tensor_tensor(
                out=tpt[:, 24:32], in0=tpt[:, 16:24], scalar=-1e30,
                in1=tpt[:, 8:16], op0=ALU.mult, op1=ALU.add))
            vop(lambda e: e.reduce_max(tpt[:, 1:2], tpt[:, 24:32], axis=AXL.X))
            vop(lambda e: e.tensor_scalar(
                out=tpt[:, 32:40], in0=tpt[:, 24:32], scalar1=tpt[:, 1:2],
                scalar2=None, op0=ALU.is_equal))
            vop(lambda e: e.tensor_tensor(
                out=tpt[:, 2:3], in0=tpt[:, 0:1], in1=tpt[:, 1:2],
                op=ALU.subtract), inc=True)
            ev_d = b.evt("VE")
            b.wait_evt("scalar", ev_d)
            sop(lambda e: e.activation(out=tpt[:, 3:4], in_=tpt[:, 2:3],
                                       func=AF.Exp, scale=-1.0))
            sop(lambda e: e.activation(out=tpt[:, 3:4], in_=tpt[:, 3:4],
                                       func=AF.Ln, bias=1.0))
            sop(lambda e: e.activation(out=tpt[:, 4:5], in_=tpt[:, 3:4],
                                       func=AF.Exp, scale=-1.0), inc=True)
            ev_w1 = b.evt("SE")
            b.wait_evt("vector", ev_w1)
            vop(lambda e: e.tensor_scalar(
                out=tpt[:, 5:6], in0=tpt[:, 4:5], scalar1=-1.0, scalar2=1.0,
                op0=ALU.mult, op1=ALU.add))
            vop(lambda e: e.tensor_scalar(
                out=tpt[:, 16:24], in0=tpt[:, 16:24], scalar1=tpt[:, 4:5],
                scalar2=None, op0=ALU.mult))
            vop(lambda e: e.tensor_scalar(
                out=tpt[:, 32:40], in0=tpt[:, 32:40], scalar1=tpt[:, 5:6],
                scalar2=None, op0=ALU.mult))
            vop(lambda e: e.tensor_tensor(
                out=tpt[:, 16:24], in0=tpt[:, 16:24], in1=tpt[:, 32:40],
                op=ALU.add))
            vop(lambda e: e.tensor_scalar(
                out=tpt[:, 16:24], in0=tpt[:, 16:24], scalar1=ALPHA,
                scalar2=None, op0=ALU.mult))
            for ex in range(NE):
                vop(lambda e, ex=ex: e.tensor_scalar(
                    out=ogb[:, ex * 16:(ex + 1) * 16],
                    in0=ogb[:, ex * 16:(ex + 1) * 16],
                    scalar1=colap(tpt, 16 + ex), scalar2=None, op0=ALU.mult),
                    inc=(ex == NE - 1))
            return b.evt("VE")

        def ln_batch_math(s1a, s1b_, s2a, s2b, mo, ro, n, extra_scale=None,
                          extra_eps=EPS):
            for i in range(n):
                if s1b_ is not None:
                    vop(lambda e, i=i: e.tensor_tensor(
                        out=colap(stat, s1a + i), in0=colap(stat, s1a + i),
                        in1=colap(stat, s1b_ + i), op=ALU.add))
                    vop(lambda e, i=i: e.tensor_tensor(
                        out=colap(stat, s2a + i), in0=colap(stat, s2a + i),
                        in1=colap(stat, s2b + i), op=ALU.add))
                vop(lambda e, i=i: e.tensor_scalar(
                    out=colap(stat, mo + i), in0=colap(stat, s1a + i),
                    scalar1=1.0 / C, scalar2=None, op0=ALU.mult))
                vop(lambda e, i=i: e.tensor_scalar(
                    out=colap(stat, ro + i), in0=colap(stat, s2a + i),
                    scalar1=1.0 / C, scalar2=None, op0=ALU.mult))
                vop(lambda e, i=i: e.tensor_tensor(
                    out=tpt[:, 0:1], in0=colap(stat, mo + i),
                    in1=colap(stat, mo + i), op=ALU.mult))
                vop(lambda e, i=i: e.tensor_tensor(
                    out=colap(stat, ro + i), in0=colap(stat, ro + i),
                    in1=tpt[:, 0:1], op=ALU.subtract),
                    inc=False)
                if extra_scale is None:
                    vop(lambda e, i=i: e.tensor_scalar(
                        out=colap(stat, ro + i), in0=colap(stat, ro + i),
                        scalar1=extra_eps, scalar2=None, op0=ALU.add),
                        inc=True)
                if extra_scale is not None:
                    vop(lambda e, i=i: e.tensor_scalar(
                        out=colap(stat, ro + i), in0=colap(stat, ro + i),
                        scalar1=extra_scale, scalar2=extra_eps, op0=ALU.mult,
                        op1=ALU.add), inc=True)
            ev_v = b.evt("VE")
            b.wait_evt("scalar", ev_v)
            for i in range(n):
                sop(lambda e, i=i: e.activation(
                    out=tpt[:, 1:2], in_=colap(stat, ro + i), func=AF.Ln))
                sop(lambda e, i=i: e.activation(
                    out=colap(stat, ro + i), in_=tpt[:, 1:2], func=AF.Exp,
                    scale=-0.5), inc=(i == n - 1))
            ev_s = b.evt("SE")
            b.wait_evt("vector", ev_s)
            for i in range(n):
                vop(lambda e, i=i: e.tensor_tensor(
                    out=colap(stat, mo + i), in0=colap(stat, mo + i),
                    in1=colap(stat, ro + i), op=ALU.mult))
                vop(lambda e, i=i: e.tensor_scalar(
                    out=colap(stat, mo + i), in0=colap(stat, mo + i),
                    scalar1=-1.0, scalar2=None, op0=ALU.mult),
                    inc=(i == n - 1))
            return b.evt("VE")

        ptb_evac = [None]

        def transpose_batch8(srcs, dst3ap):
            nsl = len(srcs)
            b.wait_evt("tensor", ptb_evac[0])
            for sl, src in enumerate(srcs):
                tpose(ptb[:, sl * 128:(sl + 1) * 128], src,
                      inc=(sl == nsl - 1))
            ev_tp = b.evt("PE")
            b.wait_evt("vector", ev_tp)
            vop(lambda e, dst3ap=dst3ap, nsl=nsl: e.tensor_copy(
                out=dst3ap, in_=ptb[:, 0:nsl * 128]
                .rearrange("p (k n) -> p k n", n=128)), inc=True)
            ptb_evac[0] = b.evt("VE")

        def transpose_tile_set(src_tok, i, base, width):
            for half8 in range(2):
                k0 = half8 * 8
                srcs = [src_tok[:, k * 128:(k + 1) * 128]
                        for k in range(k0, k0 + 8)]
                dst = bass.AP(
                    tensor=actf[:, :].tensor,
                    offset=actf[:, :].offset + (k0 * 2048 + base + i * 128),
                    ap=[actf[:, :].ap[0], [2048, 8], [1, 128]])
                transpose_batch8(srcs, dst)

        b.dma("gpsimd", eyeb[:, :], eyev, "LG")
        b.dma("sync", amask[:, :], amaskv, "LD")
        b.dma("sync", lncol[:, 0:16], lns_i, "LD")
        b.dma("sync", lncol[:, 16:32], lnb_i, "LD")
        b.dma("sync", lncol[:, 32:48], bw_i, "LD")
        b.dma("sync", mcols[:, 0:1], m0v, "LD")
        b.dma("sync", mcols[:, 1:2], m1v, "LD")
        b.dma("gpsimd", btb[:, :], btv, "LG")
        e_atg = b.dma("gpsimd", atgb[:, :].rearrange("p (k n) -> p k n", n=136),
                      atg.rearrange("(k p) n -> p k n", p=128), "LG")
        e_wx = b.dma("gpsimd", actf[:, :].rearrange("p (k n) -> p k n", n=2048),
                     wx.rearrange("(k p) n -> p k n", p=128), "LG")
        ev_consts_ld = b.evt("LD")

        ev_m_stored = None
        ev_pe_m = None
        if upto >= 1:
            bank_evac = {0: None, 1: None}
            for ebank in range(4):
                ewl = b.dma("gpsimd",
                            bbuf[:, 0:8192].rearrange("p (k n) -> p k n", n=512),
                            wwT[:, ebank * 512:(ebank + 1) * 512]
                            .rearrange("(k p) n -> p k n", p=128), "LG")
                b.wait_evt("tensor", ewl)
                if ebank == 0:
                    b.wait_evt("tensor", e_wx)
                for Ct in range(16):
                    pb = Ct % 2
                    b.wait_evt("tensor", bank_evac[pb])
                    for K in range(16):
                        mm(pp[:, pb * 512:(pb + 1) * 512],
                           actf[:, K * 2048 + Ct * 128:
                                K * 2048 + Ct * 128 + 128],
                           bbuf[:, K * 512:(K + 1) * 512],
                           start=(K == 0), stop=(K == 15), inc=(K == 15))
                    ev_pe = b.evt("PE")
                    stg = mstg[:, pb * 512:pb * 512 + 512]
                    b.wait_evt("scalar", ev_pe)
                    b.wait("scalar", "LD")
                    sop(lambda e, stg=stg, pb=pb: e.activation(
                        out=stg, in_=pp[:, pb * 512:pb * 512 + 512],
                        func=AF.Copy), inc=True)
                    ev_se = b.evt("SE")
                    bank_evac[pb] = ev_se
                    b.wait_evt("sync", ev_se)
                    b.dma("sync", mts[Ct * 128:(Ct + 1) * 128,
                                      ebank * 512:(ebank + 1) * 512], stg, "LD")
                ev_pe_m = b.evt("PE")
                b.wait_evt("gpsimd", ev_pe_m)
            ev_m_stored = b.evt("LD")

        if upto >= 2:
            for i in range(NT):
                xs = xstg[:, (i % 2) * 2048:(i % 2) * 2048 + 2048]
                b.wait("gpsimd", "VE")
                b.wait("gpsimd", "SE")
                exl = b.dma("gpsimd", xs, x_in[i * 128:(i + 1) * 128, :], "LG")
                b.wait_evt("vector", exl)
                vop(lambda e, xs=xs, i=i: e.reduce_sum(
                    colap(stat, i), xs, axis=AXL.X), inc=True)
                b.wait_evt("scalar", exl)
                sop(lambda e, xs=xs, i=i: e.activation(
                    out=f32p[:, S_T1:S_T1 + 2048], in_=xs, func=AF.Square,
                    accum_out=colap(stat, 16 + i)), inc=True)
                ev_sq = b.evt("SE")
                b.wait_evt("vector", ev_sq)
                vop(lambda e, i=i: e.tensor_scalar(
                    out=colap(stat, 32 + i), in0=colap(stat, i),
                    scalar1=1.0 / C, scalar2=None, op0=ALU.mult))
                vop(lambda e, i=i: e.tensor_scalar(
                    out=colap(stat, 48 + i), in0=colap(stat, 16 + i),
                    scalar1=1.0 / C, scalar2=None, op0=ALU.mult))
                vop(lambda e, i=i: e.tensor_tensor(
                    out=tpt[:, 0:1], in0=colap(stat, 32 + i),
                    in1=colap(stat, 32 + i), op=ALU.mult))
                vop(lambda e, i=i: e.tensor_tensor(
                    out=colap(stat, 48 + i), in0=colap(stat, 48 + i),
                    in1=tpt[:, 0:1], op=ALU.subtract))
                vop(lambda e, i=i: e.tensor_scalar(
                    out=colap(stat, 48 + i), in0=colap(stat, 48 + i),
                    scalar1=EPS, scalar2=None, op0=ALU.add), inc=True)
                ev_var = b.evt("VE")
                b.wait_evt("scalar", ev_var)
                sop(lambda e, i=i: e.activation(out=tpt[:, 1:2],
                                                in_=colap(stat, 48 + i),
                                                func=AF.Ln))
                sop(lambda e, i=i: e.activation(
                    out=colap(stat, 48 + i), in_=tpt[:, 1:2], func=AF.Exp,
                    scale=-0.5), inc=True)
                ev_rstd = b.evt("SE")
                b.wait_evt("vector", ev_rstd)
                vop(lambda e, i=i: e.tensor_tensor(
                    out=colap(stat, 32 + i), in0=colap(stat, 32 + i),
                    in1=colap(stat, 48 + i), op=ALU.mult))
                vop(lambda e, i=i: e.tensor_scalar(
                    out=colap(stat, 32 + i), in0=colap(stat, 32 + i),
                    scalar1=-1.0, scalar2=None, op0=ALU.mult))
                vop(lambda e, xs=xs, i=i: e.tensor_scalar(
                    out=ztok[:, :], in0=xs, scalar1=colap(stat, 48 + i),
                    scalar2=colap(stat, 32 + i), op0=ALU.mult, op1=ALU.add),
                    inc=True)
                ev_z1 = b.evt("VE")
                if i == 0 and ev_pe_m is not None:
                    b.wait_evt("tensor", ev_pe_m)
                b.wait_evt("tensor", ev_z1)
                transpose_tile_set(ztok, i, 0, 2048)
                b.wait("tensor", "VE")
                if i == 0:
                    b.wait_evt("tensor", e_atg)
                for K in range(KB):
                    mm(pu[:, 256:392],
                       actf[:, K * 2048 + i * 128: K * 2048 + i * 128 + 128],
                       atgb[:, K * 136:(K + 1) * 136],
                       start=(K == 0), stop=(K == KB - 1), inc=(K == KB - 1))
                ev_tl = b.evt("PE")
                b.wait_evt("vector", ev_tl)
                vop(lambda e: e.tensor_copy(out=ogb[:, 0:128],
                                            in_=pu[:, 256:384]))
                vop(lambda e: e.tensor_copy(out=tpt[:, 8:16],
                                            in_=pu[:, 384:392]), inc=True)
                ev_tg = top2_gate(None)
                b.wait_evt("tensor", ev_tg)
                transpose_batch8([ogb[:, 0:128]],
                                 ttf[:, i * 128:(i + 1) * 128]
                                 .rearrange("p (k n) -> p k n", n=128))
            dbg_store("z1f", actf[:, 0:2048])
            dbg_store("ttf", ttf[:, :])

        if upto >= 3:
            reg_evac = {0: None, 1: None}
            for half in range(2):
                b.wait("gpsimd", "PE")
                ewf = b.dma("gpsimd",
                            bbuf[:, 0:16384].rearrange("p (k n) -> p k n", n=1024),
                            wfT[:, half * 1024:(half + 1) * 1024]
                            .rearrange("(k p) n -> p k n", p=128), "LG")
                b.wait_evt("tensor", ewf)
                for i in range(NT):
                    reg = (i % 2) * 1024
                    xs = xstg[:, (i % 2) * 2048 + half * 1024:
                              (i % 2) * 2048 + half * 1024 + 1024]
                    b.wait("sync", "VE")
                    b.wait("sync", "SE")
                    exl = b.dma("sync", xs,
                                x_in[i * 128:(i + 1) * 128,
                                     half * 1024:(half + 1) * 1024], "LD")
                    b.wait_evt("tensor", reg_evac[i % 2])
                    for nb in range(2):
                        po = reg + nb * 512
                        for K in range(KB):
                            mm(pp[:, po:po + 512],
                               actf[:, K * 2048 + i * 128:
                                    K * 2048 + i * 128 + 128],
                               bbuf[:, K * 1024 + nb * 512:
                                    K * 1024 + nb * 512 + 512],
                               start=(K == 0), stop=False)
                        mm(pp[:, po:po + 512], ttf[:, i * 128:(i + 1) * 128],
                           btb[:, half * 1024 + nb * 512:
                               half * 1024 + nb * 512 + 512],
                           start=False, stop=True, inc=True)
                    ev_h = b.evt("PE")
                    b.wait_evt("vector", exl)
                    b.wait_evt("vector", ev_h)
                    vop(lambda e, xs=xs, reg=reg:
                        e.scalar_tensor_tensor(
                            out=xs, in0=pp[:, reg:reg + 1024], scalar=0.5,
                            in1=xs, op0=ALU.mult, op1=ALU.add),
                        inc=True)
                    ev_x1 = b.evt("VE")
                    reg_evac[i % 2] = ev_x1
                    b.wait_evt("scalar", ev_x1)
                    sop(lambda e, xs=xs, half=half, i=i: e.activation(
                        out=f32p[:, S_T1:S_T1 + 1024], in_=xs, func=AF.Square,
                        accum_out=colap(stat, 96 + 16 * half + i)))
                    sop(lambda e, xs=xs, half=half, i=i: e.activation(
                        out=f32p[:, S_T1:S_T1 + 1024], in_=xs, func=AF.Copy,
                        accum_out=colap(stat, 64 + 16 * half + i)), inc=True)
                    b.wait_evt("sync", ev_x1)
                    b.dma("sync", x1s[i * 128:(i + 1) * 128,
                                      half * 1024:(half + 1) * 1024], xs, "LD")
                ev_pe_f1 = b.evt("PE")
                b.wait_evt("gpsimd", ev_pe_f1)
            ev_x1_stored = b.evt("LD")
            if dbg is not None and dbg[0] == "x1row":
                b.wait_evt("gpsimd", ev_x1_stored)
                b.dma("gpsimd", xstg[:, 0:2048], x1s[0:128, :], "LG")
                b.wait("gpsimd", "LG")
                dbg_store("x1row", xstg[:, 0:2048])

        if upto >= 4:
            b.wait("vector", "SE")
            ev_ln2 = ln_batch_math(64, 80, 96, 112, 128, 144, NT,
                                   extra_scale=(1.0 + EPS),
                                   extra_eps=EPS * (1.0 + EPS))
            dbg_store("ln2", stat[:, :])
            for i in range(NT):
                xs = xstg[:, (i % 2) * 2048:(i % 2) * 2048 + 2048]
                b.wait("gpsimd", "VE")
                b.wait_evt("gpsimd", ev_x1_stored)
                ex1 = b.dma("gpsimd", xs, x1s[i * 128:(i + 1) * 128, :], "LG")
                b.wait_evt("vector", ex1)
                vop(lambda e, xs=xs, i=i: e.tensor_scalar(
                    out=ztok[:, :], in0=xs, scalar1=colap(stat, 144 + i),
                    scalar2=colap(stat, 128 + i), op0=ALU.mult, op1=ALU.add),
                    inc=True)
                ev_xh = b.evt("VE")
                b.wait_evt("tensor", ev_xh)
                transpose_tile_set(ztok, i, 0, 2048)
            for k in range(KB):
                vop(lambda e, k=k: e.tensor_scalar(
                    out=actf[:, k * 2048:(k + 1) * 2048],
                    in0=actf[:, k * 2048:(k + 1) * 2048],
                    scalar1=colap(lncol, k), scalar2=colap(lncol, 16 + k),
                    op0=ALU.mult, op1=ALU.add), inc=(k == KB - 1))
            ev_xnf = b.evt("VE")
            dbg_store("xnf", actf[:, 0:2048])

        if upto >= 5:
            pp_bank_evac = [None] * 4
            wslc_reader = [None, None]
            for g in range(NG):
                ev_ev = {}
                projs = [("zw", None), ("k", wkT), ("rz", wrT), ("v", wvT)]
                for pi, (pname, wsrc) in enumerate(projs):
                    slot = pi % 2
                    wap = wslc[:, slot * 2048:(slot + 1) * 2048]
                    wap3 = wap.rearrange("p (k n) -> p k n", n=128)
                    q = "sync" if wsrc is None else "gpsimd"
                    b.wait_evt(q, wslc_reader[slot])
                    if wsrc is None:
                        b.wait_evt("sync", ev_m_stored)
                        ew = b.dma("sync", wap3,
                                   mts[:, g * 128:(g + 1) * 128]
                                   .rearrange("(k p) n -> p k n", p=128), "LD")
                    else:
                        ew = b.dma("gpsimd", wap3,
                                   wsrc[:, g * 128:(g + 1) * 128]
                                   .rearrange("(k p) n -> p k n", p=128), "LG")
                    b.wait_evt("tensor", ew)
                    if g == 0 and pi == 0:
                        b.wait_evt("tensor", ev_xnf)
                    for bank in range(4):
                        b.wait_evt("tensor", pp_bank_evac[bank])
                        for K in range(KB):
                            mm(pp[:, bank * 512:(bank + 1) * 512],
                               wap[:, K * 128:(K + 1) * 128],
                               actf[:, K * 2048 + bank * 512:
                                    K * 2048 + bank * 512 + 512],
                               start=(K == 0), stop=(K == KB - 1),
                               inc=(K == KB - 1))
                        ev_mm = b.evt("PE")
                        dtile, doff = {"zw": (f32p, S_ZW), "k": (bbuf, B_K),
                                       "rz": (f32p, S_U),
                                       "v": (bbuf, B_V)}[pname]
                        dap = dtile[:, doff + bank * 512:
                                    doff + bank * 512 + 512]
                        if pname == "zw":
                            b.wait_evt("vector", ev_mm)
                            vop(lambda e, dap=dap, bank=bank, g=g:
                                e.tensor_scalar(
                                    out=dap,
                                    in0=pp[:, bank * 512:bank * 512 + 512],
                                    scalar1=colap(lncol, 32 + g),
                                    scalar2=None, op0=ALU.add), inc=True)
                            ev_ev[(pname, bank)] = b.evt("VE")
                        else:
                            b.wait_evt("scalar", ev_mm)
                            sop(lambda e, dap=dap, bank=bank: e.activation(
                                out=dap,
                                in_=pp[:, bank * 512:bank * 512 + 512],
                                func=AF.Copy), inc=True)
                            ev_ev[(pname, bank)] = b.evt("SE")
                        pp_bank_evac[bank] = ev_ev[(pname, bank)]
                    wslc_reader[slot] = b.evt("PE")

                if "prep" not in p5:
                    continue
                zw = f32p[:, S_ZW:S_ZW + 2048]
                lam = f32p[:, S_LAM:S_LAM + 2048]
                uu = f32p[:, S_U:S_U + 2048]
                t1 = f32p[:, S_T1:S_T1 + 2048]
                t2 = f32p[:, S_T2:S_T2 + 2048]
                for bk in range(4):
                    b.wait_evt("scalar", ev_ev[("zw", bk)])
                sop(lambda e: e.activation(out=t1, in_=zw, func=AF.Abs))
                sop(lambda e: e.activation(out=t1, in_=t1, func=AF.Exp,
                                           scale=-1.0))
                sop(lambda e: e.activation(out=t1, in_=t1, func=AF.Ln,
                                           bias=1.0), inc=True)
                ev_ln = b.evt("SE")
                b.wait_evt("vector", ev_ln)
                vop(lambda e: e.tensor_scalar(out=zw, in0=zw, scalar1=0.0,
                                              scalar2=-1.0, op0=ALU.max,
                                              op1=ALU.mult))
                vop(lambda e: e.tensor_tensor(out=zw, in0=zw, in1=t1,
                                              op=ALU.subtract))
                vop(lambda e: e.tensor_tensor_scan(
                    out=lam, data0=zw, data1=zw, initial=0.0, op0=ALU.add,
                    op1=ALU.bypass))
                vop(lambda e: e.memset(ends[:, 0:1], 0.0))
                vop(lambda e: e.tensor_copy(
                    out=ends[:, 1:33],
                    in_=bass.AP(tensor=lam.tensor, offset=lam.offset + 63,
                                ap=[lam.ap[0], [64, 32]])))
                vop(lambda e: e.tensor_tensor(out=mcb[:, :], in0=ends[:, 0:32],
                                              in1=ends[:, 1:33], op=ALU.add))
                vop(lambda e: e.tensor_scalar(out=mcb[:, :], in0=mcb[:, :],
                                              scalar1=0.5, scalar2=None,
                                              op0=ALU.mult))
                vop(lambda e: e.tensor_tensor(out=eksb[:, :],
                                              in0=ends[:, 1:33],
                                              in1=mcb[:, :], op=ALU.subtract))
                vop(lambda e: e.tensor_tensor(out=edec[:, :],
                                              in0=ends[:, 1:33],
                                              in1=ends[:, 0:32],
                                              op=ALU.subtract), inc=True)
                ev_bnd = b.evt("VE")
                b.wait_evt("scalar", ev_bnd)
                sop(lambda e: e.activation(out=eksb[:, :], in_=eksb[:, :],
                                           func=AF.Exp))
                sop(lambda e: e.activation(out=edec[:, :], in_=edec[:, :],
                                           func=AF.Exp), inc=True)
                ev_small = b.evt("SE")
                for bk in range(4):
                    b.wait_evt("scalar", ev_ev[("rz", bk)])
                sop(lambda e: e.activation(out=t2, in_=uu, func=AF.Abs))
                sop(lambda e: e.activation(out=t2, in_=t2, func=AF.Exp,
                                           scale=-1.0))
                sop(lambda e: e.activation(out=t2, in_=t2, func=AF.Ln,
                                           bias=1.0), inc=True)
                ev_ru = b.evt("SE")
                b.wait_evt("vector", ev_ru)
                vop(lambda e: e.tensor_scalar(out=uu, in0=uu, scalar1=0.0,
                                              scalar2=None, op0=ALU.min))
                vop(lambda e: e.tensor_tensor(out=uu, in0=uu, in1=t2,
                                              op=ALU.subtract))
                vop(lambda e: e.tensor_tensor(out=t1, in0=lam,
                                              in1=bcast_ap(mcb, 32, 64),
                                              op=ALU.subtract))
                vop(lambda e: e.tensor_scalar(out=t2, in0=t1, scalar1=-CLAMP,
                                              scalar2=-1.0, op0=ALU.max,
                                              op1=ALU.mult), inc=True)
                ev_ak = b.evt("VE")
                b.wait_evt("scalar", ev_ak)
                sop(lambda e: e.activation(out=bbuf[:, B_EK:B_EK + 2048],
                                           in_=t2, func=AF.Exp), inc=True)
                ev_ek = b.evt("SE")
                vop(lambda e: e.tensor_scalar(out=t1, in0=t1, scalar1=CLAMP,
                                              scalar2=None, op0=ALU.min))
                vop(lambda e: e.tensor_tensor(out=t1, in0=t1, in1=uu,
                                              op=ALU.add), inc=True)
                ev_ar = b.evt("VE")
                b.wait_evt("scalar", ev_ar)
                sop(lambda e: e.activation(out=bbuf[:, B_RA:B_RA + 2048],
                                           in_=t1, func=AF.Exp), inc=True)
                ev_ra = b.evt("SE")
                vop(lambda e: e.tensor_tensor(out=t2, in0=lam,
                                              in1=bcast_ap(ends, 32, 64),
                                              op=ALU.subtract))
                vop(lambda e: e.tensor_tensor(out=t2, in0=t2, in1=uu,
                                              op=ALU.add), inc=True)
                ev_ai = b.evt("VE")
                b.wait_evt("scalar", ev_ai)
                sop(lambda e: e.activation(out=bbuf[:, B_RI:B_RI + 2048],
                                           in_=t2, func=AF.Exp), inc=True)
                ev_ri = b.evt("SE")
                if "tp" not in p5:
                    continue
                for bk in range(4):
                    b.wait_evt("tensor", ev_ev[("v", bk)])
                for j0 in (0, 8):
                    srcs = [bbuf[:, B_V + j * 128:B_V + (j + 1) * 128]
                            for j in range(j0, j0 + 8)]
                    dst = bbuf[:, B_VT + j0 * 128:B_VT + j0 * 128 + 1024]
                    transpose_batch8(
                        srcs, dst.rearrange("p (k n) -> p k n", n=128))
                ev_vt_done = b.evt("VE")
                for bk in range(4):
                    b.wait_evt("vector", ev_ev[("k", bk)])
                b.wait_evt("vector", ev_ek)
                b.wait_evt("vector", ev_small)
                b.wait("vector", "PE")
                vop(lambda e: e.tensor_tensor(
                    out=bbuf[:, B_V:B_V + 2048],
                    in0=bbuf[:, B_K:B_K + 2048],
                    in1=bbuf[:, B_EK:B_EK + 2048], op=ALU.mult))
                vop(lambda e: e.tensor_copy(
                    out=bass.AP(tensor=bbuf[:, :].tensor,
                                offset=bbuf[:, :].offset + B_KAP,
                                ap=[[18432, 64], [128, 32], [1, 64]]),
                    in_=bass.AP(tensor=bbuf[:, :].tensor,
                                offset=bbuf[:, :].offset + B_V,
                                ap=[[18432, 64], [64, 32], [1, 64]])))
                vop(lambda e: e.tensor_copy(
                    out=bass.AP(tensor=bbuf[:, :].tensor,
                                offset=bbuf[:, :].offset + 64 * 18432
                                + B_KAP + 64,
                                ap=[[18432, 64], [128, 32], [1, 64]]),
                    in_=bass.AP(tensor=bbuf[:, :].tensor,
                                offset=bbuf[:, :].offset + 64 * 18432 + B_V,
                                ap=[[18432, 64], [64, 32], [1, 64]])))
                vop(lambda e: e.tensor_tensor(
                    out=bbuf[:, B_K:B_K + 2048],
                    in0=bbuf[:, B_V:B_V + 2048],
                    in1=bcast_ap(eksb, 32, 64), op=ALU.mult), inc=True)
                ev_ks = b.evt("VE")
                b.wait_evt("tensor", ev_ks)
                for j0 in (0, 8):
                    srcs = [bbuf[:, B_K + j * 128:B_K + (j + 1) * 128]
                            for j in range(j0, j0 + 8)]
                    dst = bbuf[:, B_KT + j0 * 128:B_KT + j0 * 128 + 1024]
                    transpose_batch8(
                        srcs, dst.rearrange("p (k n) -> p k n", n=128))
                ev_tp_done = b.evt("VE")

                b.wait("vector", "PE")
                vop(lambda e: e.memset(s0b[:, :], 0.0))
                if g == 0:
                    vop(lambda e: e.memset(pa[:, :], 0.0))
                    vop(lambda e: e.memset(pu[0:64, 64:128], 0.0))
                    vop(lambda e: e.memset(pu[64:128, 0:64], 0.0))
                    vop(lambda e: e.memset(
                        bass.AP(tensor=bbuf[:, :].tensor,
                                offset=bbuf[:, :].offset + B_KAP + 64,
                                ap=[[18432, 64], [128, 32], [1, 64]]), 0.0))
                    vop(lambda e: e.memset(
                        bass.AP(tensor=bbuf[:, :].tensor,
                                offset=bbuf[:, :].offset + 64 * 18432 + B_KAP,
                                ap=[[18432, 64], [128, 32], [1, 64]]), 0.0))
                vop(lambda e: e.engine_nop() if False else e.memset(
                    s1b[:, 64:128], 0.0), inc=True)
                ev_s0 = b.evt("VE")
                b.wait_evt("tensor", ev_tp_done)
                b.wait_evt("tensor", ev_s0)
                b.wait_evt("tensor", ev_ra)
                b.wait_evt("tensor", ev_ri)

                if "chunks" not in p5:
                    continue
                ev_stt_prev = None
                ev_yev_prev = None
                ev_as_prev = None
                ev_ogb_store = None
                for bat in range(8):
                    c0 = bat * 4
                    b.wait_evt("tensor", ev_as_prev)
                    for c in range(c0, c0 + 4):
                        par = (c & 1) * 64
                        for hh in (0, 1):
                            mm(pa[par:par + 64,
                                  ((c & 3) * 2 + hh) * 64:
                                  ((c & 3) * 2 + hh) * 64 + 64],
                               bbuf[:, B_KAP + c * 128 + hh * 64:
                                    B_KAP + c * 128 + hh * 64 + 64],
                               bbuf[:, B_RA + c * 64:B_RA + c * 64 + 64],
                               start=True, stop=True,
                               inc=(c == c0 + 3 and hh == 1),
                               tp=(0, par))
                    ev_ap = b.evt("PE")
                    b.wait_evt("vector", ev_ap)
                    vop(lambda e: e.tensor_tensor(
                        out=asb[:, :], in0=pa[:, :], in1=amask[:, :],
                        op=ALU.mult), inc=True)
                    ev_as = b.evt("VE")
                    ev_as_prev = ev_as
                    b.wait_evt("tensor", ev_as)
                    if "cy" not in p5:
                        continue
                    if c0 % 8 == 0 and ev_yev_prev is not None:
                        b.wait_evt("tensor", ev_yev_prev)
                    for c in range(c0, c0 + 4):
                        par = (c & 1) * 64
                        pair = c >> 1
                        scur = s0b if c % 2 == 0 else s1b
                        snxt = s1b if c % 2 == 0 else s0b
                        if ev_stt_prev is not None:
                            b.wait_evt("tensor", ev_stt_prev)
                        ypos = (c & 7) * 64
                        for hh, hb in ((0, 0), (1, 64)):
                            mm(py[hb:hb + 64, ypos:ypos + 64],
                               bbuf[:, B_VT + pair * 128 + hh * 64:
                                    B_VT + pair * 128 + hh * 64 + 64],
                               asb[:, ((c & 3) * 2 + hh) * 64:
                                   ((c & 3) * 2 + hh) * 64 + 64],
                               start=True, stop=False, tp=(0, hb))
                            mm(py[hb:hb + 64, ypos:ypos + 64],
                               scur[:, hh * 64:hh * 64 + 64],
                               bbuf[:, B_RI + c * 64:B_RI + c * 64 + 64],
                               start=False, stop=True,
                               inc=(hh == 1), tp=(0, hb))
                        ev_y = b.evt("PE")
                        for hh, hb in ((0, 0), (1, 64)):
                            mm(pu[hb:hb + 64, hb:hb + 64],
                               bbuf[par:par + 64,
                                    B_KT + pair * 128 + hh * 64:
                                    B_KT + pair * 128 + hh * 64 + 64],
                               bbuf[par:par + 64,
                                    B_VT + pair * 128 + hh * 64:
                                    B_VT + pair * 128 + hh * 64 + 64],
                               start=True, stop=True, inc=(hh == 1),
                               tp=(par, hb))
                        ev_upd = b.evt("PE")
                        b.wait_evt("vector", ev_upd)
                        vop(lambda e, sc=scur, sn=snxt, c=c:
                            e.scalar_tensor_tensor(
                                out=sn[:, :], in0=sc[:, :],
                                scalar=colap(edec, c), in1=pu[:, 0:128],
                                op0=ALU.mult, op1=ALU.add), inc=True)
                        ev_stt_prev = b.evt("VE")
                    if "cs" not in p5:
                        continue
                    if bat % 2 == 1:
                        b.wait_evt("vector", ev_y)
                        b.wait_evt("vector", ev_ogb_store)
                        vop(lambda e: e.tensor_copy(out=ogb[:, :],
                                                    in_=py[:, :]), inc=True)
                        ev_yev_prev = b.evt("VE")
                        b.wait_evt("sync", ev_yev_prev)
                        ev_ogb_store = b.dma(
                            "sync", outss[g * 128:(g + 1) * 128,
                                          (c0 - 4) * 64:(c0 - 4) * 64 + 512],
                            ogb[:, :], "LD")
            ev_scan_done = b.evt("LD")

        if upto >= 6:
            b.wait("vector", "PE")
            b.wait("sync", "PE")
            b.wait("sync", "VE")
            b.wait("sync", "SE")
            ev_sel_parts = []
            for k in range(KB):
                t0 = bbuf[:, B_K + (k % 2) * 2048: B_K + (k % 2) * 2048 + 1024]
                t1b = bbuf[:, B_RA + (k % 2) * 2048:
                           B_RA + (k % 2) * 2048 + 1024]
                b.wait("sync", "VE")
                b.wait_evt("sync", ev_scan_done)
                el0 = b.dma("sync", t0, outss[k * 128:(k + 1) * 128, 0:1024],
                            "LD")
                el1 = b.dma("sync", t1b,
                            outss[k * 128:(k + 1) * 128, 1024:2048], "LD")
                b.wait_evt("vector", el1)
                vop(lambda e, t0=t0: e.tensor_scalar(
                    out=t0, in0=t0, scalar1=mcols[:, 0:1], scalar2=None,
                    op0=ALU.mult))
                vop(lambda e, t0=t0, t1b=t1b, k=k: e.scalar_tensor_tensor(
                    out=actf[:, k * 2048:k * 2048 + 1024], in0=t1b,
                    scalar=mcols[:, 1:2], in1=t0, op0=ALU.mult, op1=ALU.add),
                    inc=True)
            ev_sel = b.evt("VE")
            dbg_store("omf", actf[:, 0:1024])
            reg_evac = {0: None, 1: None}
            ev_sq_prev = None
            ev_store_prev = {0: None, 1: None}
            for half in range(2):
                b.wait("gpsimd", "PE")
                b.wait_evt("gpsimd", ev_sel)
                ewo = b.dma("gpsimd",
                            bbuf[:, 0:16384].rearrange("p (k n) -> p k n", n=1024),
                            woT[:, half * 1024:(half + 1) * 1024]
                            .rearrange("(k p) n -> p k n", p=128), "LG")
                b.wait_evt("tensor", ewo)
                if half == 0:
                    b.wait_evt("tensor", ev_sel)
                for i in range(WT):
                    reg = (i % 2) * 1024
                    b.wait_evt("tensor", reg_evac[i % 2])
                    for nb in range(2):
                        po = reg + nb * 512
                        for K in range(KB):
                            mm(pp[:, po:po + 512],
                               actf[:, K * 2048 + i * 128:
                                    K * 2048 + i * 128 + 128],
                               bbuf[:, K * 1024 + nb * 512:
                                    K * 1024 + nb * 512 + 512],
                               start=(K == 0), stop=(K == KB - 1),
                               inc=(K == KB - 1))
                    ev_ymm = b.evt("PE")
                    xl = xstg[:, (i % 2) * 2048:(i % 2) * 2048 + 1024]
                    xh = xstg[:, (i % 2) * 2048 + 1024:(i % 2) * 2048 + 2048]
                    b.wait("sync", "VE")
                    b.wait("sync", "SE")
                    b.wait_evt("sync", ev_store_prev[i % 2])
                    exlo = b.dma("sync", xl,
                                 x1s[i * 128:(i + 1) * 128,
                                     half * 1024:(half + 1) * 1024], "LD")
                    exhi = b.dma("sync", xh,
                                 x1s[1024 + i * 128:1024 + (i + 1) * 128,
                                     half * 1024:(half + 1) * 1024], "LD")
                    b.wait_evt("vector", exhi)
                    b.wait_evt("vector", ev_ymm)
                    vop(lambda e, xl=xl: e.tensor_scalar(
                        out=xl, in0=xl, scalar1=mcols[:, 0:1], scalar2=None,
                        op0=ALU.mult))
                    vop(lambda e, xl=xl, xh=xh: e.scalar_tensor_tensor(
                        out=xl, in0=xh, scalar=mcols[:, 1:2], in1=xl,
                        op0=ALU.mult, op1=ALU.add))
                    vop(lambda e, xl=xl, reg=reg:
                        e.scalar_tensor_tensor(
                            out=xl, in0=pp[:, reg:reg + 1024], scalar=1.0,
                            in1=xl, op0=ALU.mult, op1=ALU.add), inc=True)
                    ev_x2 = b.evt("VE")
                    reg_evac[i % 2] = ev_x2
                    b.wait_evt("scalar", ev_x2)
                    sop(lambda e, xl=xl, half=half, i=i: e.activation(
                        out=f32p[:, S_T1:S_T1 + 1024], in_=xl, func=AF.Square,
                        accum_out=colap(stat, 32 + 16 * half + i)))
                    sop(lambda e, xl=xl, half=half, i=i: e.activation(
                        out=f32p[:, S_T1:S_T1 + 1024], in_=xl, func=AF.Copy,
                        accum_out=colap(stat, 16 * half + i)), inc=True)
                    b.wait_evt("sync", ev_x2)
                    ev_store_prev[i % 2] = b.dma(
                        "sync", x1s[i * 128:(i + 1) * 128,
                                    half * 1024:(half + 1) * 1024], xl, "LD")
                ev_pe_wo = b.evt("PE")
                b.wait_evt("gpsimd", ev_pe_wo)
            ev_x2_stored = b.evt("LD")
            b.wait("vector", "SE")
            ev_ln3 = ln_batch_math(0, 16, 32, 48, 64, 80, WT)
            for i in range(WT):
                xs = xstg[:, (i % 2) * 2048:(i % 2) * 2048 + 2048]
                b.wait("gpsimd", "VE")
                b.wait_evt("gpsimd", ev_x2_stored)
                ex2 = b.dma("gpsimd", xs, x1s[i * 128:(i + 1) * 128, :], "LG")
                b.wait_evt("vector", ex2)
                vop(lambda e, xs=xs, i=i: e.tensor_scalar(
                    out=ztok[:, :], in0=xs, scalar1=colap(stat, 80 + i),
                    scalar2=colap(stat, 64 + i), op0=ALU.mult, op1=ALU.add),
                    inc=True)
                ev_z3 = b.evt("VE")
                b.wait_evt("tensor", ev_z3)
                transpose_tile_set(ztok, i, 1024, 2048)
            ev_z3f = b.evt("VE")
            dbg_store("z3f", actf[:, 1024:2048])

        if upto >= 7:
            for i in range(WT):
                b.wait("tensor", "VE")
                for K in range(KB):
                    mm(pu[:, 256:392],
                       actf[:, K * 2048 + 1024 + i * 128:
                            K * 2048 + 1024 + i * 128 + 128],
                       atgb[:, K * 136:(K + 1) * 136],
                       start=(K == 0), stop=(K == KB - 1), inc=(K == KB - 1))
                ev_tl = b.evt("PE")
                b.wait_evt("vector", ev_tl)
                vop(lambda e: e.tensor_copy(out=ogb[:, 0:128],
                                            in_=pu[:, 256:384]))
                vop(lambda e: e.tensor_copy(out=tpt[:, 8:16],
                                            in_=pu[:, 384:392]), inc=True)
                ev_tg = top2_gate(None)
                b.wait_evt("tensor", ev_tg)
                transpose_batch8([ogb[:, 0:128]],
                                 ttf[:, i * 128:(i + 1) * 128]
                                 .rearrange("p (k n) -> p k n", n=128))
            reg_evac = {0: None, 1: None}
            for half in range(2):
                b.wait("gpsimd", "PE")
                ewf = b.dma("gpsimd",
                            bbuf[:, 0:16384].rearrange("p (k n) -> p k n", n=1024),
                            wfT[:, half * 1024:(half + 1) * 1024]
                            .rearrange("(k p) n -> p k n", p=128), "LG")
                b.wait_evt("tensor", ewf)
                for i in range(WT):
                    reg = (i % 2) * 1024
                    xs = xstg[:, (i % 2) * 2048 + half * 1024:
                              (i % 2) * 2048 + half * 1024 + 1024]
                    b.wait("sync", "VE")
                    exl = b.dma("sync", xs,
                                x1s[i * 128:(i + 1) * 128,
                                    half * 1024:(half + 1) * 1024], "LD")
                    b.wait_evt("tensor", reg_evac[i % 2])
                    for nb in range(2):
                        po = reg + nb * 512
                        for K in range(KB):
                            mm(pp[:, po:po + 512],
                               actf[:, K * 2048 + 1024 + i * 128:
                                    K * 2048 + 1024 + i * 128 + 128],
                               bbuf[:, K * 1024 + nb * 512:
                                    K * 1024 + nb * 512 + 512],
                               start=(K == 0), stop=False)
                        mm(pp[:, po:po + 512], ttf[:, i * 128:(i + 1) * 128],
                           btb[:, half * 1024 + nb * 512:
                               half * 1024 + nb * 512 + 512],
                           start=False, stop=True, inc=True)
                    ev_h = b.evt("PE")
                    b.wait_evt("vector", exl)
                    b.wait_evt("vector", ev_h)
                    vop(lambda e, xs=xs, reg=reg: e.scalar_tensor_tensor(
                        out=xs, in0=pp[:, reg:reg + 1024], scalar=0.5, in1=xs,
                        op0=ALU.mult, op1=ALU.add), inc=True)
                    ev_x3 = b.evt("VE")
                    reg_evac[i % 2] = ev_x3
                    b.wait_evt("sync", ev_x3)
                    b.dma("sync", out_t[i * 128:(i + 1) * 128,
                                        half * 1024:(half + 1) * 1024], xs,
                          "LD")
            b.wait("sync", "LD")
            b.wait("gpsimd", "LG")

        with nc.Block() as blk:
            for eng_name in KBd.ENGINES:
                opslist = b.ops[eng_name]
                def mk(opslist):
                    def run(e):
                        for f in opslist:
                            f(e)
                    return run
                getattr(blk, eng_name)(mk(opslist))

    return nc


def _layout_inputs(inputs, core):
    bidx = core // 2
    half = core % 2
    cat = np.ascontiguousarray
    A_w = inputs["L0_ffn_A_w"]
    gate = inputs["L0_ffn_gate_w"]
    ATg = np.concatenate([A_w.transpose(2, 0, 1).reshape(C, NE * NR),
                          np.asarray(gate).T], axis=1)
    amask = np.zeros((128, 512), np.float32)
    tri = np.triu(np.ones((64, 64), np.float32))
    for cl in range(4):
        par = (cl & 1) * 64
        for hh in range(2):
            blkc = (cl * 2 + hh) * 64
            amask[par:par + 64, blkc:blkc + 64] = tri
    colsify = lambda v: cat(np.asarray(v).reshape(16, 128).T.astype(np.float32))
    m0 = np.full((128, 1), 1.0 if half == 0 else 0.0, np.float32)
    m1 = np.full((128, 1), 0.0 if half == 0 else 1.0, np.float32)
    f32c = lambda a: cat(np.asarray(a).astype(np.float32, copy=False))
    return {
        "x": f32c(inputs["L0_attn_x"][bidx]),
        "WkT": f32c(np.asarray(inputs["L0_attn_Wk"]).T),
        "WvT": f32c(np.asarray(inputs["L0_attn_Wv"]).T),
        "WrT": f32c(np.asarray(inputs["L0_attn_Wr"]).T),
        "WoT": f32c(np.asarray(inputs["L0_attn_Wo"]).T),
        "WfT": f32c(np.asarray(inputs["L0_ffn_weight"]).T),
        "WwT": f32c(np.asarray(inputs["L0_attn_Ww"]).T),
        "Wx": f32c(inputs["L0_attn_Wx"]),
        "ATg": f32c(ATg),
        "BT": f32c(np.asarray(inputs["L0_ffn_B_w"]).transpose(0, 2, 1)
                   .reshape(NE * NR, C)),
        "lns": colsify(inputs["L0_attn_ln_scale"]),
        "lnb": colsify(inputs["L0_attn_ln_bias"]),
        "bw": colsify(inputs["L0_attn_bw"]),
        "m0": m0, "m1": m1,
        "eye": np.eye(128, dtype=np.float32),
        "amask": amask,
    }


def kernel(**inputs):
    nc = build_program()
    in_maps = [_layout_inputs(inputs, c) for c in range(8)]
    res = run_bass_kernel_spmd(nc, in_maps, core_ids=list(range(8)))
    out = np.empty((4, T, C), np.float32)
    for c in range(8):
        bidx, half = c // 2, c % 2
        out[bidx, half * WTOK:(half + 1) * WTOK, :] = res.results[c]["out"]
    return out


# version 02
# speedup vs baseline: 1.0463x; 1.0463x over previous
import numpy as np
from contextlib import ExitStack

import concourse.bass as bass
import concourse.mybir as mybir
from concourse.bass_utils import run_bass_kernel_spmd

F32 = mybir.dt.float32
BF16 = mybir.dt.bfloat16
AF = mybir.ActivationFunctionType
ALU = mybir.AluOpType
AXL = mybir.AxisListType

T = 2048
C = 2048
NT = 16
KB = 16
NG = 16
L = 64
NCH = 32
WTOK = 1024
WT = 8
EPS = 1e-5
CLAMP = 43.0
NE, NR = 8, 16
ALPHA = 16.0

S_ZW, S_LAM, S_U, S_T1, S_T2 = 0, 2048, 4096, 6144, 8192
B_K, B_EK, B_RA, B_RI, B_V, B_VT, B_KT, B_KAP = (
    0, 2048, 4096, 6144, 8192, 10240, 12288, 14336)


class KBd:

    ENGINES = ("sync", "gpsimd", "tensor", "vector", "scalar")

    def __init__(self, sems):
        self.sems = sems
        self.cnt = {k: 0 for k in sems}
        self.ops = {e: [] for e in self.ENGINES}

    def emit(self, eng, fn):
        self.ops[eng].append(fn)

    def wait(self, eng, ch, val=None):
        v = self.cnt[ch] if val is None else val
        if v <= 0:
            return
        h = self.sems[ch]
        self.emit(eng, lambda e: e.wait_ge(h, v))

    def evt(self, ch):
        return (ch, self.cnt[ch])

    def wait_evt(self, eng, evt):
        if evt is None:
            return
        ch, v = evt
        if ch in ("LD", "LG"):
            v = self.cnt[ch]
        if v <= 0:
            return
        h = self.sems[ch]
        self.emit(eng, lambda e: e.wait_ge(h, v))

    def dma(self, qeng, out, in_, ch, **kw):
        h = self.sems[ch]
        self.cnt[ch] += 16
        self.emit(qeng, lambda e: e.dma_start(out=out, in_=in_, **kw)
                  .then_inc(h, 16))
        return (ch, self.cnt[ch])


def build_program(dbg=None, upto=99,
                  p5=("proj", "prep", "tp", "chunks", "cy", "cu", "cs")):
    nc = bass.Bass("TRN2", target_bir_lowering=False,
                   detect_race_conditions=False)

    dt_in = lambda n, s: nc.dram_tensor(n, list(s), F32, kind="ExternalInput").ap()
    x_in = dt_in("x", (T, C))
    wkT = dt_in("WkT", (C, C))
    wvT = dt_in("WvT", (C, C))
    wrT = dt_in("WrT", (C, C))
    woT = dt_in("WoT", (C, C))
    wfT = dt_in("WfT", (C, C))
    wwT = dt_in("WwT", (C, C))
    wx = dt_in("Wx", (C, C))
    atg = dt_in("ATg", (C, NE * NR + NE))
    btv = dt_in("BT", (NE * NR, C))
    lns_i = dt_in("lns", (128, 16))
    lnb_i = dt_in("lnb", (128, 16))
    bw_i = dt_in("bw", (128, 16))
    m0v = dt_in("m0", (128, 1))
    m1v = dt_in("m1", (128, 1))
    eyev = dt_in("eye", (128, 128))
    amaskv = dt_in("amask", (128, 512))

    out_t = nc.dram_tensor("out", [WTOK, C], F32, kind="ExternalOutput").ap()
    dbg_t = None
    if dbg is not None:
        dbg_t = nc.dram_tensor("dbg", list(dbg[1]), F32, kind="ExternalOutput").ap()

    x1s = nc.dram_tensor("x1s", [T, C], F32, kind="Internal").ap()
    mts = nc.dram_tensor("mts", [C, C], BF16, kind="Internal").ap()
    outss = nc.dram_tensor("outss", [C, T], BF16, kind="Internal").ap()

    with ExitStack() as st:
        sb = lambda n, s, d: st.enter_context(nc.sbuf_tensor(n, list(s), d))
        psm = lambda n, s, d: st.enter_context(nc.psum_tensor(n, list(s), d))
        actf = sb("actf", (128, 32768), BF16)
        f32p = sb("f32p", (128, 10240), F32)
        bbuf = sb("bbuf", (128, 18432), BF16)
        ttf = sb("ttf", (128, 2048), BF16)
        btb = sb("btb", (128, 2048), BF16)
        atgb = sb("atgb", (128, 2176), BF16)
        eyeb = sb("eyeb", (128, 128), BF16)
        amask = sb("amask_s", (128, 512), F32)
        ztok = sb("ztok", (128, 2048), BF16)
        wslc = sb("wslc", (128, 4096), BF16)
        asb = sb("asb", (128, 512), BF16)
        ogb = sb("ogb", (128, 512), BF16)
        lncol = sb("lncol", (128, 48), F32)
        mcols = sb("mcols", (128, 2), F32)
        stat = sb("stat", (128, 160), F32)
        ends = sb("ends", (128, 33), F32)
        mcb = sb("mcb", (128, 32), F32)
        eksb = sb("eksb", (128, 32), F32)
        edec = sb("edec", (128, 32), F32)
        s0b = sb("s0b", (128, 128), BF16)
        s1b = sb("s1b", (128, 128), BF16)
        tpt = sb("tpt", (128, 64), F32)
        mstg = sb("mstg", (128, 1024), BF16)
        xstg = sb("xstg", (128, 4096), F32)

        pp = psm("pp", (128, 2048), F32)
        pa = psm("pa", (128, 512), F32)
        py = psm("py", (128, 512), F32)
        ptb = psm("ptb", (128, 1024), BF16)
        pu = psm("pu", (128, 512), F32)

        sems = {}
        for name in ("LD", "LG", "PE", "VE", "SE"):
            sems[name] = st.enter_context(nc.semaphore(name))
        b = KBd(sems)
        S = sems

        def mm(outap, lhs, rhs, start, stop, inc=False, tp=None, sgc=False):
            h = S["PE"]
            if inc:
                b.cnt["PE"] += 1
            def f(e, outap=outap, lhs=lhs, rhs=rhs, start=start, stop=stop,
                  inc=inc, tp=tp, sgc=sgc):
                i = e.matmul(outap, lhs, rhs, start=start, stop=stop,
                             tile_position=tp, skip_group_check=sgc)
                if inc:
                    i.then_inc(h, 1)
            b.emit("tensor", f)

        def tpose(outap, inap, inc=False, tp=None):
            h = S["PE"]
            if inc:
                b.cnt["PE"] += 1
            def f(e, outap=outap, inap=inap, inc=inc, tp=tp):
                i = e.transpose(outap, inap, eyeb[:, :], tile_position=tp)
                if inc:
                    i.then_inc(h, 1)
            b.emit("tensor", f)

        def vop(fn, inc=True):
            h = S["VE"]
            prev = b.cnt["VE"]
            b.cnt["VE"] += 1
            def f(e, fn=fn, prev=prev):
                if prev > 0:
                    e.wait_ge(h, prev)
                fn(e).then_inc(h, 1)
            b.emit("vector", f)

        def sop(fn, inc=False):
            h = S["SE"]
            if inc:
                b.cnt["SE"] += 1
            def f(e, fn=fn, inc=inc):
                i = fn(e)
                if inc:
                    i.then_inc(h, 1)
            b.emit("scalar", f)

        def bcast_ap(t2d, ncols, rep, coloff=0):
            a = t2d[:, coloff:coloff + ncols]
            return bass.AP(tensor=a.tensor, offset=a.offset,
                           ap=[a.ap[0], [a.ap[1][0], ncols], [0, rep]])

        def colap(t2d, j):
            return t2d[:, j:j + 1]

        dbg_rows = [0]
        def dbg_store(name, sbap, nrows=128):
            if dbg is None or dbg[0] != name:
                return
            r = dbg_rows[0]
            b.wait("gpsimd", "VE")
            b.wait("gpsimd", "SE")
            b.wait("gpsimd", "PE")
            b.dma("gpsimd", dbg_t[r:r + nrows, :], sbap, "LG")
            b.wait("gpsimd", "LG")
            dbg_rows[0] += nrows

        def top2_gate(src_logits_ap):
            vop(lambda e: e.reduce_max(tpt[:, 0:1], tpt[:, 8:16], axis=AXL.X))
            vop(lambda e: e.tensor_scalar(
                out=tpt[:, 16:24], in0=tpt[:, 8:16], scalar1=tpt[:, 0:1],
                scalar2=None, op0=ALU.is_equal))
            vop(lambda e: e.scalar_tensor_tensor(
                out=tpt[:, 24:32], in0=tpt[:, 16:24], scalar=-1e30,
                in1=tpt[:, 8:16], op0=ALU.mult, op1=ALU.add))
            vop(lambda e: e.reduce_max(tpt[:, 1:2], tpt[:, 24:32], axis=AXL.X))
            vop(lambda e: e.tensor_scalar(
                out=tpt[:, 32:40], in0=tpt[:, 24:32], scalar1=tpt[:, 1:2],
                scalar2=None, op0=ALU.is_equal))
            vop(lambda e: e.tensor_tensor(
                out=tpt[:, 2:3], in0=tpt[:, 0:1], in1=tpt[:, 1:2],
                op=ALU.subtract), inc=True)
            ev_d = b.evt("VE")
            b.wait_evt("scalar", ev_d)
            sop(lambda e: e.activation(out=tpt[:, 3:4], in_=tpt[:, 2:3],
                                       func=AF.Exp, scale=-1.0))
            sop(lambda e: e.activation(out=tpt[:, 3:4], in_=tpt[:, 3:4],
                                       func=AF.Ln, bias=1.0))
            sop(lambda e: e.activation(out=tpt[:, 4:5], in_=tpt[:, 3:4],
                                       func=AF.Exp, scale=-1.0), inc=True)
            ev_w1 = b.evt("SE")
            b.wait_evt("vector", ev_w1)
            vop(lambda e: e.tensor_scalar(
                out=tpt[:, 5:6], in0=tpt[:, 4:5], scalar1=-1.0, scalar2=1.0,
                op0=ALU.mult, op1=ALU.add))
            vop(lambda e: e.tensor_scalar(
                out=tpt[:, 16:24], in0=tpt[:, 16:24], scalar1=tpt[:, 4:5],
                scalar2=None, op0=ALU.mult))
            vop(lambda e: e.tensor_scalar(
                out=tpt[:, 32:40], in0=tpt[:, 32:40], scalar1=tpt[:, 5:6],
                scalar2=None, op0=ALU.mult))
            vop(lambda e: e.tensor_tensor(
                out=tpt[:, 16:24], in0=tpt[:, 16:24], in1=tpt[:, 32:40],
                op=ALU.add))
            vop(lambda e: e.tensor_scalar(
                out=tpt[:, 16:24], in0=tpt[:, 16:24], scalar1=ALPHA,
                scalar2=None, op0=ALU.mult))
            for ex in range(NE):
                vop(lambda e, ex=ex: e.tensor_scalar(
                    out=ogb[:, ex * 16:(ex + 1) * 16],
                    in0=ogb[:, ex * 16:(ex + 1) * 16],
                    scalar1=colap(tpt, 16 + ex), scalar2=None, op0=ALU.mult),
                    inc=(ex == NE - 1))
            return b.evt("VE")

        def ln_batch_math(s1a, s1b_, s2a, s2b, mo, ro, n, extra_scale=None,
                          extra_eps=EPS):
            for i in range(n):
                if s1b_ is not None:
                    vop(lambda e, i=i: e.tensor_tensor(
                        out=colap(stat, s1a + i), in0=colap(stat, s1a + i),
                        in1=colap(stat, s1b_ + i), op=ALU.add))
                    vop(lambda e, i=i: e.tensor_tensor(
                        out=colap(stat, s2a + i), in0=colap(stat, s2a + i),
                        in1=colap(stat, s2b + i), op=ALU.add))
                vop(lambda e, i=i: e.tensor_scalar(
                    out=colap(stat, mo + i), in0=colap(stat, s1a + i),
                    scalar1=1.0 / C, scalar2=None, op0=ALU.mult))
                vop(lambda e, i=i: e.tensor_scalar(
                    out=colap(stat, ro + i), in0=colap(stat, s2a + i),
                    scalar1=1.0 / C, scalar2=None, op0=ALU.mult))
                vop(lambda e, i=i: e.tensor_tensor(
                    out=tpt[:, 0:1], in0=colap(stat, mo + i),
                    in1=colap(stat, mo + i), op=ALU.mult))
                vop(lambda e, i=i: e.tensor_tensor(
                    out=colap(stat, ro + i), in0=colap(stat, ro + i),
                    in1=tpt[:, 0:1], op=ALU.subtract),
                    inc=False)
                if extra_scale is None:
                    vop(lambda e, i=i: e.tensor_scalar(
                        out=colap(stat, ro + i), in0=colap(stat, ro + i),
                        scalar1=extra_eps, scalar2=None, op0=ALU.add),
                        inc=True)
                if extra_scale is not None:
                    vop(lambda e, i=i: e.tensor_scalar(
                        out=colap(stat, ro + i), in0=colap(stat, ro + i),
                        scalar1=extra_scale, scalar2=extra_eps, op0=ALU.mult,
                        op1=ALU.add), inc=True)
            ev_v = b.evt("VE")
            b.wait_evt("scalar", ev_v)
            for i in range(n):
                sop(lambda e, i=i: e.activation(
                    out=tpt[:, 1:2], in_=colap(stat, ro + i), func=AF.Ln))
                sop(lambda e, i=i: e.activation(
                    out=colap(stat, ro + i), in_=tpt[:, 1:2], func=AF.Exp,
                    scale=-0.5), inc=(i == n - 1))
            ev_s = b.evt("SE")
            b.wait_evt("vector", ev_s)
            for i in range(n):
                vop(lambda e, i=i: e.tensor_tensor(
                    out=colap(stat, mo + i), in0=colap(stat, mo + i),
                    in1=colap(stat, ro + i), op=ALU.mult))
                vop(lambda e, i=i: e.tensor_scalar(
                    out=colap(stat, mo + i), in0=colap(stat, mo + i),
                    scalar1=-1.0, scalar2=None, op0=ALU.mult),
                    inc=(i == n - 1))
            return b.evt("VE")

        ptb_evac = [None]

        def transpose_batch8(srcs, dst3ap):
            nsl = len(srcs)
            b.wait_evt("tensor", ptb_evac[0])
            for sl, src in enumerate(srcs):
                tpose(ptb[:, sl * 128:(sl + 1) * 128], src,
                      inc=(sl == nsl - 1))
            ev_tp = b.evt("PE")
            b.wait_evt("vector", ev_tp)
            vop(lambda e, dst3ap=dst3ap, nsl=nsl: e.tensor_copy(
                out=dst3ap, in_=ptb[:, 0:nsl * 128]
                .rearrange("p (k n) -> p k n", n=128)), inc=True)
            ptb_evac[0] = b.evt("VE")

        def transpose_tile_set(src_tok, i, base, width):
            for half8 in range(2):
                k0 = half8 * 8
                srcs = [src_tok[:, k * 128:(k + 1) * 128]
                        for k in range(k0, k0 + 8)]
                dst = bass.AP(
                    tensor=actf[:, :].tensor,
                    offset=actf[:, :].offset + (k0 * 2048 + base + i * 128),
                    ap=[actf[:, :].ap[0], [2048, 8], [1, 128]])
                transpose_batch8(srcs, dst)

        b.dma("gpsimd", eyeb[:, :], eyev, "LG")
        b.dma("sync", amask[:, :], amaskv, "LD")
        b.dma("sync", lncol[:, 0:16], lns_i, "LD")
        b.dma("sync", lncol[:, 16:32], lnb_i, "LD")
        b.dma("sync", lncol[:, 32:48], bw_i, "LD")
        b.dma("sync", mcols[:, 0:1], m0v, "LD")
        b.dma("sync", mcols[:, 1:2], m1v, "LD")
        b.dma("gpsimd", btb[:, :], btv, "LG")
        e_atg = b.dma("gpsimd", atgb[:, :].rearrange("p (k n) -> p k n", n=136),
                      atg.rearrange("(k p) n -> p k n", p=128), "LG")
        e_wx = b.dma("gpsimd", actf[:, :].rearrange("p (k n) -> p k n", n=2048),
                     wx.rearrange("(k p) n -> p k n", p=128), "LG")
        ev_consts_ld = b.evt("LD")

        ev_m_stored = None
        ev_pe_m = None
        if upto >= 1:
            bank_evac = {0: None, 1: None}
            for ebank in range(4):
                ewl = b.dma("gpsimd",
                            bbuf[:, 0:8192].rearrange("p (k n) -> p k n", n=512),
                            wwT[:, ebank * 512:(ebank + 1) * 512]
                            .rearrange("(k p) n -> p k n", p=128), "LG")
                b.wait_evt("tensor", ewl)
                if ebank == 0:
                    b.wait_evt("tensor", e_wx)
                for Ct in range(16):
                    pb = Ct % 2
                    b.wait_evt("tensor", bank_evac[pb])
                    for K in range(16):
                        mm(pp[:, pb * 512:(pb + 1) * 512],
                           actf[:, K * 2048 + Ct * 128:
                                K * 2048 + Ct * 128 + 128],
                           bbuf[:, K * 512:(K + 1) * 512],
                           start=(K == 0), stop=(K == 15), inc=(K == 15))
                    ev_pe = b.evt("PE")
                    stg = mstg[:, pb * 512:pb * 512 + 512]
                    b.wait_evt("scalar", ev_pe)
                    b.wait("scalar", "LD")
                    sop(lambda e, stg=stg, pb=pb: e.activation(
                        out=stg, in_=pp[:, pb * 512:pb * 512 + 512],
                        func=AF.Copy), inc=True)
                    ev_se = b.evt("SE")
                    bank_evac[pb] = ev_se
                    b.wait_evt("sync", ev_se)
                    b.dma("sync", mts[Ct * 128:(Ct + 1) * 128,
                                      ebank * 512:(ebank + 1) * 512], stg, "LD")
                ev_pe_m = b.evt("PE")
                b.wait_evt("gpsimd", ev_pe_m)
            ev_m_stored = b.evt("LD")

        if upto >= 2:
            for i in range(NT):
                xs = xstg[:, (i % 2) * 2048:(i % 2) * 2048 + 2048]
                b.wait("gpsimd", "VE")
                b.wait("gpsimd", "SE")
                exl = b.dma("gpsimd", xs, x_in[i * 128:(i + 1) * 128, :], "LG")
                b.wait_evt("vector", exl)
                vop(lambda e, xs=xs, i=i: e.reduce_sum(
                    colap(stat, i), xs, axis=AXL.X), inc=True)
                b.wait_evt("scalar", exl)
                sop(lambda e, xs=xs, i=i: e.activation(
                    out=f32p[:, S_T1:S_T1 + 2048], in_=xs, func=AF.Square,
                    accum_out=colap(stat, 16 + i)), inc=True)
                ev_sq = b.evt("SE")
                b.wait_evt("vector", ev_sq)
                vop(lambda e, i=i: e.tensor_scalar(
                    out=colap(stat, 32 + i), in0=colap(stat, i),
                    scalar1=1.0 / C, scalar2=None, op0=ALU.mult))
                vop(lambda e, i=i: e.tensor_scalar(
                    out=colap(stat, 48 + i), in0=colap(stat, 16 + i),
                    scalar1=1.0 / C, scalar2=None, op0=ALU.mult))
                vop(lambda e, i=i: e.tensor_tensor(
                    out=tpt[:, 0:1], in0=colap(stat, 32 + i),
                    in1=colap(stat, 32 + i), op=ALU.mult))
                vop(lambda e, i=i: e.tensor_tensor(
                    out=colap(stat, 48 + i), in0=colap(stat, 48 + i),
                    in1=tpt[:, 0:1], op=ALU.subtract))
                vop(lambda e, i=i: e.tensor_scalar(
                    out=colap(stat, 48 + i), in0=colap(stat, 48 + i),
                    scalar1=EPS, scalar2=None, op0=ALU.add), inc=True)
                ev_var = b.evt("VE")
                b.wait_evt("scalar", ev_var)
                sop(lambda e, i=i: e.activation(out=tpt[:, 1:2],
                                                in_=colap(stat, 48 + i),
                                                func=AF.Ln))
                sop(lambda e, i=i: e.activation(
                    out=colap(stat, 48 + i), in_=tpt[:, 1:2], func=AF.Exp,
                    scale=-0.5), inc=True)
                ev_rstd = b.evt("SE")
                b.wait_evt("vector", ev_rstd)
                vop(lambda e, i=i: e.tensor_tensor(
                    out=colap(stat, 32 + i), in0=colap(stat, 32 + i),
                    in1=colap(stat, 48 + i), op=ALU.mult))
                vop(lambda e, i=i: e.tensor_scalar(
                    out=colap(stat, 32 + i), in0=colap(stat, 32 + i),
                    scalar1=-1.0, scalar2=None, op0=ALU.mult))
                vop(lambda e, xs=xs, i=i: e.tensor_scalar(
                    out=ztok[:, :], in0=xs, scalar1=colap(stat, 48 + i),
                    scalar2=colap(stat, 32 + i), op0=ALU.mult, op1=ALU.add),
                    inc=True)
                ev_z1 = b.evt("VE")
                if i == 0 and ev_pe_m is not None:
                    b.wait_evt("tensor", ev_pe_m)
                b.wait_evt("tensor", ev_z1)
                transpose_tile_set(ztok, i, 0, 2048)
                b.wait("tensor", "VE")
                if i == 0:
                    b.wait_evt("tensor", e_atg)
                for K in range(KB):
                    mm(pu[:, 256:392],
                       actf[:, K * 2048 + i * 128: K * 2048 + i * 128 + 128],
                       atgb[:, K * 136:(K + 1) * 136],
                       start=(K == 0), stop=(K == KB - 1), inc=(K == KB - 1))
                ev_tl = b.evt("PE")
                b.wait_evt("vector", ev_tl)
                vop(lambda e: e.tensor_copy(out=ogb[:, 0:128],
                                            in_=pu[:, 256:384]))
                vop(lambda e: e.tensor_copy(out=tpt[:, 8:16],
                                            in_=pu[:, 384:392]), inc=True)
                ev_tg = top2_gate(None)
                b.wait_evt("tensor", ev_tg)
                transpose_batch8([ogb[:, 0:128]],
                                 ttf[:, i * 128:(i + 1) * 128]
                                 .rearrange("p (k n) -> p k n", n=128))
            dbg_store("z1f", actf[:, 0:2048])
            dbg_store("ttf", ttf[:, :])

        if upto >= 3:
            reg_evac = {0: None, 1: None}
            for half in range(2):
                b.wait("gpsimd", "PE")
                ewf = b.dma("gpsimd",
                            bbuf[:, 0:16384].rearrange("p (k n) -> p k n", n=1024),
                            wfT[:, half * 1024:(half + 1) * 1024]
                            .rearrange("(k p) n -> p k n", p=128), "LG")
                b.wait_evt("tensor", ewf)
                for i in range(NT):
                    reg = (i % 2) * 1024
                    xs = xstg[:, (i % 2) * 2048 + half * 1024:
                              (i % 2) * 2048 + half * 1024 + 1024]
                    b.wait("sync", "VE")
                    b.wait("sync", "SE")
                    exl = b.dma("sync", xs,
                                x_in[i * 128:(i + 1) * 128,
                                     half * 1024:(half + 1) * 1024], "LD")
                    b.wait_evt("tensor", reg_evac[i % 2])
                    for nb in range(2):
                        po = reg + nb * 512
                        for K in range(KB):
                            mm(pp[:, po:po + 512],
                               actf[:, K * 2048 + i * 128:
                                    K * 2048 + i * 128 + 128],
                               bbuf[:, K * 1024 + nb * 512:
                                    K * 1024 + nb * 512 + 512],
                               start=(K == 0), stop=False)
                        mm(pp[:, po:po + 512], ttf[:, i * 128:(i + 1) * 128],
                           btb[:, half * 1024 + nb * 512:
                               half * 1024 + nb * 512 + 512],
                           start=False, stop=True, inc=True)
                    ev_h = b.evt("PE")
                    b.wait_evt("vector", exl)
                    b.wait_evt("vector", ev_h)
                    vop(lambda e, xs=xs, reg=reg:
                        e.scalar_tensor_tensor(
                            out=xs, in0=pp[:, reg:reg + 1024], scalar=0.5,
                            in1=xs, op0=ALU.mult, op1=ALU.add),
                        inc=True)
                    ev_x1 = b.evt("VE")
                    reg_evac[i % 2] = ev_x1
                    b.wait_evt("scalar", ev_x1)
                    sop(lambda e, xs=xs, half=half, i=i: e.activation(
                        out=f32p[:, S_T1:S_T1 + 1024], in_=xs, func=AF.Square,
                        accum_out=colap(stat, 96 + 16 * half + i)))
                    sop(lambda e, xs=xs, half=half, i=i: e.activation(
                        out=f32p[:, S_T1:S_T1 + 1024], in_=xs, func=AF.Copy,
                        accum_out=colap(stat, 64 + 16 * half + i)), inc=True)
                    b.wait_evt("sync", ev_x1)
                    b.dma("sync", x1s[i * 128:(i + 1) * 128,
                                      half * 1024:(half + 1) * 1024], xs, "LD")
                ev_pe_f1 = b.evt("PE")
                b.wait_evt("gpsimd", ev_pe_f1)
            ev_x1_stored = b.evt("LD")
            if dbg is not None and dbg[0] == "x1row":
                b.wait_evt("gpsimd", ev_x1_stored)
                b.dma("gpsimd", xstg[:, 0:2048], x1s[0:128, :], "LG")
                b.wait("gpsimd", "LG")
                dbg_store("x1row", xstg[:, 0:2048])

        if upto >= 4:
            b.wait("vector", "SE")
            ev_ln2 = ln_batch_math(64, 80, 96, 112, 128, 144, NT,
                                   extra_scale=(1.0 + EPS),
                                   extra_eps=EPS * (1.0 + EPS))
            dbg_store("ln2", stat[:, :])
            for i in range(NT):
                xs = xstg[:, (i % 2) * 2048:(i % 2) * 2048 + 2048]
                b.wait("gpsimd", "VE")
                b.wait_evt("gpsimd", ev_x1_stored)
                ex1 = b.dma("gpsimd", xs, x1s[i * 128:(i + 1) * 128, :], "LG")
                b.wait_evt("vector", ex1)
                vop(lambda e, xs=xs, i=i: e.tensor_scalar(
                    out=ztok[:, :], in0=xs, scalar1=colap(stat, 144 + i),
                    scalar2=colap(stat, 128 + i), op0=ALU.mult, op1=ALU.add),
                    inc=True)
                ev_xh = b.evt("VE")
                b.wait_evt("tensor", ev_xh)
                transpose_tile_set(ztok, i, 0, 2048)
            for k in range(KB):
                vop(lambda e, k=k: e.tensor_scalar(
                    out=actf[:, k * 2048:(k + 1) * 2048],
                    in0=actf[:, k * 2048:(k + 1) * 2048],
                    scalar1=colap(lncol, k), scalar2=colap(lncol, 16 + k),
                    op0=ALU.mult, op1=ALU.add), inc=(k == KB - 1))
            ev_xnf = b.evt("VE")
            dbg_store("xnf", actf[:, 0:2048])

        if upto >= 5:
            pp_bank_evac = [None] * 4
            wslc_reader = [None, None]

            def emit_proj(g, gate_v=None, gate_s=None, gate_pe=None):
                ev_ev = {}
                projs = [("zw", None), ("k", wkT), ("rz", wrT), ("v", wvT)]
                for pi, (pname, wsrc) in enumerate(projs):
                    slot = pi % 2
                    wap = wslc[:, slot * 2048:(slot + 1) * 2048]
                    wap3 = wap.rearrange("p (k n) -> p k n", n=128)
                    q = "sync" if wsrc is None else "gpsimd"
                    b.wait_evt(q, wslc_reader[slot])
                    if wsrc is None:
                        b.wait_evt("sync", ev_m_stored)
                        ew = b.dma("sync", wap3,
                                   mts[:, g * 128:(g + 1) * 128]
                                   .rearrange("(k p) n -> p k n", p=128), "LD")
                    else:
                        ew = b.dma("gpsimd", wap3,
                                   wsrc[:, g * 128:(g + 1) * 128]
                                   .rearrange("(k p) n -> p k n", p=128), "LG")
                    b.wait_evt("tensor", ew)
                    if g == 0 and pi == 0:
                        b.wait_evt("tensor", ev_xnf)
                    for bank in range(4):
                        b.wait_evt("tensor", pp_bank_evac[bank])
                        for K in range(KB):
                            mm(pp[:, bank * 512:(bank + 1) * 512],
                               wap[:, K * 128:(K + 1) * 128],
                               actf[:, K * 2048 + bank * 512:
                                    K * 2048 + bank * 512 + 512],
                               start=(K == 0), stop=(K == KB - 1),
                               inc=(K == KB - 1))
                        ev_mm = b.evt("PE")
                        dtile, doff = {"zw": (f32p, S_ZW), "k": (bbuf, B_K),
                                       "rz": (f32p, S_U),
                                       "v": (bbuf, B_V)}[pname]
                        dap = dtile[:, doff + bank * 512:
                                    doff + bank * 512 + 512]
                        if pname == "zw":
                            b.wait_evt("vector", ev_mm)
                            if bank == 0:
                                b.wait_evt("vector", gate_v)
                            vop(lambda e, dap=dap, bank=bank, g=g:
                                e.tensor_scalar(
                                    out=dap,
                                    in0=pp[:, bank * 512:bank * 512 + 512],
                                    scalar1=colap(lncol, 32 + g),
                                    scalar2=None, op0=ALU.add), inc=True)
                            ev_ev[(pname, bank)] = b.evt("VE")
                        else:
                            b.wait_evt("scalar", ev_mm)
                            if bank == 0:
                                b.wait_evt("scalar", gate_s)
                                b.wait_evt("scalar", gate_pe)
                                b.wait_evt("scalar", gate_v)
                            sop(lambda e, dap=dap, bank=bank: e.activation(
                                out=dap,
                                in_=pp[:, bank * 512:bank * 512 + 512],
                                func=AF.Copy), inc=True)
                            ev_ev[(pname, bank)] = b.evt("SE")
                        pp_bank_evac[bank] = ev_ev[(pname, bank)]
                    wslc_reader[slot] = b.evt("PE")
                return ev_ev

            ev_ev_next = emit_proj(0)
            for g in range(NG):
                ev_ev = ev_ev_next
                if "prep" not in p5:
                    continue
                zw = f32p[:, S_ZW:S_ZW + 2048]
                lam = f32p[:, S_LAM:S_LAM + 2048]
                uu = f32p[:, S_U:S_U + 2048]
                t1 = f32p[:, S_T1:S_T1 + 2048]
                t2 = f32p[:, S_T2:S_T2 + 2048]
                for bk in range(4):
                    b.wait_evt("scalar", ev_ev[("zw", bk)])
                sop(lambda e: e.activation(out=t1, in_=zw, func=AF.Abs))
                sop(lambda e: e.activation(out=t1, in_=t1, func=AF.Exp,
                                           scale=-1.0))
                sop(lambda e: e.activation(out=t1, in_=t1, func=AF.Ln,
                                           bias=1.0), inc=True)
                ev_ln = b.evt("SE")
                b.wait_evt("vector", ev_ln)
                vop(lambda e: e.tensor_scalar(out=zw, in0=zw, scalar1=0.0,
                                              scalar2=-1.0, op0=ALU.max,
                                              op1=ALU.mult))
                vop(lambda e: e.tensor_tensor(out=zw, in0=zw, in1=t1,
                                              op=ALU.subtract))
                vop(lambda e: e.tensor_tensor_scan(
                    out=lam, data0=zw, data1=zw, initial=0.0, op0=ALU.add,
                    op1=ALU.bypass))
                vop(lambda e: e.memset(ends[:, 0:1], 0.0))
                vop(lambda e: e.tensor_copy(
                    out=ends[:, 1:33],
                    in_=bass.AP(tensor=lam.tensor, offset=lam.offset + 63,
                                ap=[lam.ap[0], [64, 32]])))
                vop(lambda e: e.tensor_tensor(out=mcb[:, :], in0=ends[:, 0:32],
                                              in1=ends[:, 1:33], op=ALU.add))
                vop(lambda e: e.tensor_scalar(out=mcb[:, :], in0=mcb[:, :],
                                              scalar1=0.5, scalar2=None,
                                              op0=ALU.mult))
                vop(lambda e: e.tensor_tensor(out=eksb[:, :],
                                              in0=ends[:, 1:33],
                                              in1=mcb[:, :], op=ALU.subtract))
                vop(lambda e: e.tensor_tensor(out=edec[:, :],
                                              in0=ends[:, 1:33],
                                              in1=ends[:, 0:32],
                                              op=ALU.subtract), inc=True)
                ev_bnd = b.evt("VE")
                b.wait_evt("scalar", ev_bnd)
                sop(lambda e: e.activation(out=eksb[:, :], in_=eksb[:, :],
                                           func=AF.Exp))
                sop(lambda e: e.activation(out=edec[:, :], in_=edec[:, :],
                                           func=AF.Exp), inc=True)
                ev_small = b.evt("SE")
                for bk in range(4):
                    b.wait_evt("scalar", ev_ev[("rz", bk)])
                sop(lambda e: e.activation(out=t2, in_=uu, func=AF.Abs))
                sop(lambda e: e.activation(out=t2, in_=t2, func=AF.Exp,
                                           scale=-1.0))
                sop(lambda e: e.activation(out=t2, in_=t2, func=AF.Ln,
                                           bias=1.0), inc=True)
                ev_ru = b.evt("SE")
                b.wait_evt("vector", ev_ru)
                vop(lambda e: e.tensor_scalar(out=uu, in0=uu, scalar1=0.0,
                                              scalar2=None, op0=ALU.min))
                vop(lambda e: e.tensor_tensor(out=uu, in0=uu, in1=t2,
                                              op=ALU.subtract))
                vop(lambda e: e.tensor_tensor(out=t1, in0=lam,
                                              in1=bcast_ap(mcb, 32, 64),
                                              op=ALU.subtract))
                vop(lambda e: e.tensor_scalar(out=t2, in0=t1, scalar1=-CLAMP,
                                              scalar2=-1.0, op0=ALU.max,
                                              op1=ALU.mult), inc=True)
                ev_ak = b.evt("VE")
                b.wait_evt("scalar", ev_ak)
                sop(lambda e: e.activation(out=bbuf[:, B_EK:B_EK + 2048],
                                           in_=t2, func=AF.Exp), inc=True)
                ev_ek = b.evt("SE")
                vop(lambda e: e.tensor_scalar(out=t1, in0=t1, scalar1=CLAMP,
                                              scalar2=None, op0=ALU.min))
                vop(lambda e: e.tensor_tensor(out=t1, in0=t1, in1=uu,
                                              op=ALU.add), inc=True)
                ev_ar = b.evt("VE")
                b.wait_evt("scalar", ev_ar)
                sop(lambda e: e.activation(out=bbuf[:, B_RA:B_RA + 2048],
                                           in_=t1, func=AF.Exp), inc=True)
                ev_ra = b.evt("SE")
                vop(lambda e: e.tensor_tensor(out=t2, in0=lam,
                                              in1=bcast_ap(ends, 32, 64),
                                              op=ALU.subtract))
                vop(lambda e: e.tensor_tensor(out=t2, in0=t2, in1=uu,
                                              op=ALU.add), inc=True)
                ev_ai = b.evt("VE")
                b.wait_evt("scalar", ev_ai)
                sop(lambda e: e.activation(out=bbuf[:, B_RI:B_RI + 2048],
                                           in_=t2, func=AF.Exp), inc=True)
                ev_ri = b.evt("SE")
                if "tp" not in p5:
                    continue
                for bk in range(4):
                    b.wait_evt("tensor", ev_ev[("v", bk)])
                for j0 in (0, 8):
                    srcs = [bbuf[:, B_V + j * 128:B_V + (j + 1) * 128]
                            for j in range(j0, j0 + 8)]
                    dst = bbuf[:, B_VT + j0 * 128:B_VT + j0 * 128 + 1024]
                    transpose_batch8(
                        srcs, dst.rearrange("p (k n) -> p k n", n=128))
                ev_vt_done = b.evt("VE")
                for bk in range(4):
                    b.wait_evt("vector", ev_ev[("k", bk)])
                b.wait_evt("vector", ev_ek)
                b.wait_evt("vector", ev_small)
                b.wait("vector", "PE")
                vop(lambda e: e.tensor_tensor(
                    out=bbuf[:, B_V:B_V + 2048],
                    in0=bbuf[:, B_K:B_K + 2048],
                    in1=bbuf[:, B_EK:B_EK + 2048], op=ALU.mult))
                vop(lambda e: e.tensor_copy(
                    out=bass.AP(tensor=bbuf[:, :].tensor,
                                offset=bbuf[:, :].offset + B_KAP,
                                ap=[[18432, 64], [128, 32], [1, 64]]),
                    in_=bass.AP(tensor=bbuf[:, :].tensor,
                                offset=bbuf[:, :].offset + B_V,
                                ap=[[18432, 64], [64, 32], [1, 64]])))
                vop(lambda e: e.tensor_copy(
                    out=bass.AP(tensor=bbuf[:, :].tensor,
                                offset=bbuf[:, :].offset + 64 * 18432
                                + B_KAP + 64,
                                ap=[[18432, 64], [128, 32], [1, 64]]),
                    in_=bass.AP(tensor=bbuf[:, :].tensor,
                                offset=bbuf[:, :].offset + 64 * 18432 + B_V,
                                ap=[[18432, 64], [64, 32], [1, 64]])))
                vop(lambda e: e.tensor_tensor(
                    out=bbuf[:, B_K:B_K + 2048],
                    in0=bbuf[:, B_V:B_V + 2048],
                    in1=bcast_ap(eksb, 32, 64), op=ALU.mult), inc=True)
                ev_ks = b.evt("VE")
                b.wait_evt("tensor", ev_ks)
                for j0 in (0, 8):
                    srcs = [bbuf[:, B_K + j * 128:B_K + (j + 1) * 128]
                            for j in range(j0, j0 + 8)]
                    dst = bbuf[:, B_KT + j0 * 128:B_KT + j0 * 128 + 1024]
                    transpose_batch8(
                        srcs, dst.rearrange("p (k n) -> p k n", n=128))
                ev_tp_done = b.evt("VE")

                b.wait("vector", "PE")
                vop(lambda e: e.memset(s0b[:, :], 0.0))
                if g == 0:
                    vop(lambda e: e.memset(pa[:, :], 0.0))
                    vop(lambda e: e.memset(pu[0:64, 64:128], 0.0))
                    vop(lambda e: e.memset(pu[64:128, 0:64], 0.0))
                    vop(lambda e: e.memset(
                        bass.AP(tensor=bbuf[:, :].tensor,
                                offset=bbuf[:, :].offset + B_KAP + 64,
                                ap=[[18432, 64], [128, 32], [1, 64]]), 0.0))
                    vop(lambda e: e.memset(
                        bass.AP(tensor=bbuf[:, :].tensor,
                                offset=bbuf[:, :].offset + 64 * 18432 + B_KAP,
                                ap=[[18432, 64], [128, 32], [1, 64]]), 0.0))
                vop(lambda e: e.engine_nop() if False else e.memset(
                    s1b[:, 64:128], 0.0), inc=True)
                ev_s0 = b.evt("VE")
                b.wait_evt("tensor", ev_tp_done)
                b.wait_evt("tensor", ev_s0)
                b.wait_evt("tensor", ev_ra)
                b.wait_evt("tensor", ev_ri)

                if g + 1 < NG:
                    ev_ev_next = emit_proj(g + 1, gate_v=b.evt("VE"),
                                           gate_s=b.evt("SE"),
                                           gate_pe=b.evt("PE"))

                if "chunks" not in p5:
                    continue
                ev_stt_prev = None
                ev_yev_prev = None
                ev_as_prev = None
                ev_ogb_store = None
                for bat in range(8):
                    c0 = bat * 4
                    b.wait_evt("tensor", ev_as_prev)
                    for c in range(c0, c0 + 4):
                        par = (c & 1) * 64
                        for hh in (0, 1):
                            mm(pa[par:par + 64,
                                  ((c & 3) * 2 + hh) * 64:
                                  ((c & 3) * 2 + hh) * 64 + 64],
                               bbuf[:, B_KAP + c * 128 + hh * 64:
                                    B_KAP + c * 128 + hh * 64 + 64],
                               bbuf[:, B_RA + c * 64:B_RA + c * 64 + 64],
                               start=True, stop=True,
                               inc=(c == c0 + 3 and hh == 1),
                               tp=(0, par))
                    ev_ap = b.evt("PE")
                    b.wait_evt("vector", ev_ap)
                    vop(lambda e: e.tensor_tensor(
                        out=asb[:, :], in0=pa[:, :], in1=amask[:, :],
                        op=ALU.mult), inc=True)
                    ev_as = b.evt("VE")
                    ev_as_prev = ev_as
                    b.wait_evt("tensor", ev_as)
                    if "cy" not in p5:
                        continue
                    if c0 % 8 == 0 and ev_yev_prev is not None:
                        b.wait_evt("tensor", ev_yev_prev)
                    for c in range(c0, c0 + 4):
                        par = (c & 1) * 64
                        pair = c >> 1
                        scur = s0b if c % 2 == 0 else s1b
                        snxt = s1b if c % 2 == 0 else s0b
                        if ev_stt_prev is not None:
                            b.wait_evt("tensor", ev_stt_prev)
                        ypos = (c & 7) * 64
                        for hh, hb in ((0, 0), (1, 64)):
                            mm(py[hb:hb + 64, ypos:ypos + 64],
                               bbuf[:, B_VT + pair * 128 + hh * 64:
                                    B_VT + pair * 128 + hh * 64 + 64],
                               asb[:, ((c & 3) * 2 + hh) * 64:
                                   ((c & 3) * 2 + hh) * 64 + 64],
                               start=True, stop=False, tp=(0, hb))
                            mm(py[hb:hb + 64, ypos:ypos + 64],
                               scur[:, hh * 64:hh * 64 + 64],
                               bbuf[:, B_RI + c * 64:B_RI + c * 64 + 64],
                               start=False, stop=True,
                               inc=(hh == 1), tp=(0, hb))
                        ev_y = b.evt("PE")
                        for hh, hb in ((0, 0), (1, 64)):
                            mm(pu[hb:hb + 64, hb:hb + 64],
                               bbuf[par:par + 64,
                                    B_KT + pair * 128 + hh * 64:
                                    B_KT + pair * 128 + hh * 64 + 64],
                               bbuf[par:par + 64,
                                    B_VT + pair * 128 + hh * 64:
                                    B_VT + pair * 128 + hh * 64 + 64],
                               start=True, stop=True, inc=(hh == 1),
                               tp=(par, hb))
                        ev_upd = b.evt("PE")
                        b.wait_evt("vector", ev_upd)
                        vop(lambda e, sc=scur, sn=snxt, c=c:
                            e.scalar_tensor_tensor(
                                out=sn[:, :], in0=sc[:, :],
                                scalar=colap(edec, c), in1=pu[:, 0:128],
                                op0=ALU.mult, op1=ALU.add), inc=True)
                        ev_stt_prev = b.evt("VE")
                    if "cs" not in p5:
                        continue
                    if bat % 2 == 1:
                        b.wait_evt("vector", ev_y)
                        b.wait_evt("vector", ev_ogb_store)
                        vop(lambda e: e.tensor_copy(out=ogb[:, :],
                                                    in_=py[:, :]), inc=True)
                        ev_yev_prev = b.evt("VE")
                        b.wait_evt("sync", ev_yev_prev)
                        ev_ogb_store = b.dma(
                            "sync", outss[g * 128:(g + 1) * 128,
                                          (c0 - 4) * 64:(c0 - 4) * 64 + 512],
                            ogb[:, :], "LD")
            ev_scan_done = b.evt("LD")

        if upto >= 6:
            b.wait("vector", "PE")
            b.wait("sync", "PE")
            b.wait("sync", "VE")
            b.wait("sync", "SE")
            ev_sel_parts = []
            for k in range(KB):
                t0 = bbuf[:, B_K + (k % 2) * 2048: B_K + (k % 2) * 2048 + 1024]
                t1b = bbuf[:, B_RA + (k % 2) * 2048:
                           B_RA + (k % 2) * 2048 + 1024]
                b.wait("sync", "VE")
                b.wait_evt("sync", ev_scan_done)
                el0 = b.dma("sync", t0, outss[k * 128:(k + 1) * 128, 0:1024],
                            "LD")
                el1 = b.dma("sync", t1b,
                            outss[k * 128:(k + 1) * 128, 1024:2048], "LD")
                b.wait_evt("vector", el1)
                vop(lambda e, t0=t0: e.tensor_scalar(
                    out=t0, in0=t0, scalar1=mcols[:, 0:1], scalar2=None,
                    op0=ALU.mult))
                vop(lambda e, t0=t0, t1b=t1b, k=k: e.scalar_tensor_tensor(
                    out=actf[:, k * 2048:k * 2048 + 1024], in0=t1b,
                    scalar=mcols[:, 1:2], in1=t0, op0=ALU.mult, op1=ALU.add),
                    inc=True)
            ev_sel = b.evt("VE")
            dbg_store("omf", actf[:, 0:1024])
            reg_evac = {0: None, 1: None}
            ev_sq_prev = None
            ev_store_prev = {0: None, 1: None}
            for half in range(2):
                b.wait("gpsimd", "PE")
                b.wait_evt("gpsimd", ev_sel)
                ewo = b.dma("gpsimd",
                            bbuf[:, 0:16384].rearrange("p (k n) -> p k n", n=1024),
                            woT[:, half * 1024:(half + 1) * 1024]
                            .rearrange("(k p) n -> p k n", p=128), "LG")
                b.wait_evt("tensor", ewo)
                if half == 0:
                    b.wait_evt("tensor", ev_sel)
                for i in range(WT):
                    reg = (i % 2) * 1024
                    b.wait_evt("tensor", reg_evac[i % 2])
                    for nb in range(2):
                        po = reg + nb * 512
                        for K in range(KB):
                            mm(pp[:, po:po + 512],
                               actf[:, K * 2048 + i * 128:
                                    K * 2048 + i * 128 + 128],
                               bbuf[:, K * 1024 + nb * 512:
                                    K * 1024 + nb * 512 + 512],
                               start=(K == 0), stop=(K == KB - 1),
                               inc=(K == KB - 1))
                    ev_ymm = b.evt("PE")
                    xl = xstg[:, (i % 2) * 2048:(i % 2) * 2048 + 1024]
                    xh = xstg[:, (i % 2) * 2048 + 1024:(i % 2) * 2048 + 2048]
                    b.wait("sync", "VE")
                    b.wait("sync", "SE")
                    b.wait_evt("sync", ev_store_prev[i % 2])
                    exlo = b.dma("sync", xl,
                                 x1s[i * 128:(i + 1) * 128,
                                     half * 1024:(half + 1) * 1024], "LD")
                    exhi = b.dma("sync", xh,
                                 x1s[1024 + i * 128:1024 + (i + 1) * 128,
                                     half * 1024:(half + 1) * 1024], "LD")
                    b.wait_evt("vector", exhi)
                    b.wait_evt("vector", ev_ymm)
                    vop(lambda e, xl=xl: e.tensor_scalar(
                        out=xl, in0=xl, scalar1=mcols[:, 0:1], scalar2=None,
                        op0=ALU.mult))
                    vop(lambda e, xl=xl, xh=xh: e.scalar_tensor_tensor(
                        out=xl, in0=xh, scalar=mcols[:, 1:2], in1=xl,
                        op0=ALU.mult, op1=ALU.add))
                    vop(lambda e, xl=xl, reg=reg:
                        e.scalar_tensor_tensor(
                            out=xl, in0=pp[:, reg:reg + 1024], scalar=1.0,
                            in1=xl, op0=ALU.mult, op1=ALU.add), inc=True)
                    ev_x2 = b.evt("VE")
                    reg_evac[i % 2] = ev_x2
                    b.wait_evt("scalar", ev_x2)
                    sop(lambda e, xl=xl, half=half, i=i: e.activation(
                        out=f32p[:, S_T1:S_T1 + 1024], in_=xl, func=AF.Square,
                        accum_out=colap(stat, 32 + 16 * half + i)))
                    sop(lambda e, xl=xl, half=half, i=i: e.activation(
                        out=f32p[:, S_T1:S_T1 + 1024], in_=xl, func=AF.Copy,
                        accum_out=colap(stat, 16 * half + i)), inc=True)
                    b.wait_evt("sync", ev_x2)
                    ev_store_prev[i % 2] = b.dma(
                        "sync", x1s[i * 128:(i + 1) * 128,
                                    half * 1024:(half + 1) * 1024], xl, "LD")
                ev_pe_wo = b.evt("PE")
                b.wait_evt("gpsimd", ev_pe_wo)
            ev_x2_stored = b.evt("LD")
            b.wait("vector", "SE")
            ev_ln3 = ln_batch_math(0, 16, 32, 48, 64, 80, WT)
            for i in range(WT):
                xs = xstg[:, (i % 2) * 2048:(i % 2) * 2048 + 2048]
                b.wait("gpsimd", "VE")
                b.wait_evt("gpsimd", ev_x2_stored)
                ex2 = b.dma("gpsimd", xs, x1s[i * 128:(i + 1) * 128, :], "LG")
                b.wait_evt("vector", ex2)
                vop(lambda e, xs=xs, i=i: e.tensor_scalar(
                    out=ztok[:, :], in0=xs, scalar1=colap(stat, 80 + i),
                    scalar2=colap(stat, 64 + i), op0=ALU.mult, op1=ALU.add),
                    inc=True)
                ev_z3 = b.evt("VE")
                b.wait_evt("tensor", ev_z3)
                transpose_tile_set(ztok, i, 1024, 2048)
            ev_z3f = b.evt("VE")
            dbg_store("z3f", actf[:, 1024:2048])

        if upto >= 7:
            for i in range(WT):
                b.wait("tensor", "VE")
                for K in range(KB):
                    mm(pu[:, 256:392],
                       actf[:, K * 2048 + 1024 + i * 128:
                            K * 2048 + 1024 + i * 128 + 128],
                       atgb[:, K * 136:(K + 1) * 136],
                       start=(K == 0), stop=(K == KB - 1), inc=(K == KB - 1))
                ev_tl = b.evt("PE")
                b.wait_evt("vector", ev_tl)
                vop(lambda e: e.tensor_copy(out=ogb[:, 0:128],
                                            in_=pu[:, 256:384]))
                vop(lambda e: e.tensor_copy(out=tpt[:, 8:16],
                                            in_=pu[:, 384:392]), inc=True)
                ev_tg = top2_gate(None)
                b.wait_evt("tensor", ev_tg)
                transpose_batch8([ogb[:, 0:128]],
                                 ttf[:, i * 128:(i + 1) * 128]
                                 .rearrange("p (k n) -> p k n", n=128))
            reg_evac = {0: None, 1: None}
            for half in range(2):
                b.wait("gpsimd", "PE")
                ewf = b.dma("gpsimd",
                            bbuf[:, 0:16384].rearrange("p (k n) -> p k n", n=1024),
                            wfT[:, half * 1024:(half + 1) * 1024]
                            .rearrange("(k p) n -> p k n", p=128), "LG")
                b.wait_evt("tensor", ewf)
                for i in range(WT):
                    reg = (i % 2) * 1024
                    xs = xstg[:, (i % 2) * 2048 + half * 1024:
                              (i % 2) * 2048 + half * 1024 + 1024]
                    b.wait("sync", "VE")
                    exl = b.dma("sync", xs,
                                x1s[i * 128:(i + 1) * 128,
                                    half * 1024:(half + 1) * 1024], "LD")
                    b.wait_evt("tensor", reg_evac[i % 2])
                    for nb in range(2):
                        po = reg + nb * 512
                        for K in range(KB):
                            mm(pp[:, po:po + 512],
                               actf[:, K * 2048 + 1024 + i * 128:
                                    K * 2048 + 1024 + i * 128 + 128],
                               bbuf[:, K * 1024 + nb * 512:
                                    K * 1024 + nb * 512 + 512],
                               start=(K == 0), stop=False)
                        mm(pp[:, po:po + 512], ttf[:, i * 128:(i + 1) * 128],
                           btb[:, half * 1024 + nb * 512:
                               half * 1024 + nb * 512 + 512],
                           start=False, stop=True, inc=True)
                    ev_h = b.evt("PE")
                    b.wait_evt("vector", exl)
                    b.wait_evt("vector", ev_h)
                    vop(lambda e, xs=xs, reg=reg: e.scalar_tensor_tensor(
                        out=xs, in0=pp[:, reg:reg + 1024], scalar=0.5, in1=xs,
                        op0=ALU.mult, op1=ALU.add), inc=True)
                    ev_x3 = b.evt("VE")
                    reg_evac[i % 2] = ev_x3
                    b.wait_evt("sync", ev_x3)
                    b.dma("sync", out_t[i * 128:(i + 1) * 128,
                                        half * 1024:(half + 1) * 1024], xs,
                          "LD")
            b.wait("sync", "LD")
            b.wait("gpsimd", "LG")

        with nc.Block() as blk:
            for eng_name in KBd.ENGINES:
                opslist = b.ops[eng_name]
                def mk(opslist):
                    def run(e):
                        for f in opslist:
                            f(e)
                    return run
                getattr(blk, eng_name)(mk(opslist))

    return nc


def _layout_inputs(inputs, core):
    bidx = core // 2
    half = core % 2
    cat = np.ascontiguousarray
    A_w = inputs["L0_ffn_A_w"]
    gate = inputs["L0_ffn_gate_w"]
    ATg = np.concatenate([A_w.transpose(2, 0, 1).reshape(C, NE * NR),
                          np.asarray(gate).T], axis=1)
    amask = np.zeros((128, 512), np.float32)
    tri = np.triu(np.ones((64, 64), np.float32))
    for cl in range(4):
        par = (cl & 1) * 64
        for hh in range(2):
            blkc = (cl * 2 + hh) * 64
            amask[par:par + 64, blkc:blkc + 64] = tri
    colsify = lambda v: cat(np.asarray(v).reshape(16, 128).T.astype(np.float32))
    m0 = np.full((128, 1), 1.0 if half == 0 else 0.0, np.float32)
    m1 = np.full((128, 1), 0.0 if half == 0 else 1.0, np.float32)
    f32c = lambda a: cat(np.asarray(a).astype(np.float32, copy=False))
    return {
        "x": f32c(inputs["L0_attn_x"][bidx]),
        "WkT": f32c(np.asarray(inputs["L0_attn_Wk"]).T),
        "WvT": f32c(np.asarray(inputs["L0_attn_Wv"]).T),
        "WrT": f32c(np.asarray(inputs["L0_attn_Wr"]).T),
        "WoT": f32c(np.asarray(inputs["L0_attn_Wo"]).T),
        "WfT": f32c(np.asarray(inputs["L0_ffn_weight"]).T),
        "WwT": f32c(np.asarray(inputs["L0_attn_Ww"]).T),
        "Wx": f32c(inputs["L0_attn_Wx"]),
        "ATg": f32c(ATg),
        "BT": f32c(np.asarray(inputs["L0_ffn_B_w"]).transpose(0, 2, 1)
                   .reshape(NE * NR, C)),
        "lns": colsify(inputs["L0_attn_ln_scale"]),
        "lnb": colsify(inputs["L0_attn_ln_bias"]),
        "bw": colsify(inputs["L0_attn_bw"]),
        "m0": m0, "m1": m1,
        "eye": np.eye(128, dtype=np.float32),
        "amask": amask,
    }


def kernel(**inputs):
    nc = build_program()
    in_maps = [_layout_inputs(inputs, c) for c in range(8)]
    res = run_bass_kernel_spmd(nc, in_maps, core_ids=list(range(8)))
    out = np.empty((4, T, C), np.float32)
    for c in range(8):
        bidx, half = c // 2, c % 2
        out[bidx, half * WTOK:(half + 1) * WTOK, :] = res.results[c]["out"]
    return out


# version 03
# speedup vs baseline: 1.0824x; 1.0345x over previous
import numpy as np
from contextlib import ExitStack

import concourse.bass as bass
import concourse.mybir as mybir
from concourse.bass_utils import run_bass_kernel_spmd

F32 = mybir.dt.float32
BF16 = mybir.dt.bfloat16
AF = mybir.ActivationFunctionType
ALU = mybir.AluOpType
AXL = mybir.AxisListType

T = 2048
C = 2048
NT = 16
KB = 16
NG = 16
L = 64
NCH = 32
WTOK = 1024
WT = 8
EPS = 1e-5
CLAMP = 43.0
NE, NR = 8, 16
ALPHA = 16.0

S_ZW, S_LAM, S_U, S_T1, S_T2 = 0, 2048, 4096, 6144, 8192
B_K, B_K2, B_EK, B_RA, B_RI, B_V, B_V2, B_VT, B_KT, B_KAP = (
    0, 2048, 4096, 6144, 8192, 10240, 12288, 14336, 16384, 18432)


class KBd:

    ENGINES = ("sync", "gpsimd", "tensor", "vector", "scalar")

    def __init__(self, sems):
        self.sems = sems
        self.cnt = {k: 0 for k in sems}
        self.ops = {e: [] for e in self.ENGINES}

    def emit(self, eng, fn):
        self.ops[eng].append(fn)

    def wait(self, eng, ch, val=None):
        v = self.cnt[ch] if val is None else val
        if v <= 0:
            return
        h = self.sems[ch]
        self.emit(eng, lambda e: e.wait_ge(h, v))

    def evt(self, ch):
        return (ch, self.cnt[ch])

    def wait_evt(self, eng, evt):
        if evt is None:
            return
        ch, v = evt
        if ch in ("LD", "LG"):
            v = self.cnt[ch]
        if v <= 0:
            return
        h = self.sems[ch]
        self.emit(eng, lambda e: e.wait_ge(h, v))

    def dma(self, qeng, out, in_, ch, **kw):
        h = self.sems[ch]
        self.cnt[ch] += 16
        self.emit(qeng, lambda e: e.dma_start(out=out, in_=in_, **kw)
                  .then_inc(h, 16))
        return (ch, self.cnt[ch])


def build_program(dbg=None, upto=99,
                  p5=("proj", "prep", "tp", "chunks", "cy", "cu", "cs")):
    nc = bass.Bass("TRN2", target_bir_lowering=False,
                   detect_race_conditions=False)

    dt_in = lambda n, s: nc.dram_tensor(n, list(s), F32, kind="ExternalInput").ap()
    x_in = dt_in("x", (T, C))
    wkT = dt_in("WkT", (C, C))
    wvT = dt_in("WvT", (C, C))
    wrT = dt_in("WrT", (C, C))
    woT = dt_in("WoT", (C, C))
    wfT = dt_in("WfT", (C, C))
    wwT = dt_in("WwT", (C, C))
    wx = dt_in("Wx", (C, C))
    atg = dt_in("ATg", (C, NE * NR + NE))
    btv = dt_in("BT", (NE * NR, C))
    lns_i = dt_in("lns", (128, 16))
    lnb_i = dt_in("lnb", (128, 16))
    bw_i = dt_in("bw", (128, 16))
    m0v = dt_in("m0", (128, 1))
    m1v = dt_in("m1", (128, 1))
    eyev = dt_in("eye", (128, 128))
    amaskv = dt_in("amask", (128, 512))

    out_t = nc.dram_tensor("out", [WTOK, C], F32, kind="ExternalOutput").ap()
    dbg_t = None
    if dbg is not None:
        dbg_t = nc.dram_tensor("dbg", list(dbg[1]), F32, kind="ExternalOutput").ap()

    x1s = nc.dram_tensor("x1s", [T, C], F32, kind="Internal").ap()
    mts = nc.dram_tensor("mts", [C, C], BF16, kind="Internal").ap()
    outss = nc.dram_tensor("outss", [C, T], BF16, kind="Internal").ap()

    with ExitStack() as st:
        sb = lambda n, s, d: st.enter_context(nc.sbuf_tensor(n, list(s), d))
        psm = lambda n, s, d: st.enter_context(nc.psum_tensor(n, list(s), d))
        actf = sb("actf", (128, 32768), BF16)
        f32p = sb("f32p", (128, 10240), F32)
        bbuf = sb("bbuf", (128, 22528), BF16)
        ttf = sb("ttf", (128, 2048), BF16)
        btb = sb("btb", (128, 2048), BF16)
        atgb = sb("atgb", (128, 2176), BF16)
        eyeb = sb("eyeb", (128, 128), BF16)
        amask = sb("amask_s", (128, 512), F32)
        ztok = sb("ztok", (128, 2048), BF16)
        wslc = sb("wslc", (128, 4096), BF16)
        asb = sb("asb", (128, 512), BF16)
        ogb = sb("ogb", (128, 512), BF16)
        lncol = sb("lncol", (128, 48), F32)
        mcols = sb("mcols", (128, 2), F32)
        stat = sb("stat", (128, 160), F32)
        ends = sb("ends", (128, 33), F32)
        mcb = sb("mcb", (128, 32), F32)
        eksb = sb("eksb", (128, 32), F32)
        edec = sb("edec", (128, 32), F32)
        s0b = sb("s0b", (128, 128), BF16)
        s1b = sb("s1b", (128, 128), BF16)
        tpt = sb("tpt", (128, 64), F32)
        mstg = sb("mstg", (128, 1024), BF16)
        xstg = sb("xstg", (128, 4096), F32)

        pp = psm("pp", (128, 2048), F32)
        pa = psm("pa", (128, 512), F32)
        py = psm("py", (128, 512), F32)
        ptb = psm("ptb", (128, 1024), BF16)
        pu = psm("pu", (128, 512), F32)

        sems = {}
        for name in ("LD", "LG", "PE", "VE", "SE"):
            sems[name] = st.enter_context(nc.semaphore(name))
        b = KBd(sems)
        S = sems

        def mm(outap, lhs, rhs, start, stop, inc=False, tp=None, sgc=False):
            h = S["PE"]
            if inc:
                b.cnt["PE"] += 1
            def f(e, outap=outap, lhs=lhs, rhs=rhs, start=start, stop=stop,
                  inc=inc, tp=tp, sgc=sgc):
                i = e.matmul(outap, lhs, rhs, start=start, stop=stop,
                             tile_position=tp, skip_group_check=sgc)
                if inc:
                    i.then_inc(h, 1)
            b.emit("tensor", f)

        def tpose(outap, inap, inc=False, tp=None):
            h = S["PE"]
            if inc:
                b.cnt["PE"] += 1
            def f(e, outap=outap, inap=inap, inc=inc, tp=tp):
                i = e.transpose(outap, inap, eyeb[:, :], tile_position=tp)
                if inc:
                    i.then_inc(h, 1)
            b.emit("tensor", f)

        def vop(fn, inc=True):
            h = S["VE"]
            prev = b.cnt["VE"]
            b.cnt["VE"] += 1
            def f(e, fn=fn, prev=prev):
                if prev > 0:
                    e.wait_ge(h, prev)
                fn(e).then_inc(h, 1)
            b.emit("vector", f)

        def sop(fn, inc=False):
            h = S["SE"]
            if inc:
                b.cnt["SE"] += 1
            def f(e, fn=fn, inc=inc):
                i = fn(e)
                if inc:
                    i.then_inc(h, 1)
            b.emit("scalar", f)

        def bcast_ap(t2d, ncols, rep, coloff=0):
            a = t2d[:, coloff:coloff + ncols]
            return bass.AP(tensor=a.tensor, offset=a.offset,
                           ap=[a.ap[0], [a.ap[1][0], ncols], [0, rep]])

        def colap(t2d, j):
            return t2d[:, j:j + 1]

        dbg_rows = [0]
        def dbg_store(name, sbap, nrows=128):
            if dbg is None or dbg[0] != name:
                return
            r = dbg_rows[0]
            b.wait("gpsimd", "VE")
            b.wait("gpsimd", "SE")
            b.wait("gpsimd", "PE")
            b.dma("gpsimd", dbg_t[r:r + nrows, :], sbap, "LG")
            b.wait("gpsimd", "LG")
            dbg_rows[0] += nrows

        def top2_gate(src_logits_ap):
            vop(lambda e: e.reduce_max(tpt[:, 0:1], tpt[:, 8:16], axis=AXL.X))
            vop(lambda e: e.tensor_scalar(
                out=tpt[:, 16:24], in0=tpt[:, 8:16], scalar1=tpt[:, 0:1],
                scalar2=None, op0=ALU.is_equal))
            vop(lambda e: e.scalar_tensor_tensor(
                out=tpt[:, 24:32], in0=tpt[:, 16:24], scalar=-1e30,
                in1=tpt[:, 8:16], op0=ALU.mult, op1=ALU.add))
            vop(lambda e: e.reduce_max(tpt[:, 1:2], tpt[:, 24:32], axis=AXL.X))
            vop(lambda e: e.tensor_scalar(
                out=tpt[:, 32:40], in0=tpt[:, 24:32], scalar1=tpt[:, 1:2],
                scalar2=None, op0=ALU.is_equal))
            vop(lambda e: e.tensor_tensor(
                out=tpt[:, 2:3], in0=tpt[:, 0:1], in1=tpt[:, 1:2],
                op=ALU.subtract), inc=True)
            ev_d = b.evt("VE")
            b.wait_evt("scalar", ev_d)
            sop(lambda e: e.activation(out=tpt[:, 3:4], in_=tpt[:, 2:3],
                                       func=AF.Exp, scale=-1.0))
            sop(lambda e: e.activation(out=tpt[:, 3:4], in_=tpt[:, 3:4],
                                       func=AF.Ln, bias=1.0))
            sop(lambda e: e.activation(out=tpt[:, 4:5], in_=tpt[:, 3:4],
                                       func=AF.Exp, scale=-1.0), inc=True)
            ev_w1 = b.evt("SE")
            b.wait_evt("vector", ev_w1)
            vop(lambda e: e.tensor_scalar(
                out=tpt[:, 5:6], in0=tpt[:, 4:5], scalar1=-1.0, scalar2=1.0,
                op0=ALU.mult, op1=ALU.add))
            vop(lambda e: e.tensor_scalar(
                out=tpt[:, 16:24], in0=tpt[:, 16:24], scalar1=tpt[:, 4:5],
                scalar2=None, op0=ALU.mult))
            vop(lambda e: e.tensor_scalar(
                out=tpt[:, 32:40], in0=tpt[:, 32:40], scalar1=tpt[:, 5:6],
                scalar2=None, op0=ALU.mult))
            vop(lambda e: e.tensor_tensor(
                out=tpt[:, 16:24], in0=tpt[:, 16:24], in1=tpt[:, 32:40],
                op=ALU.add))
            vop(lambda e: e.tensor_scalar(
                out=tpt[:, 16:24], in0=tpt[:, 16:24], scalar1=ALPHA,
                scalar2=None, op0=ALU.mult))
            for ex in range(NE):
                vop(lambda e, ex=ex: e.tensor_scalar(
                    out=ogb[:, ex * 16:(ex + 1) * 16],
                    in0=ogb[:, ex * 16:(ex + 1) * 16],
                    scalar1=colap(tpt, 16 + ex), scalar2=None, op0=ALU.mult),
                    inc=(ex == NE - 1))
            return b.evt("VE")

        def ln_batch_math(s1a, s1b_, s2a, s2b, mo, ro, n, extra_scale=None,
                          extra_eps=EPS):
            for i in range(n):
                if s1b_ is not None:
                    vop(lambda e, i=i: e.tensor_tensor(
                        out=colap(stat, s1a + i), in0=colap(stat, s1a + i),
                        in1=colap(stat, s1b_ + i), op=ALU.add))
                    vop(lambda e, i=i: e.tensor_tensor(
                        out=colap(stat, s2a + i), in0=colap(stat, s2a + i),
                        in1=colap(stat, s2b + i), op=ALU.add))
                vop(lambda e, i=i: e.tensor_scalar(
                    out=colap(stat, mo + i), in0=colap(stat, s1a + i),
                    scalar1=1.0 / C, scalar2=None, op0=ALU.mult))
                vop(lambda e, i=i: e.tensor_scalar(
                    out=colap(stat, ro + i), in0=colap(stat, s2a + i),
                    scalar1=1.0 / C, scalar2=None, op0=ALU.mult))
                vop(lambda e, i=i: e.tensor_tensor(
                    out=tpt[:, 0:1], in0=colap(stat, mo + i),
                    in1=colap(stat, mo + i), op=ALU.mult))
                vop(lambda e, i=i: e.tensor_tensor(
                    out=colap(stat, ro + i), in0=colap(stat, ro + i),
                    in1=tpt[:, 0:1], op=ALU.subtract),
                    inc=False)
                if extra_scale is None:
                    vop(lambda e, i=i: e.tensor_scalar(
                        out=colap(stat, ro + i), in0=colap(stat, ro + i),
                        scalar1=extra_eps, scalar2=None, op0=ALU.add),
                        inc=True)
                if extra_scale is not None:
                    vop(lambda e, i=i: e.tensor_scalar(
                        out=colap(stat, ro + i), in0=colap(stat, ro + i),
                        scalar1=extra_scale, scalar2=extra_eps, op0=ALU.mult,
                        op1=ALU.add), inc=True)
            ev_v = b.evt("VE")
            b.wait_evt("scalar", ev_v)
            for i in range(n):
                sop(lambda e, i=i: e.activation(
                    out=tpt[:, 1:2], in_=colap(stat, ro + i), func=AF.Ln))
                sop(lambda e, i=i: e.activation(
                    out=colap(stat, ro + i), in_=tpt[:, 1:2], func=AF.Exp,
                    scale=-0.5), inc=(i == n - 1))
            ev_s = b.evt("SE")
            b.wait_evt("vector", ev_s)
            for i in range(n):
                vop(lambda e, i=i: e.tensor_tensor(
                    out=colap(stat, mo + i), in0=colap(stat, mo + i),
                    in1=colap(stat, ro + i), op=ALU.mult))
                vop(lambda e, i=i: e.tensor_scalar(
                    out=colap(stat, mo + i), in0=colap(stat, mo + i),
                    scalar1=-1.0, scalar2=None, op0=ALU.mult),
                    inc=(i == n - 1))
            return b.evt("VE")

        ptb_evac = [None]

        def transpose_batch8(srcs, dst3ap):
            nsl = len(srcs)
            b.wait_evt("tensor", ptb_evac[0])
            for sl, src in enumerate(srcs):
                tpose(ptb[:, sl * 128:(sl + 1) * 128], src,
                      inc=(sl == nsl - 1))
            ev_tp = b.evt("PE")
            b.wait_evt("vector", ev_tp)
            vop(lambda e, dst3ap=dst3ap, nsl=nsl: e.tensor_copy(
                out=dst3ap, in_=ptb[:, 0:nsl * 128]
                .rearrange("p (k n) -> p k n", n=128)), inc=True)
            ptb_evac[0] = b.evt("VE")

        def transpose_tile_set(src_tok, i, base, width):
            for half8 in range(2):
                k0 = half8 * 8
                srcs = [src_tok[:, k * 128:(k + 1) * 128]
                        for k in range(k0, k0 + 8)]
                dst = bass.AP(
                    tensor=actf[:, :].tensor,
                    offset=actf[:, :].offset + (k0 * 2048 + base + i * 128),
                    ap=[actf[:, :].ap[0], [2048, 8], [1, 128]])
                transpose_batch8(srcs, dst)

        b.dma("gpsimd", eyeb[:, :], eyev, "LG")
        b.dma("sync", amask[:, :], amaskv, "LD")
        b.dma("sync", lncol[:, 0:16], lns_i, "LD")
        b.dma("sync", lncol[:, 16:32], lnb_i, "LD")
        b.dma("sync", lncol[:, 32:48], bw_i, "LD")
        b.dma("sync", mcols[:, 0:1], m0v, "LD")
        b.dma("sync", mcols[:, 1:2], m1v, "LD")
        b.dma("gpsimd", btb[:, :], btv, "LG")
        e_atg = b.dma("gpsimd", atgb[:, :].rearrange("p (k n) -> p k n", n=136),
                      atg.rearrange("(k p) n -> p k n", p=128), "LG")
        e_wx = b.dma("gpsimd", actf[:, :].rearrange("p (k n) -> p k n", n=2048),
                     wx.rearrange("(k p) n -> p k n", p=128), "LG")
        ev_consts_ld = b.evt("LD")

        ev_m_stored = None
        ev_pe_m = None
        if upto >= 1:
            bank_evac = {0: None, 1: None}
            for ebank in range(4):
                ewl = b.dma("gpsimd",
                            bbuf[:, 0:8192].rearrange("p (k n) -> p k n", n=512),
                            wwT[:, ebank * 512:(ebank + 1) * 512]
                            .rearrange("(k p) n -> p k n", p=128), "LG")
                b.wait_evt("tensor", ewl)
                if ebank == 0:
                    b.wait_evt("tensor", e_wx)
                for Ct in range(16):
                    pb = Ct % 2
                    b.wait_evt("tensor", bank_evac[pb])
                    for K in range(16):
                        mm(pp[:, pb * 512:(pb + 1) * 512],
                           actf[:, K * 2048 + Ct * 128:
                                K * 2048 + Ct * 128 + 128],
                           bbuf[:, K * 512:(K + 1) * 512],
                           start=(K == 0), stop=(K == 15), inc=(K == 15))
                    ev_pe = b.evt("PE")
                    stg = mstg[:, pb * 512:pb * 512 + 512]
                    b.wait_evt("scalar", ev_pe)
                    b.wait("scalar", "LD")
                    sop(lambda e, stg=stg, pb=pb: e.activation(
                        out=stg, in_=pp[:, pb * 512:pb * 512 + 512],
                        func=AF.Copy), inc=True)
                    ev_se = b.evt("SE")
                    bank_evac[pb] = ev_se
                    b.wait_evt("sync", ev_se)
                    b.dma("sync", mts[Ct * 128:(Ct + 1) * 128,
                                      ebank * 512:(ebank + 1) * 512], stg, "LD")
                ev_pe_m = b.evt("PE")
                b.wait_evt("gpsimd", ev_pe_m)
            ev_m_stored = b.evt("LD")

        if upto >= 2:
            for i in range(NT):
                xs = xstg[:, (i % 2) * 2048:(i % 2) * 2048 + 2048]
                b.wait("gpsimd", "VE")
                b.wait("gpsimd", "SE")
                exl = b.dma("gpsimd", xs, x_in[i * 128:(i + 1) * 128, :], "LG")
                b.wait_evt("vector", exl)
                vop(lambda e, xs=xs, i=i: e.reduce_sum(
                    colap(stat, i), xs, axis=AXL.X), inc=True)
                b.wait_evt("scalar", exl)
                sop(lambda e, xs=xs, i=i: e.activation(
                    out=f32p[:, S_T1:S_T1 + 2048], in_=xs, func=AF.Square,
                    accum_out=colap(stat, 16 + i)), inc=True)
                ev_sq = b.evt("SE")
                b.wait_evt("vector", ev_sq)
                vop(lambda e, i=i: e.tensor_scalar(
                    out=colap(stat, 32 + i), in0=colap(stat, i),
                    scalar1=1.0 / C, scalar2=None, op0=ALU.mult))
                vop(lambda e, i=i: e.tensor_scalar(
                    out=colap(stat, 48 + i), in0=colap(stat, 16 + i),
                    scalar1=1.0 / C, scalar2=None, op0=ALU.mult))
                vop(lambda e, i=i: e.tensor_tensor(
                    out=tpt[:, 0:1], in0=colap(stat, 32 + i),
                    in1=colap(stat, 32 + i), op=ALU.mult))
                vop(lambda e, i=i: e.tensor_tensor(
                    out=colap(stat, 48 + i), in0=colap(stat, 48 + i),
                    in1=tpt[:, 0:1], op=ALU.subtract))
                vop(lambda e, i=i: e.tensor_scalar(
                    out=colap(stat, 48 + i), in0=colap(stat, 48 + i),
                    scalar1=EPS, scalar2=None, op0=ALU.add), inc=True)
                ev_var = b.evt("VE")
                b.wait_evt("scalar", ev_var)
                sop(lambda e, i=i: e.activation(out=tpt[:, 1:2],
                                                in_=colap(stat, 48 + i),
                                                func=AF.Ln))
                sop(lambda e, i=i: e.activation(
                    out=colap(stat, 48 + i), in_=tpt[:, 1:2], func=AF.Exp,
                    scale=-0.5), inc=True)
                ev_rstd = b.evt("SE")
                b.wait_evt("vector", ev_rstd)
                vop(lambda e, i=i: e.tensor_tensor(
                    out=colap(stat, 32 + i), in0=colap(stat, 32 + i),
                    in1=colap(stat, 48 + i), op=ALU.mult))
                vop(lambda e, i=i: e.tensor_scalar(
                    out=colap(stat, 32 + i), in0=colap(stat, 32 + i),
                    scalar1=-1.0, scalar2=None, op0=ALU.mult))
                vop(lambda e, xs=xs, i=i: e.tensor_scalar(
                    out=ztok[:, :], in0=xs, scalar1=colap(stat, 48 + i),
                    scalar2=colap(stat, 32 + i), op0=ALU.mult, op1=ALU.add),
                    inc=True)
                ev_z1 = b.evt("VE")
                if i == 0 and ev_pe_m is not None:
                    b.wait_evt("tensor", ev_pe_m)
                b.wait_evt("tensor", ev_z1)
                transpose_tile_set(ztok, i, 0, 2048)
                b.wait("tensor", "VE")
                if i == 0:
                    b.wait_evt("tensor", e_atg)
                for K in range(KB):
                    mm(pu[:, 256:392],
                       actf[:, K * 2048 + i * 128: K * 2048 + i * 128 + 128],
                       atgb[:, K * 136:(K + 1) * 136],
                       start=(K == 0), stop=(K == KB - 1), inc=(K == KB - 1))
                ev_tl = b.evt("PE")
                b.wait_evt("vector", ev_tl)
                vop(lambda e: e.tensor_copy(out=ogb[:, 0:128],
                                            in_=pu[:, 256:384]))
                vop(lambda e: e.tensor_copy(out=tpt[:, 8:16],
                                            in_=pu[:, 384:392]), inc=True)
                ev_tg = top2_gate(None)
                b.wait_evt("tensor", ev_tg)
                transpose_batch8([ogb[:, 0:128]],
                                 ttf[:, i * 128:(i + 1) * 128]
                                 .rearrange("p (k n) -> p k n", n=128))
            dbg_store("z1f", actf[:, 0:2048])
            dbg_store("ttf", ttf[:, :])

        if upto >= 3:
            reg_evac = {0: None, 1: None}
            for half in range(2):
                b.wait("gpsimd", "PE")
                ewf = b.dma("gpsimd",
                            bbuf[:, 0:16384].rearrange("p (k n) -> p k n", n=1024),
                            wfT[:, half * 1024:(half + 1) * 1024]
                            .rearrange("(k p) n -> p k n", p=128), "LG")
                b.wait_evt("tensor", ewf)
                for i in range(NT):
                    reg = (i % 2) * 1024
                    xs = xstg[:, (i % 2) * 2048 + half * 1024:
                              (i % 2) * 2048 + half * 1024 + 1024]
                    b.wait("sync", "VE")
                    b.wait("sync", "SE")
                    exl = b.dma("sync", xs,
                                x_in[i * 128:(i + 1) * 128,
                                     half * 1024:(half + 1) * 1024], "LD")
                    b.wait_evt("tensor", reg_evac[i % 2])
                    for nb in range(2):
                        po = reg + nb * 512
                        for K in range(KB):
                            mm(pp[:, po:po + 512],
                               actf[:, K * 2048 + i * 128:
                                    K * 2048 + i * 128 + 128],
                               bbuf[:, K * 1024 + nb * 512:
                                    K * 1024 + nb * 512 + 512],
                               start=(K == 0), stop=False)
                        mm(pp[:, po:po + 512], ttf[:, i * 128:(i + 1) * 128],
                           btb[:, half * 1024 + nb * 512:
                               half * 1024 + nb * 512 + 512],
                           start=False, stop=True, inc=True)
                    ev_h = b.evt("PE")
                    b.wait_evt("vector", exl)
                    b.wait_evt("vector", ev_h)
                    vop(lambda e, xs=xs, reg=reg:
                        e.scalar_tensor_tensor(
                            out=xs, in0=pp[:, reg:reg + 1024], scalar=0.5,
                            in1=xs, op0=ALU.mult, op1=ALU.add),
                        inc=True)
                    ev_x1 = b.evt("VE")
                    reg_evac[i % 2] = ev_x1
                    b.wait_evt("scalar", ev_x1)
                    sop(lambda e, xs=xs, half=half, i=i: e.activation(
                        out=f32p[:, S_T1:S_T1 + 1024], in_=xs, func=AF.Square,
                        accum_out=colap(stat, 96 + 16 * half + i)))
                    sop(lambda e, xs=xs, half=half, i=i: e.activation(
                        out=f32p[:, S_T1:S_T1 + 1024], in_=xs, func=AF.Copy,
                        accum_out=colap(stat, 64 + 16 * half + i)), inc=True)
                    b.wait_evt("sync", ev_x1)
                    b.dma("sync", x1s[i * 128:(i + 1) * 128,
                                      half * 1024:(half + 1) * 1024], xs, "LD")
                ev_pe_f1 = b.evt("PE")
                b.wait_evt("gpsimd", ev_pe_f1)
            ev_x1_stored = b.evt("LD")
            if dbg is not None and dbg[0] == "x1row":
                b.wait_evt("gpsimd", ev_x1_stored)
                b.dma("gpsimd", xstg[:, 0:2048], x1s[0:128, :], "LG")
                b.wait("gpsimd", "LG")
                dbg_store("x1row", xstg[:, 0:2048])

        if upto >= 4:
            b.wait("vector", "SE")
            ev_ln2 = ln_batch_math(64, 80, 96, 112, 128, 144, NT,
                                   extra_scale=(1.0 + EPS),
                                   extra_eps=EPS * (1.0 + EPS))
            dbg_store("ln2", stat[:, :])
            for i in range(NT):
                xs = xstg[:, (i % 2) * 2048:(i % 2) * 2048 + 2048]
                b.wait("gpsimd", "VE")
                b.wait_evt("gpsimd", ev_x1_stored)
                ex1 = b.dma("gpsimd", xs, x1s[i * 128:(i + 1) * 128, :], "LG")
                b.wait_evt("vector", ex1)
                vop(lambda e, xs=xs, i=i: e.tensor_scalar(
                    out=ztok[:, :], in0=xs, scalar1=colap(stat, 144 + i),
                    scalar2=colap(stat, 128 + i), op0=ALU.mult, op1=ALU.add),
                    inc=True)
                ev_xh = b.evt("VE")
                b.wait_evt("tensor", ev_xh)
                transpose_tile_set(ztok, i, 0, 2048)
            for k in range(KB):
                vop(lambda e, k=k: e.tensor_scalar(
                    out=actf[:, k * 2048:(k + 1) * 2048],
                    in0=actf[:, k * 2048:(k + 1) * 2048],
                    scalar1=colap(lncol, k), scalar2=colap(lncol, 16 + k),
                    op0=ALU.mult, op1=ALU.add), inc=(k == KB - 1))
            ev_xnf = b.evt("VE")
            dbg_store("xnf", actf[:, 0:2048])

        if upto >= 5:
            pp_bank_evac = [None] * 4
            wslc_reader = [None, None]

            def emit_proj(g):
                kvp = (g & 1) * 2048
                ev_ev = {}
                projs = [("zw", None), ("k", wkT), ("rz", wrT), ("v", wvT)]
                for pi, (pname, wsrc) in enumerate(projs):
                    slot = pi % 2
                    wap = wslc[:, slot * 2048:(slot + 1) * 2048]
                    wap3 = wap.rearrange("p (k n) -> p k n", n=128)
                    q = "sync" if wsrc is None else "gpsimd"
                    b.wait_evt(q, wslc_reader[slot])
                    if wsrc is None:
                        b.wait_evt("sync", ev_m_stored)
                        ew = b.dma("sync", wap3,
                                   mts[:, g * 128:(g + 1) * 128]
                                   .rearrange("(k p) n -> p k n", p=128), "LD")
                    else:
                        ew = b.dma("gpsimd", wap3,
                                   wsrc[:, g * 128:(g + 1) * 128]
                                   .rearrange("(k p) n -> p k n", p=128), "LG")
                    b.wait_evt("tensor", ew)
                    if g == 0 and pi == 0:
                        b.wait_evt("tensor", ev_xnf)
                    for bank in range(4):
                        b.wait_evt("tensor", pp_bank_evac[bank])
                        for K in range(KB):
                            mm(pp[:, bank * 512:(bank + 1) * 512],
                               wap[:, K * 128:(K + 1) * 128],
                               actf[:, K * 2048 + bank * 512:
                                    K * 2048 + bank * 512 + 512],
                               start=(K == 0), stop=(K == KB - 1),
                               inc=(K == KB - 1))
                        ev_mm = b.evt("PE")
                        dtile, doff = {"zw": (f32p, S_ZW),
                                       "k": (bbuf, B_K + kvp),
                                       "rz": (f32p, S_U),
                                       "v": (bbuf, B_V + kvp)}[pname]
                        dap = dtile[:, doff + bank * 512:
                                    doff + bank * 512 + 512]
                        if pname == "zw":
                            b.wait_evt("vector", ev_mm)
                            vop(lambda e, dap=dap, bank=bank, g=g:
                                e.tensor_scalar(
                                    out=dap,
                                    in0=pp[:, bank * 512:bank * 512 + 512],
                                    scalar1=colap(lncol, 32 + g),
                                    scalar2=None, op0=ALU.add), inc=True)
                            ev_ev[(pname, bank)] = b.evt("VE")
                        else:
                            b.wait_evt("scalar", ev_mm)
                            sop(lambda e, dap=dap, bank=bank: e.activation(
                                out=dap,
                                in_=pp[:, bank * 512:bank * 512 + 512],
                                func=AF.Copy), inc=True)
                            ev_ev[(pname, bank)] = b.evt("SE")
                        pp_bank_evac[bank] = ev_ev[(pname, bank)]
                    wslc_reader[slot] = b.evt("PE")
                return ev_ev

            ev_ev_next = emit_proj(0)
            for g in range(NG):
                ev_ev = ev_ev_next
                if "prep" not in p5:
                    continue
                zw = f32p[:, S_ZW:S_ZW + 2048]
                lam = f32p[:, S_LAM:S_LAM + 2048]
                uu = f32p[:, S_U:S_U + 2048]
                t1 = f32p[:, S_T1:S_T1 + 2048]
                t2 = f32p[:, S_T2:S_T2 + 2048]
                for bk in range(4):
                    b.wait_evt("scalar", ev_ev[("zw", bk)])
                sop(lambda e: e.activation(out=t1, in_=zw, func=AF.Abs))
                sop(lambda e: e.activation(out=t1, in_=t1, func=AF.Exp,
                                           scale=-1.0))
                sop(lambda e: e.activation(out=t1, in_=t1, func=AF.Ln,
                                           bias=1.0), inc=True)
                ev_ln = b.evt("SE")
                b.wait_evt("vector", ev_ln)
                vop(lambda e: e.tensor_scalar(out=zw, in0=zw, scalar1=0.0,
                                              scalar2=-1.0, op0=ALU.max,
                                              op1=ALU.mult))
                vop(lambda e: e.tensor_tensor(out=zw, in0=zw, in1=t1,
                                              op=ALU.subtract))
                vop(lambda e: e.tensor_tensor_scan(
                    out=lam, data0=zw, data1=zw, initial=0.0, op0=ALU.add,
                    op1=ALU.bypass))
                vop(lambda e: e.memset(ends[:, 0:1], 0.0))
                vop(lambda e: e.tensor_copy(
                    out=ends[:, 1:33],
                    in_=bass.AP(tensor=lam.tensor, offset=lam.offset + 63,
                                ap=[lam.ap[0], [64, 32]])))
                vop(lambda e: e.tensor_tensor(out=mcb[:, :], in0=ends[:, 0:32],
                                              in1=ends[:, 1:33], op=ALU.add))
                vop(lambda e: e.tensor_scalar(out=mcb[:, :], in0=mcb[:, :],
                                              scalar1=0.5, scalar2=None,
                                              op0=ALU.mult))
                vop(lambda e: e.tensor_tensor(out=eksb[:, :],
                                              in0=ends[:, 1:33],
                                              in1=mcb[:, :], op=ALU.subtract))
                vop(lambda e: e.tensor_tensor(out=edec[:, :],
                                              in0=ends[:, 1:33],
                                              in1=ends[:, 0:32],
                                              op=ALU.subtract), inc=True)
                ev_bnd = b.evt("VE")
                b.wait_evt("scalar", ev_bnd)
                sop(lambda e: e.activation(out=eksb[:, :], in_=eksb[:, :],
                                           func=AF.Exp))
                sop(lambda e: e.activation(out=edec[:, :], in_=edec[:, :],
                                           func=AF.Exp), inc=True)
                ev_small = b.evt("SE")
                for bk in range(4):
                    b.wait_evt("scalar", ev_ev[("rz", bk)])
                sop(lambda e: e.activation(out=t2, in_=uu, func=AF.Abs))
                sop(lambda e: e.activation(out=t2, in_=t2, func=AF.Exp,
                                           scale=-1.0))
                sop(lambda e: e.activation(out=t2, in_=t2, func=AF.Ln,
                                           bias=1.0), inc=True)
                ev_ru = b.evt("SE")
                b.wait_evt("vector", ev_ru)
                vop(lambda e: e.tensor_scalar(out=uu, in0=uu, scalar1=0.0,
                                              scalar2=None, op0=ALU.min))
                vop(lambda e: e.tensor_tensor(out=uu, in0=uu, in1=t2,
                                              op=ALU.subtract))
                vop(lambda e: e.tensor_tensor(out=t1, in0=lam,
                                              in1=bcast_ap(mcb, 32, 64),
                                              op=ALU.subtract))
                vop(lambda e: e.tensor_scalar(out=t2, in0=t1, scalar1=-CLAMP,
                                              scalar2=-1.0, op0=ALU.max,
                                              op1=ALU.mult), inc=True)
                ev_ak = b.evt("VE")
                b.wait_evt("scalar", ev_ak)
                sop(lambda e: e.activation(out=bbuf[:, B_EK:B_EK + 2048],
                                           in_=t2, func=AF.Exp), inc=True)
                ev_ek = b.evt("SE")
                vop(lambda e: e.tensor_scalar(out=t1, in0=t1, scalar1=CLAMP,
                                              scalar2=None, op0=ALU.min))
                vop(lambda e: e.tensor_tensor(out=t1, in0=t1, in1=uu,
                                              op=ALU.add), inc=True)
                ev_ar = b.evt("VE")
                b.wait_evt("scalar", ev_ar)
                sop(lambda e: e.activation(out=bbuf[:, B_RA:B_RA + 2048],
                                           in_=t1, func=AF.Exp), inc=True)
                ev_ra = b.evt("SE")
                vop(lambda e: e.tensor_tensor(out=t2, in0=lam,
                                              in1=bcast_ap(ends, 32, 64),
                                              op=ALU.subtract))
                vop(lambda e: e.tensor_tensor(out=t2, in0=t2, in1=uu,
                                              op=ALU.add), inc=True)
                ev_ai = b.evt("VE")
                b.wait_evt("scalar", ev_ai)
                sop(lambda e: e.activation(out=bbuf[:, B_RI:B_RI + 2048],
                                           in_=t2, func=AF.Exp), inc=True)
                ev_ri = b.evt("SE")
                if "tp" not in p5:
                    continue
                kvp = (g & 1) * 2048
                if g + 1 < NG:
                    ev_ev_next = emit_proj(g + 1)
                for bk in range(4):
                    b.wait_evt("tensor", ev_ev[("v", bk)])
                for j0 in (0, 8):
                    srcs = [bbuf[:, B_V + kvp + j * 128:
                                 B_V + kvp + (j + 1) * 128]
                            for j in range(j0, j0 + 8)]
                    dst = bbuf[:, B_VT + j0 * 128:B_VT + j0 * 128 + 1024]
                    transpose_batch8(
                        srcs, dst.rearrange("p (k n) -> p k n", n=128))
                ev_vt_done = b.evt("VE")
                for bk in range(4):
                    b.wait_evt("vector", ev_ev[("k", bk)])
                b.wait_evt("vector", ev_ek)
                b.wait_evt("vector", ev_small)
                b.wait("vector", "PE")
                vop(lambda e, kvp=kvp: e.tensor_tensor(
                    out=bbuf[:, B_V + kvp:B_V + kvp + 2048],
                    in0=bbuf[:, B_K + kvp:B_K + kvp + 2048],
                    in1=bbuf[:, B_EK:B_EK + 2048], op=ALU.mult))
                vop(lambda e, kvp=kvp: e.tensor_copy(
                    out=bass.AP(tensor=bbuf[:, :].tensor,
                                offset=bbuf[:, :].offset + B_KAP,
                                ap=[[22528, 64], [128, 32], [1, 64]]),
                    in_=bass.AP(tensor=bbuf[:, :].tensor,
                                offset=bbuf[:, :].offset + B_V + kvp,
                                ap=[[22528, 64], [64, 32], [1, 64]])))
                vop(lambda e, kvp=kvp: e.tensor_copy(
                    out=bass.AP(tensor=bbuf[:, :].tensor,
                                offset=bbuf[:, :].offset + 64 * 22528
                                + B_KAP + 64,
                                ap=[[22528, 64], [128, 32], [1, 64]]),
                    in_=bass.AP(tensor=bbuf[:, :].tensor,
                                offset=bbuf[:, :].offset + 64 * 22528
                                + B_V + kvp,
                                ap=[[22528, 64], [64, 32], [1, 64]])))
                vop(lambda e, kvp=kvp: e.tensor_tensor(
                    out=bbuf[:, B_K + kvp:B_K + kvp + 2048],
                    in0=bbuf[:, B_V + kvp:B_V + kvp + 2048],
                    in1=bcast_ap(eksb, 32, 64), op=ALU.mult), inc=True)
                ev_ks = b.evt("VE")
                b.wait_evt("tensor", ev_ks)
                for j0 in (0, 8):
                    srcs = [bbuf[:, B_K + kvp + j * 128:
                                 B_K + kvp + (j + 1) * 128]
                            for j in range(j0, j0 + 8)]
                    dst = bbuf[:, B_KT + j0 * 128:B_KT + j0 * 128 + 1024]
                    transpose_batch8(
                        srcs, dst.rearrange("p (k n) -> p k n", n=128))
                ev_tp_done = b.evt("VE")

                b.wait("vector", "PE")
                vop(lambda e: e.memset(s0b[:, :], 0.0))
                if g == 0:
                    vop(lambda e: e.memset(pa[:, :], 0.0))
                    vop(lambda e: e.memset(pu[0:64, 64:128], 0.0))
                    vop(lambda e: e.memset(pu[64:128, 0:64], 0.0))
                    vop(lambda e: e.memset(
                        bass.AP(tensor=bbuf[:, :].tensor,
                                offset=bbuf[:, :].offset + B_KAP + 64,
                                ap=[[22528, 64], [128, 32], [1, 64]]), 0.0))
                    vop(lambda e: e.memset(
                        bass.AP(tensor=bbuf[:, :].tensor,
                                offset=bbuf[:, :].offset + 64 * 22528 + B_KAP,
                                ap=[[22528, 64], [128, 32], [1, 64]]), 0.0))
                vop(lambda e: e.engine_nop() if False else e.memset(
                    s1b[:, 64:128], 0.0), inc=True)
                ev_s0 = b.evt("VE")
                b.wait_evt("tensor", ev_tp_done)
                b.wait_evt("tensor", ev_s0)
                b.wait_evt("tensor", ev_ra)
                b.wait_evt("tensor", ev_ri)

                if "chunks" not in p5:
                    continue
                ev_stt_prev = None
                ev_yev_prev = None
                ev_as_prev = None
                ev_ogb_store = None
                for bat in range(8):
                    c0 = bat * 4
                    b.wait_evt("tensor", ev_as_prev)
                    for c in range(c0, c0 + 4):
                        par = (c & 1) * 64
                        for hh in (0, 1):
                            mm(pa[par:par + 64,
                                  ((c & 3) * 2 + hh) * 64:
                                  ((c & 3) * 2 + hh) * 64 + 64],
                               bbuf[:, B_KAP + c * 128 + hh * 64:
                                    B_KAP + c * 128 + hh * 64 + 64],
                               bbuf[:, B_RA + c * 64:B_RA + c * 64 + 64],
                               start=True, stop=True,
                               inc=(c == c0 + 3 and hh == 1),
                               tp=(0, par))
                    ev_ap = b.evt("PE")
                    b.wait_evt("vector", ev_ap)
                    vop(lambda e: e.tensor_tensor(
                        out=asb[:, :], in0=pa[:, :], in1=amask[:, :],
                        op=ALU.mult), inc=True)
                    ev_as = b.evt("VE")
                    ev_as_prev = ev_as
                    b.wait_evt("tensor", ev_as)
                    if "cy" not in p5:
                        continue
                    if c0 % 8 == 0 and ev_yev_prev is not None:
                        b.wait_evt("tensor", ev_yev_prev)
                    for c in range(c0, c0 + 4):
                        par = (c & 1) * 64
                        pair = c >> 1
                        scur = s0b if c % 2 == 0 else s1b
                        snxt = s1b if c % 2 == 0 else s0b
                        if ev_stt_prev is not None:
                            b.wait_evt("tensor", ev_stt_prev)
                        ypos = (c & 7) * 64
                        for hh, hb in ((0, 0), (1, 64)):
                            mm(py[hb:hb + 64, ypos:ypos + 64],
                               bbuf[:, B_VT + pair * 128 + hh * 64:
                                    B_VT + pair * 128 + hh * 64 + 64],
                               asb[:, ((c & 3) * 2 + hh) * 64:
                                   ((c & 3) * 2 + hh) * 64 + 64],
                               start=True, stop=False, tp=(0, hb))
                            mm(py[hb:hb + 64, ypos:ypos + 64],
                               scur[:, hh * 64:hh * 64 + 64],
                               bbuf[:, B_RI + c * 64:B_RI + c * 64 + 64],
                               start=False, stop=True,
                               inc=(hh == 1), tp=(0, hb))
                        ev_y = b.evt("PE")
                        for hh, hb in ((0, 0), (1, 64)):
                            mm(pu[hb:hb + 64, hb:hb + 64],
                               bbuf[par:par + 64,
                                    B_KT + pair * 128 + hh * 64:
                                    B_KT + pair * 128 + hh * 64 + 64],
                               bbuf[par:par + 64,
                                    B_VT + pair * 128 + hh * 64:
                                    B_VT + pair * 128 + hh * 64 + 64],
                               start=True, stop=True, inc=(hh == 1),
                               tp=(par, hb))
                        ev_upd = b.evt("PE")
                        b.wait_evt("vector", ev_upd)
                        vop(lambda e, sc=scur, sn=snxt, c=c:
                            e.scalar_tensor_tensor(
                                out=sn[:, :], in0=sc[:, :],
                                scalar=colap(edec, c), in1=pu[:, 0:128],
                                op0=ALU.mult, op1=ALU.add), inc=True)
                        ev_stt_prev = b.evt("VE")
                    if "cs" not in p5:
                        continue
                    if bat % 2 == 1:
                        b.wait_evt("vector", ev_y)
                        b.wait_evt("vector", ev_ogb_store)
                        vop(lambda e: e.tensor_copy(out=ogb[:, :],
                                                    in_=py[:, :]), inc=True)
                        ev_yev_prev = b.evt("VE")
                        b.wait_evt("sync", ev_yev_prev)
                        ev_ogb_store = b.dma(
                            "sync", outss[g * 128:(g + 1) * 128,
                                          (c0 - 4) * 64:(c0 - 4) * 64 + 512],
                            ogb[:, :], "LD")
            ev_scan_done = b.evt("LD")

        if upto >= 6:
            b.wait("vector", "PE")
            b.wait("sync", "PE")
            b.wait("sync", "VE")
            b.wait("sync", "SE")
            ev_sel_parts = []
            for k in range(KB):
                t0 = bbuf[:, B_K + (k % 2) * 2048: B_K + (k % 2) * 2048 + 1024]
                t1b = bbuf[:, B_RA + (k % 2) * 2048:
                           B_RA + (k % 2) * 2048 + 1024]
                b.wait("sync", "VE")
                b.wait_evt("sync", ev_scan_done)
                el0 = b.dma("sync", t0, outss[k * 128:(k + 1) * 128, 0:1024],
                            "LD")
                el1 = b.dma("sync", t1b,
                            outss[k * 128:(k + 1) * 128, 1024:2048], "LD")
                b.wait_evt("vector", el1)
                vop(lambda e, t0=t0: e.tensor_scalar(
                    out=t0, in0=t0, scalar1=mcols[:, 0:1], scalar2=None,
                    op0=ALU.mult))
                vop(lambda e, t0=t0, t1b=t1b, k=k: e.scalar_tensor_tensor(
                    out=actf[:, k * 2048:k * 2048 + 1024], in0=t1b,
                    scalar=mcols[:, 1:2], in1=t0, op0=ALU.mult, op1=ALU.add),
                    inc=True)
            ev_sel = b.evt("VE")
            dbg_store("omf", actf[:, 0:1024])
            reg_evac = {0: None, 1: None}
            ev_sq_prev = None
            ev_store_prev = {0: None, 1: None}
            for half in range(2):
                b.wait("gpsimd", "PE")
                b.wait_evt("gpsimd", ev_sel)
                ewo = b.dma("gpsimd",
                            bbuf[:, 0:16384].rearrange("p (k n) -> p k n", n=1024),
                            woT[:, half * 1024:(half + 1) * 1024]
                            .rearrange("(k p) n -> p k n", p=128), "LG")
                b.wait_evt("tensor", ewo)
                if half == 0:
                    b.wait_evt("tensor", ev_sel)
                for i in range(WT):
                    reg = (i % 2) * 1024
                    b.wait_evt("tensor", reg_evac[i % 2])
                    for nb in range(2):
                        po = reg + nb * 512
                        for K in range(KB):
                            mm(pp[:, po:po + 512],
                               actf[:, K * 2048 + i * 128:
                                    K * 2048 + i * 128 + 128],
                               bbuf[:, K * 1024 + nb * 512:
                                    K * 1024 + nb * 512 + 512],
                               start=(K == 0), stop=(K == KB - 1),
                               inc=(K == KB - 1))
                    ev_ymm = b.evt("PE")
                    xl = xstg[:, (i % 2) * 2048:(i % 2) * 2048 + 1024]
                    xh = xstg[:, (i % 2) * 2048 + 1024:(i % 2) * 2048 + 2048]
                    b.wait("sync", "VE")
                    b.wait("sync", "SE")
                    b.wait_evt("sync", ev_store_prev[i % 2])
                    exlo = b.dma("sync", xl,
                                 x1s[i * 128:(i + 1) * 128,
                                     half * 1024:(half + 1) * 1024], "LD")
                    exhi = b.dma("sync", xh,
                                 x1s[1024 + i * 128:1024 + (i + 1) * 128,
                                     half * 1024:(half + 1) * 1024], "LD")
                    b.wait_evt("vector", exhi)
                    b.wait_evt("vector", ev_ymm)
                    vop(lambda e, xl=xl: e.tensor_scalar(
                        out=xl, in0=xl, scalar1=mcols[:, 0:1], scalar2=None,
                        op0=ALU.mult))
                    vop(lambda e, xl=xl, xh=xh: e.scalar_tensor_tensor(
                        out=xl, in0=xh, scalar=mcols[:, 1:2], in1=xl,
                        op0=ALU.mult, op1=ALU.add))
                    vop(lambda e, xl=xl, reg=reg:
                        e.scalar_tensor_tensor(
                            out=xl, in0=pp[:, reg:reg + 1024], scalar=1.0,
                            in1=xl, op0=ALU.mult, op1=ALU.add), inc=True)
                    ev_x2 = b.evt("VE")
                    reg_evac[i % 2] = ev_x2
                    b.wait_evt("scalar", ev_x2)
                    sop(lambda e, xl=xl, half=half, i=i: e.activation(
                        out=f32p[:, S_T1:S_T1 + 1024], in_=xl, func=AF.Square,
                        accum_out=colap(stat, 32 + 16 * half + i)))
                    sop(lambda e, xl=xl, half=half, i=i: e.activation(
                        out=f32p[:, S_T1:S_T1 + 1024], in_=xl, func=AF.Copy,
                        accum_out=colap(stat, 16 * half + i)), inc=True)
                    b.wait_evt("sync", ev_x2)
                    ev_store_prev[i % 2] = b.dma(
                        "sync", x1s[i * 128:(i + 1) * 128,
                                    half * 1024:(half + 1) * 1024], xl, "LD")
                ev_pe_wo = b.evt("PE")
                b.wait_evt("gpsimd", ev_pe_wo)
            ev_x2_stored = b.evt("LD")
            b.wait("vector", "SE")
            ev_ln3 = ln_batch_math(0, 16, 32, 48, 64, 80, WT)
            for i in range(WT):
                xs = xstg[:, (i % 2) * 2048:(i % 2) * 2048 + 2048]
                b.wait("gpsimd", "VE")
                b.wait_evt("gpsimd", ev_x2_stored)
                ex2 = b.dma("gpsimd", xs, x1s[i * 128:(i + 1) * 128, :], "LG")
                b.wait_evt("vector", ex2)
                vop(lambda e, xs=xs, i=i: e.tensor_scalar(
                    out=ztok[:, :], in0=xs, scalar1=colap(stat, 80 + i),
                    scalar2=colap(stat, 64 + i), op0=ALU.mult, op1=ALU.add),
                    inc=True)
                ev_z3 = b.evt("VE")
                b.wait_evt("tensor", ev_z3)
                transpose_tile_set(ztok, i, 1024, 2048)
            ev_z3f = b.evt("VE")
            dbg_store("z3f", actf[:, 1024:2048])

        if upto >= 7:
            for i in range(WT):
                b.wait("tensor", "VE")
                for K in range(KB):
                    mm(pu[:, 256:392],
                       actf[:, K * 2048 + 1024 + i * 128:
                            K * 2048 + 1024 + i * 128 + 128],
                       atgb[:, K * 136:(K + 1) * 136],
                       start=(K == 0), stop=(K == KB - 1), inc=(K == KB - 1))
                ev_tl = b.evt("PE")
                b.wait_evt("vector", ev_tl)
                vop(lambda e: e.tensor_copy(out=ogb[:, 0:128],
                                            in_=pu[:, 256:384]))
                vop(lambda e: e.tensor_copy(out=tpt[:, 8:16],
                                            in_=pu[:, 384:392]), inc=True)
                ev_tg = top2_gate(None)
                b.wait_evt("tensor", ev_tg)
                transpose_batch8([ogb[:, 0:128]],
                                 ttf[:, i * 128:(i + 1) * 128]
                                 .rearrange("p (k n) -> p k n", n=128))
            reg_evac = {0: None, 1: None}
            for half in range(2):
                b.wait("gpsimd", "PE")
                ewf = b.dma("gpsimd",
                            bbuf[:, 0:16384].rearrange("p (k n) -> p k n", n=1024),
                            wfT[:, half * 1024:(half + 1) * 1024]
                            .rearrange("(k p) n -> p k n", p=128), "LG")
                b.wait_evt("tensor", ewf)
                for i in range(WT):
                    reg = (i % 2) * 1024
                    xs = xstg[:, (i % 2) * 2048 + half * 1024:
                              (i % 2) * 2048 + half * 1024 + 1024]
                    b.wait("sync", "VE")
                    exl = b.dma("sync", xs,
                                x1s[i * 128:(i + 1) * 128,
                                    half * 1024:(half + 1) * 1024], "LD")
                    b.wait_evt("tensor", reg_evac[i % 2])
                    for nb in range(2):
                        po = reg + nb * 512
                        for K in range(KB):
                            mm(pp[:, po:po + 512],
                               actf[:, K * 2048 + 1024 + i * 128:
                                    K * 2048 + 1024 + i * 128 + 128],
                               bbuf[:, K * 1024 + nb * 512:
                                    K * 1024 + nb * 512 + 512],
                               start=(K == 0), stop=False)
                        mm(pp[:, po:po + 512], ttf[:, i * 128:(i + 1) * 128],
                           btb[:, half * 1024 + nb * 512:
                               half * 1024 + nb * 512 + 512],
                           start=False, stop=True, inc=True)
                    ev_h = b.evt("PE")
                    b.wait_evt("vector", exl)
                    b.wait_evt("vector", ev_h)
                    vop(lambda e, xs=xs, reg=reg: e.scalar_tensor_tensor(
                        out=xs, in0=pp[:, reg:reg + 1024], scalar=0.5, in1=xs,
                        op0=ALU.mult, op1=ALU.add), inc=True)
                    ev_x3 = b.evt("VE")
                    reg_evac[i % 2] = ev_x3
                    b.wait_evt("sync", ev_x3)
                    b.dma("sync", out_t[i * 128:(i + 1) * 128,
                                        half * 1024:(half + 1) * 1024], xs,
                          "LD")
            b.wait("sync", "LD")
            b.wait("gpsimd", "LG")

        with nc.Block() as blk:
            for eng_name in KBd.ENGINES:
                opslist = b.ops[eng_name]
                def mk(opslist):
                    def run(e):
                        for f in opslist:
                            f(e)
                    return run
                getattr(blk, eng_name)(mk(opslist))

    return nc


def _layout_inputs(inputs, core):
    bidx = core // 2
    half = core % 2
    cat = np.ascontiguousarray
    A_w = inputs["L0_ffn_A_w"]
    gate = inputs["L0_ffn_gate_w"]
    ATg = np.concatenate([A_w.transpose(2, 0, 1).reshape(C, NE * NR),
                          np.asarray(gate).T], axis=1)
    amask = np.zeros((128, 512), np.float32)
    tri = np.triu(np.ones((64, 64), np.float32))
    for cl in range(4):
        par = (cl & 1) * 64
        for hh in range(2):
            blkc = (cl * 2 + hh) * 64
            amask[par:par + 64, blkc:blkc + 64] = tri
    colsify = lambda v: cat(np.asarray(v).reshape(16, 128).T.astype(np.float32))
    m0 = np.full((128, 1), 1.0 if half == 0 else 0.0, np.float32)
    m1 = np.full((128, 1), 0.0 if half == 0 else 1.0, np.float32)
    f32c = lambda a: cat(np.asarray(a).astype(np.float32, copy=False))
    return {
        "x": f32c(inputs["L0_attn_x"][bidx]),
        "WkT": f32c(np.asarray(inputs["L0_attn_Wk"]).T),
        "WvT": f32c(np.asarray(inputs["L0_attn_Wv"]).T),
        "WrT": f32c(np.asarray(inputs["L0_attn_Wr"]).T),
        "WoT": f32c(np.asarray(inputs["L0_attn_Wo"]).T),
        "WfT": f32c(np.asarray(inputs["L0_ffn_weight"]).T),
        "WwT": f32c(np.asarray(inputs["L0_attn_Ww"]).T),
        "Wx": f32c(inputs["L0_attn_Wx"]),
        "ATg": f32c(ATg),
        "BT": f32c(np.asarray(inputs["L0_ffn_B_w"]).transpose(0, 2, 1)
                   .reshape(NE * NR, C)),
        "lns": colsify(inputs["L0_attn_ln_scale"]),
        "lnb": colsify(inputs["L0_attn_ln_bias"]),
        "bw": colsify(inputs["L0_attn_bw"]),
        "m0": m0, "m1": m1,
        "eye": np.eye(128, dtype=np.float32),
        "amask": amask,
    }


def kernel(**inputs):
    nc = build_program()
    in_maps = [_layout_inputs(inputs, c) for c in range(8)]
    res = run_bass_kernel_spmd(nc, in_maps, core_ids=list(range(8)))
    out = np.empty((4, T, C), np.float32)
    for c in range(8):
        bidx, half = c // 2, c % 2
        out[bidx, half * WTOK:(half + 1) * WTOK, :] = res.results[c]["out"]
    return out


# version 05
# speedup vs baseline: 1.1294x; 1.0435x over previous
import numpy as np
from contextlib import ExitStack

import concourse.bass as bass
import concourse.mybir as mybir
from concourse.bass_utils import run_bass_kernel_spmd

F32 = mybir.dt.float32
BF16 = mybir.dt.bfloat16
AF = mybir.ActivationFunctionType
ALU = mybir.AluOpType
AXL = mybir.AxisListType

T = 2048
C = 2048
NT = 16
KB = 16
NG = 16
L = 64
NCH = 32
WTOK = 1024
WT = 8
EPS = 1e-5
CLAMP = 43.0
NE, NR = 8, 16
ALPHA = 16.0

S_ZW, S_LAM, S_U, S_T1, S_T2 = 0, 2048, 4096, 6144, 8192
B_K, B_K2, B_EK, B_RA, B_RI, B_V, B_V2, B_VT, B_KT, B_KAP = (
    0, 2048, 4096, 6144, 8192, 10240, 12288, 14336, 16384, 18432)


class KBd:

    ENGINES = ("sync", "gpsimd", "tensor", "vector", "scalar")

    def __init__(self, sems):
        self.sems = sems
        self.cnt = {k: 0 for k in sems}
        self.ops = {e: [] for e in self.ENGINES}

    def emit(self, eng, fn):
        self.ops[eng].append(fn)

    def wait(self, eng, ch, val=None):
        v = self.cnt[ch] if val is None else val
        if v <= 0:
            return
        h = self.sems[ch]
        self.emit(eng, lambda e: e.wait_ge(h, v))

    def evt(self, ch):
        return (ch, self.cnt[ch])

    def wait_evt(self, eng, evt):
        if evt is None:
            return
        ch, v = evt
        if ch in ("LD", "LG"):
            v = self.cnt[ch]
        if v <= 0:
            return
        h = self.sems[ch]
        self.emit(eng, lambda e: e.wait_ge(h, v))

    def dma(self, qeng, out, in_, ch, **kw):
        h = self.sems[ch]
        self.cnt[ch] += 16
        self.emit(qeng, lambda e: e.dma_start(out=out, in_=in_, **kw)
                  .then_inc(h, 16))
        return (ch, self.cnt[ch])


def build_program(dbg=None, upto=99,
                  p5=("proj", "prep", "tp", "chunks", "cy", "cu", "cs")):
    nc = bass.Bass("TRN2", target_bir_lowering=False,
                   detect_race_conditions=False)

    dt_in = lambda n, s: nc.dram_tensor(n, list(s), F32, kind="ExternalInput").ap()
    x_in = dt_in("x", (T, C))
    wkT = dt_in("WkT", (C, C))
    wvT = dt_in("WvT", (C, C))
    wrT = dt_in("WrT", (C, C))
    woT = dt_in("WoT", (C, C))
    wfT = dt_in("WfT", (C, C))
    wwT = dt_in("WwT", (C, C))
    wx = dt_in("Wx", (C, C))
    atg = dt_in("ATg", (C, NE * NR + NE))
    btv = dt_in("BT", (NE * NR, C))
    lns_i = dt_in("lns", (128, 16))
    lnb_i = dt_in("lnb", (128, 16))
    bw_i = dt_in("bw", (128, 16))
    m0v = dt_in("m0", (128, 1))
    m1v = dt_in("m1", (128, 1))
    eyev = dt_in("eye", (128, 128))
    amaskv = dt_in("amask", (128, 512))

    out_t = nc.dram_tensor("out", [WTOK, C], F32, kind="ExternalOutput").ap()
    dbg_t = None
    if dbg is not None:
        dbg_t = nc.dram_tensor("dbg", list(dbg[1]), F32, kind="ExternalOutput").ap()

    x1s = nc.dram_tensor("x1s", [T, C], F32, kind="Internal").ap()
    mts = nc.dram_tensor("mts", [C, C], BF16, kind="Internal").ap()
    outss = nc.dram_tensor("outss", [C, T], BF16, kind="Internal").ap()

    with ExitStack() as st:
        sb = lambda n, s, d: st.enter_context(nc.sbuf_tensor(n, list(s), d))
        psm = lambda n, s, d: st.enter_context(nc.psum_tensor(n, list(s), d))
        actf = sb("actf", (128, 32768), BF16)
        f32p = sb("f32p", (128, 10240), F32)
        bbuf = sb("bbuf", (128, 22528), BF16)
        ttf = sb("ttf", (128, 2048), BF16)
        btb = sb("btb", (128, 2048), BF16)
        atgb = sb("atgb", (128, 2176), BF16)
        eyeb = sb("eyeb", (128, 128), BF16)
        amask = sb("amask_s", (128, 512), F32)
        ztok = sb("ztok", (128, 2048), BF16)
        wslc = sb("wslc", (128, 4096), BF16)
        asb = sb("asb", (128, 512), BF16)
        ogb = sb("ogb", (128, 512), BF16)
        lncol = sb("lncol", (128, 48), F32)
        mcols = sb("mcols", (128, 2), F32)
        stat = sb("stat", (128, 160), F32)
        ends = sb("ends", (128, 33), F32)
        mcb = sb("mcb", (128, 32), F32)
        eksb = sb("eksb", (128, 32), F32)
        edec = sb("edec", (128, 32), F32)
        s0b = sb("s0b", (128, 128), BF16)
        s1b = sb("s1b", (128, 128), BF16)
        tpt = sb("tpt", (128, 64), F32)
        mstg = sb("mstg", (128, 1024), BF16)
        xstg = sb("xstg", (128, 4096), F32)

        pp = psm("pp", (128, 2048), F32)
        pa = psm("pa", (128, 512), F32)
        py = psm("py", (128, 512), F32)
        ptb = psm("ptb", (128, 1024), BF16)
        pu = psm("pu", (128, 512), F32)

        sems = {}
        for name in ("LD", "LG", "PE", "VE", "SE"):
            sems[name] = st.enter_context(nc.semaphore(name))
        b = KBd(sems)
        S = sems

        def mm(outap, lhs, rhs, start, stop, inc=False, tp=None, sgc=False):
            h = S["PE"]
            if inc:
                b.cnt["PE"] += 1
            def f(e, outap=outap, lhs=lhs, rhs=rhs, start=start, stop=stop,
                  inc=inc, tp=tp, sgc=sgc):
                i = e.matmul(outap, lhs, rhs, start=start, stop=stop,
                             tile_position=tp, skip_group_check=sgc)
                if inc:
                    i.then_inc(h, 1)
            b.emit("tensor", f)

        def tpose(outap, inap, inc=False, tp=None):
            h = S["PE"]
            if inc:
                b.cnt["PE"] += 1
            def f(e, outap=outap, inap=inap, inc=inc, tp=tp):
                i = e.transpose(outap, inap, eyeb[:, :], tile_position=tp)
                if inc:
                    i.then_inc(h, 1)
            b.emit("tensor", f)

        def vop(fn, inc=True):
            h = S["VE"]
            prev = b.cnt["VE"]
            b.cnt["VE"] += 1
            def f(e, fn=fn, prev=prev):
                if prev > 0:
                    e.wait_ge(h, prev)
                fn(e).then_inc(h, 1)
            b.emit("vector", f)

        def sop(fn, inc=False):
            h = S["SE"]
            if inc:
                b.cnt["SE"] += 1
            def f(e, fn=fn, inc=inc):
                i = fn(e)
                if inc:
                    i.then_inc(h, 1)
            b.emit("scalar", f)

        def bcast_ap(t2d, ncols, rep, coloff=0):
            a = t2d[:, coloff:coloff + ncols]
            return bass.AP(tensor=a.tensor, offset=a.offset,
                           ap=[a.ap[0], [a.ap[1][0], ncols], [0, rep]])

        def colap(t2d, j):
            return t2d[:, j:j + 1]

        dbg_rows = [0]
        def dbg_store(name, sbap, nrows=128):
            if dbg is None or dbg[0] != name:
                return
            r = dbg_rows[0]
            b.wait("gpsimd", "VE")
            b.wait("gpsimd", "SE")
            b.wait("gpsimd", "PE")
            b.dma("gpsimd", dbg_t[r:r + nrows, :], sbap, "LG")
            b.wait("gpsimd", "LG")
            dbg_rows[0] += nrows

        def top2_gate(src_logits_ap):
            vop(lambda e: e.reduce_max(tpt[:, 0:1], tpt[:, 8:16], axis=AXL.X))
            vop(lambda e: e.tensor_scalar(
                out=tpt[:, 16:24], in0=tpt[:, 8:16], scalar1=tpt[:, 0:1],
                scalar2=None, op0=ALU.is_equal))
            vop(lambda e: e.scalar_tensor_tensor(
                out=tpt[:, 24:32], in0=tpt[:, 16:24], scalar=-1e30,
                in1=tpt[:, 8:16], op0=ALU.mult, op1=ALU.add))
            vop(lambda e: e.reduce_max(tpt[:, 1:2], tpt[:, 24:32], axis=AXL.X))
            vop(lambda e: e.tensor_scalar(
                out=tpt[:, 32:40], in0=tpt[:, 24:32], scalar1=tpt[:, 1:2],
                scalar2=None, op0=ALU.is_equal))
            vop(lambda e: e.tensor_tensor(
                out=tpt[:, 2:3], in0=tpt[:, 0:1], in1=tpt[:, 1:2],
                op=ALU.subtract), inc=True)
            ev_d = b.evt("VE")
            b.wait_evt("scalar", ev_d)
            sop(lambda e: e.activation(out=tpt[:, 3:4], in_=tpt[:, 2:3],
                                       func=AF.Exp, scale=-1.0))
            sop(lambda e: e.activation(out=tpt[:, 3:4], in_=tpt[:, 3:4],
                                       func=AF.Ln, bias=1.0))
            sop(lambda e: e.activation(out=tpt[:, 4:5], in_=tpt[:, 3:4],
                                       func=AF.Exp, scale=-1.0), inc=True)
            ev_w1 = b.evt("SE")
            b.wait_evt("vector", ev_w1)
            vop(lambda e: e.tensor_scalar(
                out=tpt[:, 5:6], in0=tpt[:, 4:5], scalar1=-1.0, scalar2=1.0,
                op0=ALU.mult, op1=ALU.add))
            vop(lambda e: e.tensor_scalar(
                out=tpt[:, 16:24], in0=tpt[:, 16:24], scalar1=tpt[:, 4:5],
                scalar2=None, op0=ALU.mult))
            vop(lambda e: e.tensor_scalar(
                out=tpt[:, 32:40], in0=tpt[:, 32:40], scalar1=tpt[:, 5:6],
                scalar2=None, op0=ALU.mult))
            vop(lambda e: e.tensor_tensor(
                out=tpt[:, 16:24], in0=tpt[:, 16:24], in1=tpt[:, 32:40],
                op=ALU.add))
            vop(lambda e: e.tensor_scalar(
                out=tpt[:, 16:24], in0=tpt[:, 16:24], scalar1=ALPHA,
                scalar2=None, op0=ALU.mult))
            for ex in range(NE):
                vop(lambda e, ex=ex: e.tensor_scalar(
                    out=ogb[:, ex * 16:(ex + 1) * 16],
                    in0=ogb[:, ex * 16:(ex + 1) * 16],
                    scalar1=colap(tpt, 16 + ex), scalar2=None, op0=ALU.mult),
                    inc=(ex == NE - 1))
            return b.evt("VE")

        def ln_batch_math(s1a, s1b_, s2a, s2b, mo, ro, n, extra_scale=None,
                          extra_eps=EPS):
            for i in range(n):
                if s1b_ is not None:
                    vop(lambda e, i=i: e.tensor_tensor(
                        out=colap(stat, s1a + i), in0=colap(stat, s1a + i),
                        in1=colap(stat, s1b_ + i), op=ALU.add))
                    vop(lambda e, i=i: e.tensor_tensor(
                        out=colap(stat, s2a + i), in0=colap(stat, s2a + i),
                        in1=colap(stat, s2b + i), op=ALU.add))
                vop(lambda e, i=i: e.tensor_scalar(
                    out=colap(stat, mo + i), in0=colap(stat, s1a + i),
                    scalar1=1.0 / C, scalar2=None, op0=ALU.mult))
                vop(lambda e, i=i: e.tensor_scalar(
                    out=colap(stat, ro + i), in0=colap(stat, s2a + i),
                    scalar1=1.0 / C, scalar2=None, op0=ALU.mult))
                vop(lambda e, i=i: e.tensor_tensor(
                    out=tpt[:, 0:1], in0=colap(stat, mo + i),
                    in1=colap(stat, mo + i), op=ALU.mult))
                vop(lambda e, i=i: e.tensor_tensor(
                    out=colap(stat, ro + i), in0=colap(stat, ro + i),
                    in1=tpt[:, 0:1], op=ALU.subtract),
                    inc=False)
                if extra_scale is None:
                    vop(lambda e, i=i: e.tensor_scalar(
                        out=colap(stat, ro + i), in0=colap(stat, ro + i),
                        scalar1=extra_eps, scalar2=None, op0=ALU.add),
                        inc=True)
                if extra_scale is not None:
                    vop(lambda e, i=i: e.tensor_scalar(
                        out=colap(stat, ro + i), in0=colap(stat, ro + i),
                        scalar1=extra_scale, scalar2=extra_eps, op0=ALU.mult,
                        op1=ALU.add), inc=True)
            ev_v = b.evt("VE")
            b.wait_evt("scalar", ev_v)
            for i in range(n):
                sop(lambda e, i=i: e.activation(
                    out=tpt[:, 1:2], in_=colap(stat, ro + i), func=AF.Ln))
                sop(lambda e, i=i: e.activation(
                    out=colap(stat, ro + i), in_=tpt[:, 1:2], func=AF.Exp,
                    scale=-0.5), inc=(i == n - 1))
            ev_s = b.evt("SE")
            b.wait_evt("vector", ev_s)
            for i in range(n):
                vop(lambda e, i=i: e.tensor_tensor(
                    out=colap(stat, mo + i), in0=colap(stat, mo + i),
                    in1=colap(stat, ro + i), op=ALU.mult))
                vop(lambda e, i=i: e.tensor_scalar(
                    out=colap(stat, mo + i), in0=colap(stat, mo + i),
                    scalar1=-1.0, scalar2=None, op0=ALU.mult),
                    inc=(i == n - 1))
            return b.evt("VE")

        ptb_evac = [None]

        def transpose_batch8(srcs, dst3ap):
            nsl = len(srcs)
            b.wait_evt("tensor", ptb_evac[0])
            for sl, src in enumerate(srcs):
                tpose(ptb[:, sl * 128:(sl + 1) * 128], src,
                      inc=(sl == nsl - 1))
            ev_tp = b.evt("PE")
            b.wait_evt("vector", ev_tp)
            vop(lambda e, dst3ap=dst3ap, nsl=nsl: e.tensor_copy(
                out=dst3ap, in_=ptb[:, 0:nsl * 128]
                .rearrange("p (k n) -> p k n", n=128)), inc=True)
            ptb_evac[0] = b.evt("VE")

        def transpose_tile_set(src_tok, i, base, width):
            for half8 in range(2):
                k0 = half8 * 8
                srcs = [src_tok[:, k * 128:(k + 1) * 128]
                        for k in range(k0, k0 + 8)]
                dst = bass.AP(
                    tensor=actf[:, :].tensor,
                    offset=actf[:, :].offset + (k0 * 2048 + base + i * 128),
                    ap=[actf[:, :].ap[0], [2048, 8], [1, 128]])
                transpose_batch8(srcs, dst)

        b.dma("gpsimd", eyeb[:, :], eyev, "LG")
        b.dma("sync", amask[:, :], amaskv, "LD")
        b.dma("sync", lncol[:, 0:16], lns_i, "LD")
        b.dma("sync", lncol[:, 16:32], lnb_i, "LD")
        b.dma("sync", lncol[:, 32:48], bw_i, "LD")
        b.dma("sync", mcols[:, 0:1], m0v, "LD")
        b.dma("sync", mcols[:, 1:2], m1v, "LD")
        b.dma("gpsimd", btb[:, :], btv, "LG")
        e_atg = b.dma("gpsimd", atgb[:, :].rearrange("p (k n) -> p k n", n=136),
                      atg.rearrange("(k p) n -> p k n", p=128), "LG")
        e_wx = b.dma("gpsimd", actf[:, :].rearrange("p (k n) -> p k n", n=2048),
                     wx.rearrange("(k p) n -> p k n", p=128), "LG")
        ev_consts_ld = b.evt("LD")

        ev_m_stored = None
        ev_pe_m = None
        if upto >= 1:
            bank_evac = {0: None, 1: None}
            for ebank in range(4):
                ewl = b.dma("gpsimd",
                            bbuf[:, 0:8192].rearrange("p (k n) -> p k n", n=512),
                            wwT[:, ebank * 512:(ebank + 1) * 512]
                            .rearrange("(k p) n -> p k n", p=128), "LG")
                b.wait_evt("tensor", ewl)
                if ebank == 0:
                    b.wait_evt("tensor", e_wx)
                for Ct in range(16):
                    pb = Ct % 2
                    b.wait_evt("tensor", bank_evac[pb])
                    for K in range(16):
                        mm(pp[:, pb * 512:(pb + 1) * 512],
                           actf[:, K * 2048 + Ct * 128:
                                K * 2048 + Ct * 128 + 128],
                           bbuf[:, K * 512:(K + 1) * 512],
                           start=(K == 0), stop=(K == 15), inc=(K == 15))
                    ev_pe = b.evt("PE")
                    stg = mstg[:, pb * 512:pb * 512 + 512]
                    b.wait_evt("scalar", ev_pe)
                    b.wait("scalar", "LD")
                    sop(lambda e, stg=stg, pb=pb: e.activation(
                        out=stg, in_=pp[:, pb * 512:pb * 512 + 512],
                        func=AF.Copy), inc=True)
                    ev_se = b.evt("SE")
                    bank_evac[pb] = ev_se
                    b.wait_evt("sync", ev_se)
                    b.dma("sync", mts[Ct * 128:(Ct + 1) * 128,
                                      ebank * 512:(ebank + 1) * 512], stg, "LD")
                ev_pe_m = b.evt("PE")
                b.wait_evt("gpsimd", ev_pe_m)
            ev_m_stored = b.evt("LD")

        if upto >= 2:
            for i in range(NT):
                xs = xstg[:, (i % 2) * 2048:(i % 2) * 2048 + 2048]
                b.wait("gpsimd", "VE")
                b.wait("gpsimd", "SE")
                exl = b.dma("gpsimd", xs, x_in[i * 128:(i + 1) * 128, :], "LG")
                b.wait_evt("vector", exl)
                vop(lambda e, xs=xs, i=i: e.reduce_sum(
                    colap(stat, i), xs, axis=AXL.X), inc=True)
                b.wait_evt("scalar", exl)
                sop(lambda e, xs=xs, i=i: e.activation(
                    out=f32p[:, S_T1:S_T1 + 2048], in_=xs, func=AF.Square,
                    accum_out=colap(stat, 16 + i)), inc=True)
                ev_sq = b.evt("SE")
                b.wait_evt("vector", ev_sq)
                vop(lambda e, i=i: e.tensor_scalar(
                    out=colap(stat, 32 + i), in0=colap(stat, i),
                    scalar1=1.0 / C, scalar2=None, op0=ALU.mult))
                vop(lambda e, i=i: e.tensor_scalar(
                    out=colap(stat, 48 + i), in0=colap(stat, 16 + i),
                    scalar1=1.0 / C, scalar2=None, op0=ALU.mult))
                vop(lambda e, i=i: e.tensor_tensor(
                    out=tpt[:, 0:1], in0=colap(stat, 32 + i),
                    in1=colap(stat, 32 + i), op=ALU.mult))
                vop(lambda e, i=i: e.tensor_tensor(
                    out=colap(stat, 48 + i), in0=colap(stat, 48 + i),
                    in1=tpt[:, 0:1], op=ALU.subtract))
                vop(lambda e, i=i: e.tensor_scalar(
                    out=colap(stat, 48 + i), in0=colap(stat, 48 + i),
                    scalar1=EPS, scalar2=None, op0=ALU.add), inc=True)
                ev_var = b.evt("VE")
                b.wait_evt("scalar", ev_var)
                sop(lambda e, i=i: e.activation(out=tpt[:, 1:2],
                                                in_=colap(stat, 48 + i),
                                                func=AF.Ln))
                sop(lambda e, i=i: e.activation(
                    out=colap(stat, 48 + i), in_=tpt[:, 1:2], func=AF.Exp,
                    scale=-0.5), inc=True)
                ev_rstd = b.evt("SE")
                b.wait_evt("vector", ev_rstd)
                vop(lambda e, i=i: e.tensor_tensor(
                    out=colap(stat, 32 + i), in0=colap(stat, 32 + i),
                    in1=colap(stat, 48 + i), op=ALU.mult))
                vop(lambda e, i=i: e.tensor_scalar(
                    out=colap(stat, 32 + i), in0=colap(stat, 32 + i),
                    scalar1=-1.0, scalar2=None, op0=ALU.mult))
                vop(lambda e, xs=xs, i=i: e.tensor_scalar(
                    out=ztok[:, :], in0=xs, scalar1=colap(stat, 48 + i),
                    scalar2=colap(stat, 32 + i), op0=ALU.mult, op1=ALU.add),
                    inc=True)
                ev_z1 = b.evt("VE")
                if i == 0 and ev_pe_m is not None:
                    b.wait_evt("tensor", ev_pe_m)
                b.wait_evt("tensor", ev_z1)
                transpose_tile_set(ztok, i, 0, 2048)
                b.wait("tensor", "VE")
                if i == 0:
                    b.wait_evt("tensor", e_atg)
                for K in range(KB):
                    mm(pu[:, 256:392],
                       actf[:, K * 2048 + i * 128: K * 2048 + i * 128 + 128],
                       atgb[:, K * 136:(K + 1) * 136],
                       start=(K == 0), stop=(K == KB - 1), inc=(K == KB - 1))
                ev_tl = b.evt("PE")
                b.wait_evt("vector", ev_tl)
                vop(lambda e: e.tensor_copy(out=ogb[:, 0:128],
                                            in_=pu[:, 256:384]))
                vop(lambda e: e.tensor_copy(out=tpt[:, 8:16],
                                            in_=pu[:, 384:392]), inc=True)
                ev_tg = top2_gate(None)
                b.wait_evt("tensor", ev_tg)
                transpose_batch8([ogb[:, 0:128]],
                                 ttf[:, i * 128:(i + 1) * 128]
                                 .rearrange("p (k n) -> p k n", n=128))
            dbg_store("z1f", actf[:, 0:2048])
            dbg_store("ttf", ttf[:, :])

        if upto >= 3:
            reg_evac = {0: None, 1: None}
            for half in range(2):
                b.wait("gpsimd", "PE")
                ewf = b.dma("gpsimd",
                            bbuf[:, 0:16384].rearrange("p (k n) -> p k n", n=1024),
                            wfT[:, half * 1024:(half + 1) * 1024]
                            .rearrange("(k p) n -> p k n", p=128), "LG")
                b.wait_evt("tensor", ewf)
                for i in range(NT):
                    reg = (i % 2) * 1024
                    xs = xstg[:, (i % 2) * 2048 + half * 1024:
                              (i % 2) * 2048 + half * 1024 + 1024]
                    b.wait("sync", "VE")
                    b.wait("sync", "SE")
                    exl = b.dma("sync", xs,
                                x_in[i * 128:(i + 1) * 128,
                                     half * 1024:(half + 1) * 1024], "LD")
                    b.wait_evt("tensor", reg_evac[i % 2])
                    for nb in range(2):
                        po = reg + nb * 512
                        for K in range(KB):
                            mm(pp[:, po:po + 512],
                               actf[:, K * 2048 + i * 128:
                                    K * 2048 + i * 128 + 128],
                               bbuf[:, K * 1024 + nb * 512:
                                    K * 1024 + nb * 512 + 512],
                               start=(K == 0), stop=False)
                        mm(pp[:, po:po + 512], ttf[:, i * 128:(i + 1) * 128],
                           btb[:, half * 1024 + nb * 512:
                               half * 1024 + nb * 512 + 512],
                           start=False, stop=True, inc=True)
                    ev_h = b.evt("PE")
                    b.wait_evt("vector", exl)
                    b.wait_evt("vector", ev_h)
                    vop(lambda e, xs=xs, reg=reg:
                        e.scalar_tensor_tensor(
                            out=xs, in0=pp[:, reg:reg + 1024], scalar=0.5,
                            in1=xs, op0=ALU.mult, op1=ALU.add),
                        inc=True)
                    ev_x1 = b.evt("VE")
                    reg_evac[i % 2] = ev_x1
                    b.wait_evt("scalar", ev_x1)
                    sop(lambda e, xs=xs, half=half, i=i: e.activation(
                        out=f32p[:, S_T1:S_T1 + 1024], in_=xs, func=AF.Square,
                        accum_out=colap(stat, 96 + 16 * half + i)))
                    sop(lambda e, xs=xs, half=half, i=i: e.activation(
                        out=f32p[:, S_T1:S_T1 + 1024], in_=xs, func=AF.Copy,
                        accum_out=colap(stat, 64 + 16 * half + i)), inc=True)
                    b.wait_evt("sync", ev_x1)
                    b.dma("sync", x1s[i * 128:(i + 1) * 128,
                                      half * 1024:(half + 1) * 1024], xs, "LD")
                ev_pe_f1 = b.evt("PE")
                b.wait_evt("gpsimd", ev_pe_f1)
            ev_x1_stored = b.evt("LD")
            if dbg is not None and dbg[0] == "x1row":
                b.wait_evt("gpsimd", ev_x1_stored)
                b.dma("gpsimd", xstg[:, 0:2048], x1s[0:128, :], "LG")
                b.wait("gpsimd", "LG")
                dbg_store("x1row", xstg[:, 0:2048])

        if upto >= 4:
            b.wait("vector", "SE")
            ev_ln2 = ln_batch_math(64, 80, 96, 112, 128, 144, NT,
                                   extra_scale=(1.0 + EPS),
                                   extra_eps=EPS * (1.0 + EPS))
            dbg_store("ln2", stat[:, :])
            for i in range(NT):
                xs = xstg[:, (i % 2) * 2048:(i % 2) * 2048 + 2048]
                b.wait("gpsimd", "VE")
                b.wait_evt("gpsimd", ev_x1_stored)
                ex1 = b.dma("gpsimd", xs, x1s[i * 128:(i + 1) * 128, :], "LG")
                b.wait_evt("vector", ex1)
                vop(lambda e, xs=xs, i=i: e.tensor_scalar(
                    out=ztok[:, :], in0=xs, scalar1=colap(stat, 144 + i),
                    scalar2=colap(stat, 128 + i), op0=ALU.mult, op1=ALU.add),
                    inc=True)
                ev_xh = b.evt("VE")
                b.wait_evt("tensor", ev_xh)
                transpose_tile_set(ztok, i, 0, 2048)
            for k in range(KB):
                vop(lambda e, k=k: e.tensor_scalar(
                    out=actf[:, k * 2048:(k + 1) * 2048],
                    in0=actf[:, k * 2048:(k + 1) * 2048],
                    scalar1=colap(lncol, k), scalar2=colap(lncol, 16 + k),
                    op0=ALU.mult, op1=ALU.add), inc=(k == KB - 1))
            ev_xnf = b.evt("VE")
            dbg_store("xnf", actf[:, 0:2048])

        if upto >= 5:
            pp_bank_evac = [None] * 4
            wslc_reader = [None, None]

            def emit_proj(g):
                kvp = (g & 1) * 2048
                ev_ev = {}
                projs = [("zw", None), ("k", wkT), ("rz", wrT), ("v", wvT)]
                for pi, (pname, wsrc) in enumerate(projs):
                    slot = pi % 2
                    wap = wslc[:, slot * 2048:(slot + 1) * 2048]
                    wap3 = wap.rearrange("p (k n) -> p k n", n=128)
                    q = "sync" if wsrc is None else "gpsimd"
                    b.wait_evt(q, wslc_reader[slot])
                    if wsrc is None:
                        b.wait_evt("sync", ev_m_stored)
                        ew = b.dma("sync", wap3,
                                   mts[:, g * 128:(g + 1) * 128]
                                   .rearrange("(k p) n -> p k n", p=128), "LD")
                    else:
                        ew = b.dma("gpsimd", wap3,
                                   wsrc[:, g * 128:(g + 1) * 128]
                                   .rearrange("(k p) n -> p k n", p=128), "LG")
                    b.wait_evt("tensor", ew)
                    if g == 0 and pi == 0:
                        b.wait_evt("tensor", ev_xnf)
                    for bank in range(4):
                        b.wait_evt("tensor", pp_bank_evac[bank])
                        for K in range(KB):
                            mm(pp[:, bank * 512:(bank + 1) * 512],
                               wap[:, K * 128:(K + 1) * 128],
                               actf[:, K * 2048 + bank * 512:
                                    K * 2048 + bank * 512 + 512],
                               start=(K == 0), stop=(K == KB - 1),
                               inc=(K == KB - 1))
                        ev_mm = b.evt("PE")
                        dtile, doff = {"zw": (f32p, S_ZW),
                                       "k": (bbuf, B_K + kvp),
                                       "rz": (f32p, S_U),
                                       "v": (bbuf, B_V + kvp)}[pname]
                        dap = dtile[:, doff + bank * 512:
                                    doff + bank * 512 + 512]
                        if pname == "zw":
                            b.wait_evt("vector", ev_mm)
                            vop(lambda e, dap=dap, bank=bank, g=g:
                                e.tensor_scalar(
                                    out=dap,
                                    in0=pp[:, bank * 512:bank * 512 + 512],
                                    scalar1=colap(lncol, 32 + g),
                                    scalar2=None, op0=ALU.add), inc=True)
                            ev_ev[(pname, bank)] = b.evt("VE")
                        else:
                            b.wait_evt("scalar", ev_mm)
                            sop(lambda e, dap=dap, bank=bank: e.activation(
                                out=dap,
                                in_=pp[:, bank * 512:bank * 512 + 512],
                                func=AF.Copy), inc=True)
                            ev_ev[(pname, bank)] = b.evt("SE")
                        pp_bank_evac[bank] = ev_ev[(pname, bank)]
                    wslc_reader[slot] = b.evt("PE")
                return ev_ev

            ev_ev_next = emit_proj(0)
            for g in range(NG):
                ev_ev = ev_ev_next
                if "prep" not in p5:
                    continue
                zw = f32p[:, S_ZW:S_ZW + 2048]
                lam = f32p[:, S_LAM:S_LAM + 2048]
                uu = f32p[:, S_U:S_U + 2048]
                t1 = f32p[:, S_T1:S_T1 + 2048]
                t2 = f32p[:, S_T2:S_T2 + 2048]
                for bk in range(4):
                    b.wait_evt("scalar", ev_ev[("zw", bk)])
                sop(lambda e: e.activation(out=t1, in_=zw, func=AF.Abs))
                sop(lambda e: e.activation(out=t1, in_=t1, func=AF.Exp,
                                           scale=-1.0))
                sop(lambda e: e.activation(out=t1, in_=t1, func=AF.Ln,
                                           bias=1.0), inc=True)
                ev_ln = b.evt("SE")
                b.wait_evt("vector", ev_ln)
                vop(lambda e: e.tensor_scalar(out=zw, in0=zw, scalar1=0.0,
                                              scalar2=-1.0, op0=ALU.max,
                                              op1=ALU.mult))
                vop(lambda e: e.tensor_tensor(out=zw, in0=zw, in1=t1,
                                              op=ALU.subtract))
                vop(lambda e: e.tensor_tensor_scan(
                    out=lam, data0=zw, data1=zw, initial=0.0, op0=ALU.add,
                    op1=ALU.bypass))
                vop(lambda e: e.memset(ends[:, 0:1], 0.0))
                vop(lambda e: e.tensor_copy(
                    out=ends[:, 1:33],
                    in_=bass.AP(tensor=lam.tensor, offset=lam.offset + 63,
                                ap=[lam.ap[0], [64, 32]])))
                vop(lambda e: e.tensor_tensor(out=mcb[:, :], in0=ends[:, 0:32],
                                              in1=ends[:, 1:33], op=ALU.add))
                vop(lambda e: e.tensor_scalar(out=mcb[:, :], in0=mcb[:, :],
                                              scalar1=0.5, scalar2=None,
                                              op0=ALU.mult))
                vop(lambda e: e.tensor_tensor(out=eksb[:, :],
                                              in0=ends[:, 1:33],
                                              in1=mcb[:, :], op=ALU.subtract))
                vop(lambda e: e.tensor_tensor(out=edec[:, :],
                                              in0=ends[:, 1:33],
                                              in1=ends[:, 0:32],
                                              op=ALU.subtract), inc=True)
                ev_bnd = b.evt("VE")
                b.wait_evt("scalar", ev_bnd)
                sop(lambda e: e.activation(out=eksb[:, :], in_=eksb[:, :],
                                           func=AF.Exp))
                sop(lambda e: e.activation(out=edec[:, :], in_=edec[:, :],
                                           func=AF.Exp), inc=True)
                ev_small = b.evt("SE")
                for bk in range(4):
                    b.wait_evt("scalar", ev_ev[("rz", bk)])
                sop(lambda e: e.activation(out=t2, in_=uu, func=AF.Abs))
                sop(lambda e: e.activation(out=t2, in_=t2, func=AF.Exp,
                                           scale=-1.0))
                sop(lambda e: e.activation(out=t2, in_=t2, func=AF.Ln,
                                           bias=1.0), inc=True)
                ev_ru = b.evt("SE")
                b.wait_evt("vector", ev_ru)
                vop(lambda e: e.tensor_scalar(out=uu, in0=uu, scalar1=0.0,
                                              scalar2=None, op0=ALU.min))
                vop(lambda e: e.tensor_tensor(out=uu, in0=uu, in1=t2,
                                              op=ALU.subtract))
                vop(lambda e: e.tensor_tensor(out=t1, in0=lam,
                                              in1=bcast_ap(mcb, 32, 64),
                                              op=ALU.subtract))
                vop(lambda e: e.tensor_scalar(out=t2, in0=t1, scalar1=-CLAMP,
                                              scalar2=-1.0, op0=ALU.max,
                                              op1=ALU.mult), inc=True)
                ev_ak = b.evt("VE")
                b.wait_evt("scalar", ev_ak)
                sop(lambda e: e.activation(out=bbuf[:, B_EK:B_EK + 2048],
                                           in_=t2, func=AF.Exp), inc=True)
                ev_ek = b.evt("SE")
                vop(lambda e: e.tensor_scalar(out=t1, in0=t1, scalar1=CLAMP,
                                              scalar2=None, op0=ALU.min))
                vop(lambda e: e.tensor_tensor(out=t1, in0=t1, in1=uu,
                                              op=ALU.add), inc=True)
                ev_ar = b.evt("VE")
                b.wait_evt("scalar", ev_ar)
                sop(lambda e: e.activation(out=bbuf[:, B_RA:B_RA + 2048],
                                           in_=t1, func=AF.Exp), inc=True)
                ev_ra = b.evt("SE")
                vop(lambda e: e.tensor_tensor(out=t2, in0=lam,
                                              in1=bcast_ap(ends, 32, 64),
                                              op=ALU.subtract))
                vop(lambda e: e.tensor_tensor(out=t2, in0=t2, in1=uu,
                                              op=ALU.add), inc=True)
                ev_ai = b.evt("VE")
                b.wait_evt("scalar", ev_ai)
                sop(lambda e: e.activation(out=bbuf[:, B_RI:B_RI + 2048],
                                           in_=t2, func=AF.Exp), inc=True)
                ev_ri = b.evt("SE")
                if "tp" not in p5:
                    continue
                kvp = (g & 1) * 2048
                if g + 1 < NG:
                    ev_ev_next = emit_proj(g + 1)
                if g + 1 == NG - 1:
                    pegate = b.evt("PE")
                    b.wait_evt("vector", pegate)
                    ev_sel_early = None
                    for k in range(NG - 2):
                        t0 = mstg[:, 0:1024]
                        t1b = ztok[:, 0:1024]
                        b.wait("sync", "VE")
                        b.wait_evt("sync", b.evt("LD"))
                        el0 = b.dma("sync", t0,
                                    outss[k * 128:(k + 1) * 128, 0:1024],
                                    "LD")
                        el1 = b.dma("sync", t1b,
                                    outss[k * 128:(k + 1) * 128, 1024:2048],
                                    "LD")
                        b.wait_evt("vector", el1)
                        vop(lambda e, t0=t0: e.tensor_scalar(
                            out=t0, in0=t0, scalar1=mcols[:, 0:1],
                            scalar2=None, op0=ALU.mult))
                        vop(lambda e, t0=t0, t1b=t1b, k=k:
                            e.scalar_tensor_tensor(
                                out=actf[:, k * 2048:k * 2048 + 1024],
                                in0=t1b, scalar=mcols[:, 1:2], in1=t0,
                                op0=ALU.mult, op1=ALU.add), inc=True)
                    b.wait_evt("gpsimd", pegate)
                    ev_wo0 = b.dma(
                        "gpsimd",
                        bass.AP(tensor=actf[:, :].tensor,
                                offset=actf[:, :].offset + 1024,
                                ap=[actf[:, :].ap[0], [2048, 16], [1, 1024]]),
                        woT[:, 0:1024].rearrange("(k p) n -> p k n", p=128),
                        "LG")
                for bk in range(4):
                    b.wait_evt("tensor", ev_ev[("v", bk)])
                for j0 in (0, 8):
                    srcs = [bbuf[:, B_V + kvp + j * 128:
                                 B_V + kvp + (j + 1) * 128]
                            for j in range(j0, j0 + 8)]
                    dst = bbuf[:, B_VT + j0 * 128:B_VT + j0 * 128 + 1024]
                    transpose_batch8(
                        srcs, dst.rearrange("p (k n) -> p k n", n=128))
                ev_vt_done = b.evt("VE")
                for bk in range(4):
                    b.wait_evt("vector", ev_ev[("k", bk)])
                b.wait_evt("vector", ev_ek)
                b.wait_evt("vector", ev_small)
                b.wait("vector", "PE")
                vop(lambda e, kvp=kvp: e.tensor_tensor(
                    out=bbuf[:, B_V + kvp:B_V + kvp + 2048],
                    in0=bbuf[:, B_K + kvp:B_K + kvp + 2048],
                    in1=bbuf[:, B_EK:B_EK + 2048], op=ALU.mult))
                vop(lambda e, kvp=kvp: e.tensor_copy(
                    out=bass.AP(tensor=bbuf[:, :].tensor,
                                offset=bbuf[:, :].offset + B_KAP,
                                ap=[[22528, 64], [128, 32], [1, 64]]),
                    in_=bass.AP(tensor=bbuf[:, :].tensor,
                                offset=bbuf[:, :].offset + B_V + kvp,
                                ap=[[22528, 64], [64, 32], [1, 64]])))
                vop(lambda e, kvp=kvp: e.tensor_copy(
                    out=bass.AP(tensor=bbuf[:, :].tensor,
                                offset=bbuf[:, :].offset + 64 * 22528
                                + B_KAP + 64,
                                ap=[[22528, 64], [128, 32], [1, 64]]),
                    in_=bass.AP(tensor=bbuf[:, :].tensor,
                                offset=bbuf[:, :].offset + 64 * 22528
                                + B_V + kvp,
                                ap=[[22528, 64], [64, 32], [1, 64]])))
                vop(lambda e, kvp=kvp: e.tensor_tensor(
                    out=bbuf[:, B_K + kvp:B_K + kvp + 2048],
                    in0=bbuf[:, B_V + kvp:B_V + kvp + 2048],
                    in1=bcast_ap(eksb, 32, 64), op=ALU.mult), inc=True)
                ev_ks = b.evt("VE")
                b.wait_evt("tensor", ev_ks)
                for j0 in (0, 8):
                    srcs = [bbuf[:, B_K + kvp + j * 128:
                                 B_K + kvp + (j + 1) * 128]
                            for j in range(j0, j0 + 8)]
                    dst = bbuf[:, B_KT + j0 * 128:B_KT + j0 * 128 + 1024]
                    transpose_batch8(
                        srcs, dst.rearrange("p (k n) -> p k n", n=128))
                ev_tp_done = b.evt("VE")

                b.wait("vector", "PE")
                vop(lambda e: e.memset(s0b[:, :], 0.0))
                vop(lambda e: e.memset(s1b[:, :], 0.0))
                if g == 0:
                    vop(lambda e: e.memset(pa[:, :], 0.0))
                    vop(lambda e: e.memset(
                        bass.AP(tensor=bbuf[:, :].tensor,
                                offset=bbuf[:, :].offset + B_KAP + 64,
                                ap=[[22528, 64], [128, 32], [1, 64]]), 0.0))
                    vop(lambda e: e.memset(
                        bass.AP(tensor=bbuf[:, :].tensor,
                                offset=bbuf[:, :].offset + 64 * 22528 + B_KAP,
                                ap=[[22528, 64], [128, 32], [1, 64]]), 0.0))
                vop(lambda e: e.memset(ogb[:, 0:1], 0.0), inc=True)
                ev_s0 = b.evt("VE")
                b.wait_evt("tensor", ev_tp_done)
                b.wait_evt("tensor", ev_s0)
                b.wait_evt("tensor", ev_ra)
                b.wait_evt("tensor", ev_ri)

                if "chunks" not in p5:
                    continue
                ev_stt_prev = None
                ev_yev_prev = None
                ev_as_prev = None
                ev_ogb_store = None
                for bat in range(8):
                    c0 = bat * 4
                    b.wait_evt("tensor", ev_as_prev)
                    b.wait_evt("tensor", ev_stt_prev)
                    for c in range(c0, c0 + 4):
                        par = (c & 1) * 64
                        for hh in (0, 1):
                            mm(pa[par:par + 64,
                                  ((c & 3) * 2 + hh) * 64:
                                  ((c & 3) * 2 + hh) * 64 + 64],
                               bbuf[:, B_KAP + c * 128 + hh * 64:
                                    B_KAP + c * 128 + hh * 64 + 64],
                               bbuf[:, B_RA + c * 64:B_RA + c * 64 + 64],
                               start=True, stop=True,
                               inc=(c == c0 + 3 and hh == 1),
                               tp=(0, par))
                    ev_ap = b.evt("PE")
                    b.wait_evt("vector", ev_ap)
                    vop(lambda e: e.tensor_tensor(
                        out=asb[:, :], in0=pa[:, :], in1=amask[:, :],
                        op=ALU.mult), inc=True)
                    ev_as = b.evt("VE")
                    ev_as_prev = ev_as
                    b.wait_evt("tensor", ev_as)
                    if "cy" not in p5:
                        continue
                    if c0 % 8 == 0 and ev_yev_prev is not None:
                        b.wait_evt("tensor", ev_yev_prev)
                    for c in range(c0, c0 + 4):
                        par = (c & 1) * 64
                        pair = c >> 1
                        scur = s0b if c % 2 == 0 else s1b
                        snxt = s1b if c % 2 == 0 else s0b
                        ypos = (c & 7) * 64
                        upsum = pu if c % 2 == 0 else pa
                        for hh, hb in ((0, 0), (1, 64)):
                            mm(upsum[hb:hb + 64, hb:hb + 64],
                               bbuf[par:par + 64,
                                    B_KT + pair * 128 + hh * 64:
                                    B_KT + pair * 128 + hh * 64 + 64],
                               bbuf[par:par + 64,
                                    B_VT + pair * 128 + hh * 64:
                                    B_VT + pair * 128 + hh * 64 + 64],
                               start=True, stop=True, inc=(hh == 1),
                               tp=(par, hb))
                        ev_upd = b.evt("PE")
                        if ev_stt_prev is not None:
                            b.wait_evt("tensor", ev_stt_prev)
                        for hh, hb in ((0, 0), (1, 64)):
                            mm(py[hb:hb + 64, ypos:ypos + 64],
                               bbuf[:, B_VT + pair * 128 + hh * 64:
                                    B_VT + pair * 128 + hh * 64 + 64],
                               asb[:, ((c & 3) * 2 + hh) * 64:
                                   ((c & 3) * 2 + hh) * 64 + 64],
                               start=True, stop=False, tp=(0, hb))
                            mm(py[hb:hb + 64, ypos:ypos + 64],
                               scur[:, hh * 64:hh * 64 + 64],
                               bbuf[:, B_RI + c * 64:B_RI + c * 64 + 64],
                               start=False, stop=True,
                               inc=(hh == 1), tp=(0, hb))
                        ev_y = b.evt("PE")
                        b.wait_evt("vector", ev_upd)
                        vop(lambda e, sc=scur, sn=snxt, c=c, up=upsum:
                            e.scalar_tensor_tensor(
                                out=sn[0:64, 0:64], in0=sc[0:64, 0:64],
                                scalar=edec[0:64, c:c + 1],
                                in1=up[0:64, 0:64],
                                op0=ALU.mult, op1=ALU.add))
                        vop(lambda e, sc=scur, sn=snxt, c=c, up=upsum:
                            e.scalar_tensor_tensor(
                                out=sn[64:128, 64:128], in0=sc[64:128, 64:128],
                                scalar=edec[64:128, c:c + 1],
                                in1=up[64:128, 64:128],
                                op0=ALU.mult, op1=ALU.add), inc=True)
                        ev_stt_prev = b.evt("VE")
                    if "cs" not in p5:
                        continue
                    if bat % 2 == 1:
                        b.wait_evt("vector", ev_y)
                        b.wait_evt("vector", ev_ogb_store)
                        vop(lambda e: e.tensor_copy(out=ogb[:, :],
                                                    in_=py[:, :]), inc=True)
                        ev_yev_prev = b.evt("VE")
                        b.wait_evt("sync", ev_yev_prev)
                        ev_ogb_store = b.dma(
                            "sync", outss[g * 128:(g + 1) * 128,
                                          (c0 - 4) * 64:(c0 - 4) * 64 + 512],
                            ogb[:, :], "LD")
            ev_scan_done = b.evt("LD")

        if upto >= 6:
            b.wait("vector", "PE")
            b.wait("sync", "PE")
            b.wait("sync", "VE")
            b.wait("sync", "SE")
            for k in range(NG - 2, KB):
                t0 = mstg[:, 0:1024]
                t1b = ztok[:, 0:1024]
                b.wait("sync", "VE")
                b.wait_evt("sync", ev_scan_done)
                el0 = b.dma("sync", t0, outss[k * 128:(k + 1) * 128, 0:1024],
                            "LD")
                el1 = b.dma("sync", t1b,
                            outss[k * 128:(k + 1) * 128, 1024:2048], "LD")
                b.wait_evt("vector", el1)
                vop(lambda e, t0=t0: e.tensor_scalar(
                    out=t0, in0=t0, scalar1=mcols[:, 0:1], scalar2=None,
                    op0=ALU.mult))
                vop(lambda e, t0=t0, t1b=t1b, k=k: e.scalar_tensor_tensor(
                    out=actf[:, k * 2048:k * 2048 + 1024], in0=t1b,
                    scalar=mcols[:, 1:2], in1=t0, op0=ALU.mult, op1=ALU.add),
                    inc=True)
            ev_sel = b.evt("VE")
            dbg_store("omf", actf[:, 0:1024])
            reg_evac = {0: None, 1: None}
            ev_sq_prev = None
            ev_store_prev = {0: None, 1: None}
            for half in range(2):
                if half == 1:
                    b.wait("gpsimd", "PE")
                    b.wait_evt("gpsimd", ev_sel)
                    ewo = b.dma("gpsimd",
                                bbuf[:, 0:16384]
                                .rearrange("p (k n) -> p k n", n=1024),
                                woT[:, 1024:2048]
                                .rearrange("(k p) n -> p k n", p=128), "LG")
                    b.wait_evt("tensor", ewo)
                else:
                    b.wait_evt("tensor", ev_wo0)
                    b.wait_evt("tensor", ev_sel)
                for i in range(WT):
                    reg = (i % 2) * 1024
                    b.wait_evt("tensor", reg_evac[i % 2])
                    for nb in range(2):
                        po = reg + nb * 512
                        for K in range(KB):
                            wsrc_ap = (actf[:, K * 2048 + 1024 + nb * 512:
                                            K * 2048 + 1024 + nb * 512 + 512]
                                       if half == 0 else
                                       bbuf[:, K * 1024 + nb * 512:
                                            K * 1024 + nb * 512 + 512])
                            mm(pp[:, po:po + 512],
                               actf[:, K * 2048 + i * 128:
                                    K * 2048 + i * 128 + 128],
                               wsrc_ap,
                               start=(K == 0), stop=(K == KB - 1),
                               inc=(K == KB - 1))
                    ev_ymm = b.evt("PE")
                    xl = xstg[:, (i % 2) * 2048:(i % 2) * 2048 + 1024]
                    xh = xstg[:, (i % 2) * 2048 + 1024:(i % 2) * 2048 + 2048]
                    b.wait("sync", "VE")
                    b.wait("sync", "SE")
                    b.wait_evt("sync", ev_store_prev[i % 2])
                    exlo = b.dma("sync", xl,
                                 x1s[i * 128:(i + 1) * 128,
                                     half * 1024:(half + 1) * 1024], "LD")
                    exhi = b.dma("sync", xh,
                                 x1s[1024 + i * 128:1024 + (i + 1) * 128,
                                     half * 1024:(half + 1) * 1024], "LD")
                    b.wait_evt("vector", exhi)
                    b.wait_evt("vector", ev_ymm)
                    vop(lambda e, xl=xl: e.tensor_scalar(
                        out=xl, in0=xl, scalar1=mcols[:, 0:1], scalar2=None,
                        op0=ALU.mult))
                    vop(lambda e, xl=xl, xh=xh: e.scalar_tensor_tensor(
                        out=xl, in0=xh, scalar=mcols[:, 1:2], in1=xl,
                        op0=ALU.mult, op1=ALU.add))
                    vop(lambda e, xl=xl, reg=reg:
                        e.scalar_tensor_tensor(
                            out=xl, in0=pp[:, reg:reg + 1024], scalar=1.0,
                            in1=xl, op0=ALU.mult, op1=ALU.add), inc=True)
                    ev_x2 = b.evt("VE")
                    reg_evac[i % 2] = ev_x2
                    b.wait_evt("scalar", ev_x2)
                    sop(lambda e, xl=xl, half=half, i=i: e.activation(
                        out=f32p[:, S_T1:S_T1 + 1024], in_=xl, func=AF.Square,
                        accum_out=colap(stat, 32 + 16 * half + i)))
                    sop(lambda e, xl=xl, half=half, i=i: e.activation(
                        out=f32p[:, S_T1:S_T1 + 1024], in_=xl, func=AF.Copy,
                        accum_out=colap(stat, 16 * half + i)), inc=True)
                    b.wait_evt("sync", ev_x2)
                    ev_store_prev[i % 2] = b.dma(
                        "sync", x1s[i * 128:(i + 1) * 128,
                                    half * 1024:(half + 1) * 1024], xl, "LD")
                ev_pe_wo = b.evt("PE")
                b.wait_evt("gpsimd", ev_pe_wo)
            ev_x2_stored = b.evt("LD")
            b.wait("vector", "SE")
            ev_ln3 = ln_batch_math(0, 16, 32, 48, 64, 80, WT)
            for i in range(WT):
                xs = xstg[:, (i % 2) * 2048:(i % 2) * 2048 + 2048]
                b.wait("gpsimd", "VE")
                b.wait_evt("gpsimd", ev_x2_stored)
                ex2 = b.dma("gpsimd", xs, x1s[i * 128:(i + 1) * 128, :], "LG")
                b.wait_evt("vector", ex2)
                vop(lambda e, xs=xs, i=i: e.tensor_scalar(
                    out=ztok[:, :], in0=xs, scalar1=colap(stat, 80 + i),
                    scalar2=colap(stat, 64 + i), op0=ALU.mult, op1=ALU.add),
                    inc=True)
                ev_z3 = b.evt("VE")
                b.wait_evt("tensor", ev_z3)
                transpose_tile_set(ztok, i, 1024, 2048)
            ev_z3f = b.evt("VE")
            dbg_store("z3f", actf[:, 1024:2048])

        if upto >= 7:
            for i in range(WT):
                b.wait("tensor", "VE")
                for K in range(KB):
                    mm(pu[:, 256:392],
                       actf[:, K * 2048 + 1024 + i * 128:
                            K * 2048 + 1024 + i * 128 + 128],
                       atgb[:, K * 136:(K + 1) * 136],
                       start=(K == 0), stop=(K == KB - 1), inc=(K == KB - 1))
                ev_tl = b.evt("PE")
                b.wait_evt("vector", ev_tl)
                vop(lambda e: e.tensor_copy(out=ogb[:, 0:128],
                                            in_=pu[:, 256:384]))
                vop(lambda e: e.tensor_copy(out=tpt[:, 8:16],
                                            in_=pu[:, 384:392]), inc=True)
                ev_tg = top2_gate(None)
                b.wait_evt("tensor", ev_tg)
                transpose_batch8([ogb[:, 0:128]],
                                 ttf[:, i * 128:(i + 1) * 128]
                                 .rearrange("p (k n) -> p k n", n=128))
            reg_evac = {0: None, 1: None}
            for half in range(2):
                b.wait("gpsimd", "PE")
                ewf = b.dma("gpsimd",
                            bbuf[:, 0:16384].rearrange("p (k n) -> p k n", n=1024),
                            wfT[:, half * 1024:(half + 1) * 1024]
                            .rearrange("(k p) n -> p k n", p=128), "LG")
                b.wait_evt("tensor", ewf)
                for i in range(WT):
                    reg = (i % 2) * 1024
                    xs = xstg[:, (i % 2) * 2048 + half * 1024:
                              (i % 2) * 2048 + half * 1024 + 1024]
                    b.wait("sync", "VE")
                    exl = b.dma("sync", xs,
                                x1s[i * 128:(i + 1) * 128,
                                    half * 1024:(half + 1) * 1024], "LD")
                    b.wait_evt("tensor", reg_evac[i % 2])
                    for nb in range(2):
                        po = reg + nb * 512
                        for K in range(KB):
                            mm(pp[:, po:po + 512],
                               actf[:, K * 2048 + 1024 + i * 128:
                                    K * 2048 + 1024 + i * 128 + 128],
                               bbuf[:, K * 1024 + nb * 512:
                                    K * 1024 + nb * 512 + 512],
                               start=(K == 0), stop=False)
                        mm(pp[:, po:po + 512], ttf[:, i * 128:(i + 1) * 128],
                           btb[:, half * 1024 + nb * 512:
                               half * 1024 + nb * 512 + 512],
                           start=False, stop=True, inc=True)
                    ev_h = b.evt("PE")
                    b.wait_evt("vector", exl)
                    b.wait_evt("vector", ev_h)
                    vop(lambda e, xs=xs, reg=reg: e.scalar_tensor_tensor(
                        out=xs, in0=pp[:, reg:reg + 1024], scalar=0.5, in1=xs,
                        op0=ALU.mult, op1=ALU.add), inc=True)
                    ev_x3 = b.evt("VE")
                    reg_evac[i % 2] = ev_x3
                    b.wait_evt("sync", ev_x3)
                    b.dma("sync", out_t[i * 128:(i + 1) * 128,
                                        half * 1024:(half + 1) * 1024], xs,
                          "LD")
            b.wait("sync", "LD")
            b.wait("gpsimd", "LG")

        with nc.Block() as blk:
            for eng_name in KBd.ENGINES:
                opslist = b.ops[eng_name]
                def mk(opslist):
                    def run(e):
                        for f in opslist:
                            f(e)
                    return run
                getattr(blk, eng_name)(mk(opslist))

    return nc


def _layout_inputs(inputs, core):
    bidx = core // 2
    half = core % 2
    cat = np.ascontiguousarray
    A_w = inputs["L0_ffn_A_w"]
    gate = inputs["L0_ffn_gate_w"]
    ATg = np.concatenate([A_w.transpose(2, 0, 1).reshape(C, NE * NR),
                          np.asarray(gate).T], axis=1)
    amask = np.zeros((128, 512), np.float32)
    tri = np.triu(np.ones((64, 64), np.float32))
    for cl in range(4):
        par = (cl & 1) * 64
        for hh in range(2):
            blkc = (cl * 2 + hh) * 64
            amask[par:par + 64, blkc:blkc + 64] = tri
    colsify = lambda v: cat(np.asarray(v).reshape(16, 128).T.astype(np.float32))
    m0 = np.full((128, 1), 1.0 if half == 0 else 0.0, np.float32)
    m1 = np.full((128, 1), 0.0 if half == 0 else 1.0, np.float32)
    f32c = lambda a: cat(np.asarray(a).astype(np.float32, copy=False))
    return {
        "x": f32c(inputs["L0_attn_x"][bidx]),
        "WkT": f32c(np.asarray(inputs["L0_attn_Wk"]).T),
        "WvT": f32c(np.asarray(inputs["L0_attn_Wv"]).T),
        "WrT": f32c(np.asarray(inputs["L0_attn_Wr"]).T),
        "WoT": f32c(np.asarray(inputs["L0_attn_Wo"]).T),
        "WfT": f32c(np.asarray(inputs["L0_ffn_weight"]).T),
        "WwT": f32c(np.asarray(inputs["L0_attn_Ww"]).T),
        "Wx": f32c(inputs["L0_attn_Wx"]),
        "ATg": f32c(ATg),
        "BT": f32c(np.asarray(inputs["L0_ffn_B_w"]).transpose(0, 2, 1)
                   .reshape(NE * NR, C)),
        "lns": colsify(inputs["L0_attn_ln_scale"]),
        "lnb": colsify(inputs["L0_attn_ln_bias"]),
        "bw": colsify(inputs["L0_attn_bw"]),
        "m0": m0, "m1": m1,
        "eye": np.eye(128, dtype=np.float32),
        "amask": amask,
    }


def kernel(**inputs):
    nc = build_program()
    in_maps = [_layout_inputs(inputs, c) for c in range(8)]
    res = run_bass_kernel_spmd(nc, in_maps, core_ids=list(range(8)))
    out = np.empty((4, T, C), np.float32)
    for c in range(8):
        bidx, half = c // 2, c % 2
        out[bidx, half * WTOK:(half + 1) * WTOK, :] = res.results[c]["out"]
    return out


# version 06
# speedup vs baseline: 1.1572x; 1.0245x over previous
import numpy as np
from contextlib import ExitStack

import concourse.bass as bass
import concourse.mybir as mybir
from concourse.bass_utils import run_bass_kernel_spmd

F32 = mybir.dt.float32
BF16 = mybir.dt.bfloat16
AF = mybir.ActivationFunctionType
ALU = mybir.AluOpType
AXL = mybir.AxisListType

T = 2048
C = 2048
NT = 16
KB = 16
NG = 16
L = 64
NCH = 32
WTOK = 1024
WT = 8
EPS = 1e-5
CLAMP = 43.0
NE, NR = 8, 16
ALPHA = 16.0

S_ZW, S_LAM, S_U, S_T1, S_T2 = 0, 2048, 4096, 6144, 8192
B_K, B_K2, B_EK, B_RA, B_RI, B_V, B_V2, B_VT, B_KT, B_KAP = (
    0, 2048, 4096, 6144, 8192, 10240, 12288, 14336, 16384, 18432)


class KBd:

    ENGINES = ("sync", "gpsimd", "tensor", "vector", "scalar")

    def __init__(self, sems):
        self.sems = sems
        self.cnt = {k: 0 for k in sems}
        self.ops = {e: [] for e in self.ENGINES}

    def emit(self, eng, fn):
        self.ops[eng].append(fn)

    def wait(self, eng, ch, val=None):
        v = self.cnt[ch] if val is None else val
        if v <= 0:
            return
        h = self.sems[ch]
        self.emit(eng, lambda e: e.wait_ge(h, v))

    def evt(self, ch):
        return (ch, self.cnt[ch])

    def wait_evt(self, eng, evt):
        if evt is None:
            return
        ch, v = evt
        if ch in ("LD", "LG"):
            v = self.cnt[ch]
        if v <= 0:
            return
        h = self.sems[ch]
        self.emit(eng, lambda e: e.wait_ge(h, v))

    def dma(self, qeng, out, in_, ch, **kw):
        h = self.sems[ch]
        self.cnt[ch] += 16
        self.emit(qeng, lambda e: e.dma_start(out=out, in_=in_, **kw)
                  .then_inc(h, 16))
        return (ch, self.cnt[ch])


def build_program(dbg=None, upto=99,
                  p5=("proj", "prep", "tp", "chunks", "cy", "cu", "cs")):
    nc = bass.Bass("TRN2", target_bir_lowering=False,
                   detect_race_conditions=False)

    dt_in = lambda n, s: nc.dram_tensor(n, list(s), F32, kind="ExternalInput").ap()
    x_in = dt_in("x", (T, C))
    wkT = dt_in("WkT", (C, C))
    wvT = dt_in("WvT", (C, C))
    wrT = dt_in("WrT", (C, C))
    woT = dt_in("WoT", (C, C))
    wfT = dt_in("WfT", (C, C))
    wwT = dt_in("WwT", (C, C))
    wx = dt_in("Wx", (C, C))
    atg = dt_in("ATg", (C, NE * NR + NE))
    btv = dt_in("BT", (NE * NR, C))
    lns_i = dt_in("lns", (128, 16))
    lnb_i = dt_in("lnb", (128, 16))
    bw_i = dt_in("bw", (128, 16))
    m0v = dt_in("m0", (128, 1))
    m1v = dt_in("m1", (128, 1))
    eyev = dt_in("eye", (128, 128))
    amaskv = dt_in("amask", (128, 512))

    out_t = nc.dram_tensor("out", [WTOK, C], F32, kind="ExternalOutput").ap()
    dbg_t = None
    if dbg is not None:
        dbg_t = nc.dram_tensor("dbg", list(dbg[1]), F32, kind="ExternalOutput").ap()

    x1s = nc.dram_tensor("x1s", [T, C], F32, kind="Internal").ap()
    mts = nc.dram_tensor("mts", [C, C], BF16, kind="Internal").ap()
    outss = nc.dram_tensor("outss", [C, T], BF16, kind="Internal").ap()

    with ExitStack() as st:
        sb = lambda n, s, d: st.enter_context(nc.sbuf_tensor(n, list(s), d))
        psm = lambda n, s, d: st.enter_context(nc.psum_tensor(n, list(s), d))
        actf = sb("actf", (128, 32768), BF16)
        f32p = sb("f32p", (128, 10240), F32)
        bbuf = sb("bbuf", (128, 22528), BF16)
        ttf = sb("ttf", (128, 2048), BF16)
        btb = sb("btb", (128, 2048), BF16)
        atgb = sb("atgb", (128, 2176), BF16)
        eyeb = sb("eyeb", (128, 128), BF16)
        amask = sb("amask_s", (128, 512), F32)
        ztok = sb("ztok", (128, 2048), BF16)
        wslc = sb("wslc", (128, 4096), BF16)
        asb = sb("asb", (128, 512), BF16)
        ogb = sb("ogb", (128, 512), BF16)
        lncol = sb("lncol", (128, 48), F32)
        mcols = sb("mcols", (128, 2), F32)
        stat = sb("stat", (128, 160), F32)
        ends = sb("ends", (128, 33), F32)
        mcb = sb("mcb", (128, 32), F32)
        eksb = sb("eksb", (128, 32), F32)
        edec = sb("edec", (128, 32), F32)
        s0b = sb("s0b", (128, 128), BF16)
        s1b = sb("s1b", (128, 128), BF16)
        tpt = sb("tpt", (128, 64), F32)
        mstg = sb("mstg", (128, 1024), BF16)
        xstg = sb("xstg", (128, 4096), F32)

        pp = psm("pp", (128, 2048), F32)
        pa = psm("pa", (128, 512), F32)
        py = psm("py", (128, 512), F32)
        ptb = psm("ptb", (128, 1024), BF16)
        pu = psm("pu", (128, 512), F32)

        sems = {}
        for name in ("LD", "LG", "PE", "VE", "SE"):
            sems[name] = st.enter_context(nc.semaphore(name))
        b = KBd(sems)
        S = sems

        def mm(outap, lhs, rhs, start, stop, inc=False, tp=None, sgc=False):
            h = S["PE"]
            if inc:
                b.cnt["PE"] += 1
            def f(e, outap=outap, lhs=lhs, rhs=rhs, start=start, stop=stop,
                  inc=inc, tp=tp, sgc=sgc):
                i = e.matmul(outap, lhs, rhs, start=start, stop=stop,
                             tile_position=tp, skip_group_check=sgc)
                if inc:
                    i.then_inc(h, 1)
            b.emit("tensor", f)

        def tpose(outap, inap, inc=False, tp=None):
            h = S["PE"]
            if inc:
                b.cnt["PE"] += 1
            def f(e, outap=outap, inap=inap, inc=inc, tp=tp):
                i = e.transpose(outap, inap, eyeb[:, :], tile_position=tp)
                if inc:
                    i.then_inc(h, 1)
            b.emit("tensor", f)

        def vop(fn, inc=True):
            h = S["VE"]
            prev = b.cnt["VE"]
            b.cnt["VE"] += 1
            def f(e, fn=fn, prev=prev):
                if prev > 0:
                    e.wait_ge(h, prev)
                fn(e).then_inc(h, 1)
            b.emit("vector", f)

        def sop(fn, inc=False):
            h = S["SE"]
            if inc:
                b.cnt["SE"] += 1
            def f(e, fn=fn, inc=inc):
                i = fn(e)
                if inc:
                    i.then_inc(h, 1)
            b.emit("scalar", f)

        def bcast_ap(t2d, ncols, rep, coloff=0):
            a = t2d[:, coloff:coloff + ncols]
            return bass.AP(tensor=a.tensor, offset=a.offset,
                           ap=[a.ap[0], [a.ap[1][0], ncols], [0, rep]])

        def colap(t2d, j):
            return t2d[:, j:j + 1]

        dbg_rows = [0]
        def dbg_store(name, sbap, nrows=128):
            if dbg is None or dbg[0] != name:
                return
            r = dbg_rows[0]
            b.wait("gpsimd", "VE")
            b.wait("gpsimd", "SE")
            b.wait("gpsimd", "PE")
            b.dma("gpsimd", dbg_t[r:r + nrows, :], sbap, "LG")
            b.wait("gpsimd", "LG")
            dbg_rows[0] += nrows

        def emit_tproj(i, base):
            for K in range(KB):
                mm(pu[:, 256:392],
                   actf[:, K * 2048 + base + i * 128:
                        K * 2048 + base + i * 128 + 128],
                   atgb[:, K * 136:(K + 1) * 136],
                   start=(K == 0), stop=(K == KB - 1), inc=(K == KB - 1))
            return b.evt("PE")

        def emit_gating(ev_tl):
            b.wait_evt("vector", ev_tl)
            vop(lambda e: e.tensor_copy(out=ogb[:, 0:128],
                                        in_=pu[:, 256:384]))
            vop(lambda e: e.tensor_copy(out=tpt[:, 8:16],
                                        in_=pu[:, 384:392]), inc=True)
            return top2_gate(None)

        def emit_ttile(i, ev_tg):
            b.wait_evt("tensor", ev_tg)
            transpose_batch8([ogb[:, 0:128]],
                             ttf[:, i * 128:(i + 1) * 128]
                             .rearrange("p (k n) -> p k n", n=128))
            b.wait_evt("tensor", ptb_evac[0])

        def top2_gate(src_logits_ap):
            vop(lambda e: e.reduce_max(tpt[:, 0:1], tpt[:, 8:16], axis=AXL.X))
            vop(lambda e: e.tensor_scalar(
                out=tpt[:, 16:24], in0=tpt[:, 8:16], scalar1=tpt[:, 0:1],
                scalar2=None, op0=ALU.is_equal))
            vop(lambda e: e.scalar_tensor_tensor(
                out=tpt[:, 24:32], in0=tpt[:, 16:24], scalar=-1e30,
                in1=tpt[:, 8:16], op0=ALU.mult, op1=ALU.add))
            vop(lambda e: e.reduce_max(tpt[:, 1:2], tpt[:, 24:32], axis=AXL.X))
            vop(lambda e: e.tensor_scalar(
                out=tpt[:, 32:40], in0=tpt[:, 24:32], scalar1=tpt[:, 1:2],
                scalar2=None, op0=ALU.is_equal))
            vop(lambda e: e.tensor_tensor(
                out=tpt[:, 2:3], in0=tpt[:, 0:1], in1=tpt[:, 1:2],
                op=ALU.subtract), inc=True)
            ev_d = b.evt("VE")
            b.wait_evt("scalar", ev_d)
            sop(lambda e: e.activation(out=tpt[:, 3:4], in_=tpt[:, 2:3],
                                       func=AF.Exp, scale=-1.0))
            sop(lambda e: e.activation(out=tpt[:, 3:4], in_=tpt[:, 3:4],
                                       func=AF.Ln, bias=1.0))
            sop(lambda e: e.activation(out=tpt[:, 4:5], in_=tpt[:, 3:4],
                                       func=AF.Exp, scale=-1.0), inc=True)
            ev_w1 = b.evt("SE")
            b.wait_evt("vector", ev_w1)
            vop(lambda e: e.tensor_scalar(
                out=tpt[:, 5:6], in0=tpt[:, 4:5], scalar1=-1.0, scalar2=1.0,
                op0=ALU.mult, op1=ALU.add))
            vop(lambda e: e.tensor_scalar(
                out=tpt[:, 16:24], in0=tpt[:, 16:24], scalar1=tpt[:, 4:5],
                scalar2=None, op0=ALU.mult))
            vop(lambda e: e.tensor_scalar(
                out=tpt[:, 32:40], in0=tpt[:, 32:40], scalar1=tpt[:, 5:6],
                scalar2=None, op0=ALU.mult))
            vop(lambda e: e.tensor_tensor(
                out=tpt[:, 16:24], in0=tpt[:, 16:24], in1=tpt[:, 32:40],
                op=ALU.add))
            vop(lambda e: e.tensor_scalar(
                out=tpt[:, 16:24], in0=tpt[:, 16:24], scalar1=ALPHA,
                scalar2=None, op0=ALU.mult))
            for ex in range(NE):
                vop(lambda e, ex=ex: e.tensor_scalar(
                    out=ogb[:, ex * 16:(ex + 1) * 16],
                    in0=ogb[:, ex * 16:(ex + 1) * 16],
                    scalar1=colap(tpt, 16 + ex), scalar2=None, op0=ALU.mult),
                    inc=(ex == NE - 1))
            return b.evt("VE")

        def ln_batch_math(s1a, s1b_, s2a, s2b, mo, ro, n, extra_scale=None,
                          extra_eps=EPS):
            A = lambda o: stat[:, o:o + n]
            if s1b_ is not None:
                vop(lambda e: e.tensor_tensor(out=A(s1a), in0=A(s1a),
                                              in1=A(s1b_), op=ALU.add))
                vop(lambda e: e.tensor_tensor(out=A(s2a), in0=A(s2a),
                                              in1=A(s2b), op=ALU.add))
            vop(lambda e: e.tensor_scalar(out=A(mo), in0=A(s1a),
                                          scalar1=1.0 / C, scalar2=None,
                                          op0=ALU.mult))
            vop(lambda e: e.tensor_scalar(out=A(ro), in0=A(s2a),
                                          scalar1=1.0 / C, scalar2=None,
                                          op0=ALU.mult))
            vop(lambda e: e.tensor_tensor(out=tpt[:, 0:n], in0=A(mo),
                                          in1=A(mo), op=ALU.mult))
            vop(lambda e: e.tensor_tensor(out=A(ro), in0=A(ro),
                                          in1=tpt[:, 0:n], op=ALU.subtract))
            sc = 1.0 if extra_scale is None else extra_scale
            vop(lambda e: e.tensor_scalar(out=A(ro), in0=A(ro),
                                          scalar1=sc, scalar2=extra_eps,
                                          op0=ALU.mult, op1=ALU.add),
                inc=True)
            ev_v = b.evt("VE")
            b.wait_evt("scalar", ev_v)
            sop(lambda e: e.activation(out=tpt[:, 0:n], in_=A(ro),
                                       func=AF.Ln))
            sop(lambda e: e.activation(out=A(ro), in_=tpt[:, 0:n],
                                       func=AF.Exp, scale=-0.5), inc=True)
            ev_s = b.evt("SE")
            b.wait_evt("vector", ev_s)
            vop(lambda e: e.tensor_tensor(out=A(mo), in0=A(mo), in1=A(ro),
                                          op=ALU.mult))
            vop(lambda e: e.tensor_scalar(out=A(mo), in0=A(mo),
                                          scalar1=-1.0, scalar2=None,
                                          op0=ALU.mult), inc=True)
            return b.evt("VE")

        ptb_evac = [None]

        def transpose_batch8(srcs, dst3ap):
            nsl = len(srcs)
            b.wait_evt("tensor", ptb_evac[0])
            for sl, src in enumerate(srcs):
                tpose(ptb[:, sl * 128:(sl + 1) * 128], src,
                      inc=(sl == nsl - 1))
            ev_tp = b.evt("PE")
            b.wait_evt("vector", ev_tp)
            vop(lambda e, dst3ap=dst3ap, nsl=nsl: e.tensor_copy(
                out=dst3ap, in_=ptb[:, 0:nsl * 128]
                .rearrange("p (k n) -> p k n", n=128)), inc=True)
            ptb_evac[0] = b.evt("VE")

        def transpose_tile_set(src_tok, i, base, width):
            for half8 in range(2):
                k0 = half8 * 8
                srcs = [src_tok[:, k * 128:(k + 1) * 128]
                        for k in range(k0, k0 + 8)]
                dst = bass.AP(
                    tensor=actf[:, :].tensor,
                    offset=actf[:, :].offset + (k0 * 2048 + base + i * 128),
                    ap=[actf[:, :].ap[0], [2048, 8], [1, 128]])
                transpose_batch8(srcs, dst)

        b.dma("gpsimd", eyeb[:, :], eyev, "LG")
        b.dma("sync", amask[:, :], amaskv, "LD")
        b.dma("sync", lncol[:, 0:16], lns_i, "LD")
        b.dma("sync", lncol[:, 16:32], lnb_i, "LD")
        b.dma("sync", lncol[:, 32:48], bw_i, "LD")
        b.dma("sync", mcols[:, 0:1], m0v, "LD")
        b.dma("sync", mcols[:, 1:2], m1v, "LD")
        b.dma("gpsimd", btb[:, :], btv, "LG")
        e_atg = b.dma("gpsimd", atgb[:, :].rearrange("p (k n) -> p k n", n=136),
                      atg.rearrange("(k p) n -> p k n", p=128), "LG")
        e_wx = b.dma("gpsimd", actf[:, :].rearrange("p (k n) -> p k n", n=2048),
                     wx.rearrange("(k p) n -> p k n", p=128), "LG")
        ev_consts_ld = b.evt("LD")

        ev_m_stored = None
        ev_pe_m = None
        if upto >= 1:
            bank_evac = {0: None, 1: None}
            for ebank in range(4):
                ewl = b.dma("gpsimd",
                            bbuf[:, 0:8192].rearrange("p (k n) -> p k n", n=512),
                            wwT[:, ebank * 512:(ebank + 1) * 512]
                            .rearrange("(k p) n -> p k n", p=128), "LG")
                b.wait_evt("tensor", ewl)
                if ebank == 0:
                    b.wait_evt("tensor", e_wx)
                for Ct in range(16):
                    pb = Ct % 2
                    b.wait_evt("tensor", bank_evac[pb])
                    for K in range(16):
                        mm(pp[:, pb * 512:(pb + 1) * 512],
                           actf[:, K * 2048 + Ct * 128:
                                K * 2048 + Ct * 128 + 128],
                           bbuf[:, K * 512:(K + 1) * 512],
                           start=(K == 0), stop=(K == 15), inc=(K == 15))
                    ev_pe = b.evt("PE")
                    stg = mstg[:, pb * 512:pb * 512 + 512]
                    b.wait_evt("scalar", ev_pe)
                    b.wait("scalar", "LD")
                    sop(lambda e, stg=stg, pb=pb: e.activation(
                        out=stg, in_=pp[:, pb * 512:pb * 512 + 512],
                        func=AF.Copy), inc=True)
                    ev_se = b.evt("SE")
                    bank_evac[pb] = ev_se
                    b.wait_evt("sync", ev_se)
                    b.dma("sync", mts[Ct * 128:(Ct + 1) * 128,
                                      ebank * 512:(ebank + 1) * 512], stg, "LD")
                ev_pe_m = b.evt("PE")
                b.wait_evt("gpsimd", ev_pe_m)
            ev_m_stored = b.evt("LD")

        if upto >= 2:
            for i in range(NT):
                xs = xstg[:, (i % 2) * 2048:(i % 2) * 2048 + 2048]
                b.wait("gpsimd", "VE")
                b.wait("gpsimd", "SE")
                exl = b.dma("gpsimd", xs, x_in[i * 128:(i + 1) * 128, :], "LG")
                b.wait_evt("vector", exl)
                vop(lambda e, xs=xs, i=i: e.reduce_sum(
                    colap(stat, i), xs, axis=AXL.X), inc=True)
                b.wait_evt("scalar", exl)
                sop(lambda e, xs=xs, i=i: e.activation(
                    out=f32p[:, S_T1:S_T1 + 2048], in_=xs, func=AF.Square,
                    accum_out=colap(stat, 16 + i)), inc=True)
            b.wait("vector", "SE")
            ln_batch_math(0, None, 16, None, 32, 48, NT)
            for i in range(NT):
                xs = xstg[:, (i % 2) * 2048:(i % 2) * 2048 + 2048]
                b.wait("gpsimd", "VE")
                exl = b.dma("gpsimd", xs,
                            x_in[i * 128:(i + 1) * 128, :], "LG")
                b.wait_evt("vector", exl)
                vop(lambda e, xs=xs, i=i: e.tensor_scalar(
                    out=ztok[:, :], in0=xs, scalar1=colap(stat, 48 + i),
                    scalar2=colap(stat, 32 + i), op0=ALU.mult, op1=ALU.add),
                    inc=True)
                ev_z1 = b.evt("VE")
                if i == 0 and ev_pe_m is not None:
                    b.wait_evt("tensor", ev_pe_m)
                b.wait_evt("tensor", ev_z1)
                transpose_tile_set(ztok, i, 0, 2048)
            dbg_store("z1f", actf[:, 0:2048])
            dbg_store("ttf", ttf[:, :])

        if upto >= 3:
            reg_evac = {0: None, 1: None}
            for half in range(2):
                b.wait("gpsimd", "PE")
                ewf = b.dma("gpsimd",
                            bbuf[:, 0:16384].rearrange("p (k n) -> p k n", n=1024),
                            wfT[:, half * 1024:(half + 1) * 1024]
                            .rearrange("(k p) n -> p k n", p=128), "LG")
                b.wait_evt("tensor", ewf)
                for i in range(NT):
                    reg = (i % 2) * 1024
                    xs = xstg[:, (i % 2) * 2048 + half * 1024:
                              (i % 2) * 2048 + half * 1024 + 1024]
                    b.wait("sync", "VE")
                    b.wait("sync", "SE")
                    exl = b.dma("sync", xs,
                                x_in[i * 128:(i + 1) * 128,
                                     half * 1024:(half + 1) * 1024], "LD")
                    if half == 0:
                        ev_tl = emit_tproj(i, 0)
                        ev_tg_pend = emit_gating(ev_tl)
                    b.wait_evt("tensor", reg_evac[i % 2])
                    for nb in range(2):
                        po = reg + nb * 512
                        for K in range(KB):
                            mm(pp[:, po:po + 512],
                               actf[:, K * 2048 + i * 128:
                                    K * 2048 + i * 128 + 128],
                               bbuf[:, K * 1024 + nb * 512:
                                    K * 1024 + nb * 512 + 512],
                               start=(K == 0), stop=False)
                        if half == 0 and nb == 0:
                            emit_ttile(i, ev_tg_pend)
                        mm(pp[:, po:po + 512], ttf[:, i * 128:(i + 1) * 128],
                           btb[:, half * 1024 + nb * 512:
                               half * 1024 + nb * 512 + 512],
                           start=False, stop=True, inc=True)
                    ev_h = b.evt("PE")
                    b.wait_evt("vector", exl)
                    b.wait_evt("vector", ev_h)
                    vop(lambda e, xs=xs, reg=reg:
                        e.scalar_tensor_tensor(
                            out=xs, in0=pp[:, reg:reg + 1024], scalar=0.5,
                            in1=xs, op0=ALU.mult, op1=ALU.add),
                        inc=True)
                    ev_x1 = b.evt("VE")
                    reg_evac[i % 2] = ev_x1
                    b.wait_evt("scalar", ev_x1)
                    sop(lambda e, xs=xs, half=half, i=i: e.activation(
                        out=f32p[:, S_T1:S_T1 + 1024], in_=xs, func=AF.Square,
                        accum_out=colap(stat, 96 + 16 * half + i)))
                    sop(lambda e, xs=xs, half=half, i=i: e.activation(
                        out=f32p[:, S_T1:S_T1 + 1024], in_=xs, func=AF.Copy,
                        accum_out=colap(stat, 64 + 16 * half + i)), inc=True)
                    b.wait_evt("sync", ev_x1)
                    b.dma("sync", x1s[i * 128:(i + 1) * 128,
                                      half * 1024:(half + 1) * 1024], xs, "LD")
                ev_pe_f1 = b.evt("PE")
                b.wait_evt("gpsimd", ev_pe_f1)
            ev_x1_stored = b.evt("LD")
            if dbg is not None and dbg[0] == "x1row":
                b.wait_evt("gpsimd", ev_x1_stored)
                b.dma("gpsimd", xstg[:, 0:2048], x1s[0:128, :], "LG")
                b.wait("gpsimd", "LG")
                dbg_store("x1row", xstg[:, 0:2048])

        if upto >= 4:
            b.wait("vector", "SE")
            ev_ln2 = ln_batch_math(64, 80, 96, 112, 128, 144, NT,
                                   extra_scale=(1.0 + EPS),
                                   extra_eps=EPS * (1.0 + EPS))
            dbg_store("ln2", stat[:, :])
            for i in range(NT):
                xs = xstg[:, (i % 2) * 2048:(i % 2) * 2048 + 2048]
                b.wait("gpsimd", "VE")
                b.wait_evt("gpsimd", ev_x1_stored)
                ex1 = b.dma("gpsimd", xs, x1s[i * 128:(i + 1) * 128, :], "LG")
                b.wait_evt("vector", ex1)
                vop(lambda e, xs=xs, i=i: e.tensor_scalar(
                    out=ztok[:, :], in0=xs, scalar1=colap(stat, 144 + i),
                    scalar2=colap(stat, 128 + i), op0=ALU.mult, op1=ALU.add),
                    inc=True)
                ev_xh = b.evt("VE")
                b.wait_evt("tensor", ev_xh)
                transpose_tile_set(ztok, i, 0, 2048)
            for k in range(KB):
                vop(lambda e, k=k: e.tensor_scalar(
                    out=actf[:, k * 2048:(k + 1) * 2048],
                    in0=actf[:, k * 2048:(k + 1) * 2048],
                    scalar1=colap(lncol, k), scalar2=colap(lncol, 16 + k),
                    op0=ALU.mult, op1=ALU.add), inc=(k == KB - 1))
            ev_xnf = b.evt("VE")
            dbg_store("xnf", actf[:, 0:2048])

        if upto >= 5:
            pp_bank_evac = [None] * 4
            wslc_reader = [None, None]

            def emit_proj(g):
                kvp = (g & 1) * 2048
                ev_ev = {}
                projs = [("zw", None), ("k", wkT), ("rz", wrT), ("v", wvT)]
                for pi, (pname, wsrc) in enumerate(projs):
                    slot = pi % 2
                    wap = wslc[:, slot * 2048:(slot + 1) * 2048]
                    wap3 = wap.rearrange("p (k n) -> p k n", n=128)
                    q = "sync" if wsrc is None else "gpsimd"
                    b.wait_evt(q, wslc_reader[slot])
                    if wsrc is None:
                        b.wait_evt("sync", ev_m_stored)
                        ew = b.dma("sync", wap3,
                                   mts[:, g * 128:(g + 1) * 128]
                                   .rearrange("(k p) n -> p k n", p=128), "LD")
                    else:
                        ew = b.dma("gpsimd", wap3,
                                   wsrc[:, g * 128:(g + 1) * 128]
                                   .rearrange("(k p) n -> p k n", p=128), "LG")
                    b.wait_evt("tensor", ew)
                    if g == 0 and pi == 0:
                        b.wait_evt("tensor", ev_xnf)
                    for bank in range(4):
                        b.wait_evt("tensor", pp_bank_evac[bank])
                        for K in range(KB):
                            mm(pp[:, bank * 512:(bank + 1) * 512],
                               wap[:, K * 128:(K + 1) * 128],
                               actf[:, K * 2048 + bank * 512:
                                    K * 2048 + bank * 512 + 512],
                               start=(K == 0), stop=(K == KB - 1),
                               inc=(K == KB - 1))
                        ev_mm = b.evt("PE")
                        dtile, doff = {"zw": (f32p, S_ZW),
                                       "k": (bbuf, B_K + kvp),
                                       "rz": (f32p, S_U),
                                       "v": (bbuf, B_V + kvp)}[pname]
                        dap = dtile[:, doff + bank * 512:
                                    doff + bank * 512 + 512]
                        if pname == "zw":
                            b.wait_evt("vector", ev_mm)
                            vop(lambda e, dap=dap, bank=bank, g=g:
                                e.tensor_scalar(
                                    out=dap,
                                    in0=pp[:, bank * 512:bank * 512 + 512],
                                    scalar1=colap(lncol, 32 + g),
                                    scalar2=None, op0=ALU.add), inc=True)
                            ev_ev[(pname, bank)] = b.evt("VE")
                        else:
                            b.wait_evt("scalar", ev_mm)
                            sop(lambda e, dap=dap, bank=bank: e.activation(
                                out=dap,
                                in_=pp[:, bank * 512:bank * 512 + 512],
                                func=AF.Copy), inc=True)
                            ev_ev[(pname, bank)] = b.evt("SE")
                        pp_bank_evac[bank] = ev_ev[(pname, bank)]
                    wslc_reader[slot] = b.evt("PE")
                return ev_ev

            ev_ev_next = emit_proj(0)
            for g in range(NG):
                ev_ev = ev_ev_next
                if "prep" not in p5:
                    continue
                zw = f32p[:, S_ZW:S_ZW + 2048]
                lam = f32p[:, S_LAM:S_LAM + 2048]
                uu = f32p[:, S_U:S_U + 2048]
                t1 = f32p[:, S_T1:S_T1 + 2048]
                t2 = f32p[:, S_T2:S_T2 + 2048]
                for bk in range(4):
                    b.wait_evt("scalar", ev_ev[("zw", bk)])
                sop(lambda e: e.activation(out=t1, in_=zw, func=AF.Abs))
                sop(lambda e: e.activation(out=t1, in_=t1, func=AF.Exp,
                                           scale=-1.0))
                sop(lambda e: e.activation(out=t1, in_=t1, func=AF.Ln,
                                           bias=1.0), inc=True)
                ev_ln = b.evt("SE")
                b.wait_evt("vector", ev_ln)
                vop(lambda e: e.tensor_scalar(out=zw, in0=zw, scalar1=0.0,
                                              scalar2=-1.0, op0=ALU.max,
                                              op1=ALU.mult))
                vop(lambda e: e.tensor_tensor(out=zw, in0=zw, in1=t1,
                                              op=ALU.subtract))
                vop(lambda e: e.tensor_tensor_scan(
                    out=lam, data0=zw, data1=zw, initial=0.0, op0=ALU.add,
                    op1=ALU.bypass))
                vop(lambda e: e.memset(ends[:, 0:1], 0.0))
                vop(lambda e: e.tensor_copy(
                    out=ends[:, 1:33],
                    in_=bass.AP(tensor=lam.tensor, offset=lam.offset + 63,
                                ap=[lam.ap[0], [64, 32]])))
                vop(lambda e: e.tensor_tensor(out=mcb[:, :], in0=ends[:, 0:32],
                                              in1=ends[:, 1:33], op=ALU.add))
                vop(lambda e: e.tensor_scalar(out=mcb[:, :], in0=mcb[:, :],
                                              scalar1=0.5, scalar2=None,
                                              op0=ALU.mult))
                vop(lambda e: e.tensor_tensor(out=eksb[:, :],
                                              in0=ends[:, 1:33],
                                              in1=mcb[:, :], op=ALU.subtract))
                vop(lambda e: e.tensor_tensor(out=edec[:, :],
                                              in0=ends[:, 1:33],
                                              in1=ends[:, 0:32],
                                              op=ALU.subtract), inc=True)
                ev_bnd = b.evt("VE")
                b.wait_evt("scalar", ev_bnd)
                sop(lambda e: e.activation(out=eksb[:, :], in_=eksb[:, :],
                                           func=AF.Exp))
                sop(lambda e: e.activation(out=edec[:, :], in_=edec[:, :],
                                           func=AF.Exp), inc=True)
                ev_small = b.evt("SE")
                for bk in range(4):
                    b.wait_evt("scalar", ev_ev[("rz", bk)])
                sop(lambda e: e.activation(out=t2, in_=uu, func=AF.Abs))
                sop(lambda e: e.activation(out=t2, in_=t2, func=AF.Exp,
                                           scale=-1.0))
                sop(lambda e: e.activation(out=t2, in_=t2, func=AF.Ln,
                                           bias=1.0), inc=True)
                ev_ru = b.evt("SE")
                b.wait_evt("vector", ev_ru)
                vop(lambda e: e.tensor_scalar(out=uu, in0=uu, scalar1=0.0,
                                              scalar2=None, op0=ALU.min))
                vop(lambda e: e.tensor_tensor(out=uu, in0=uu, in1=t2,
                                              op=ALU.subtract))
                vop(lambda e: e.tensor_tensor(out=t1, in0=lam,
                                              in1=bcast_ap(mcb, 32, 64),
                                              op=ALU.subtract))
                vop(lambda e: e.tensor_scalar(out=t2, in0=t1, scalar1=-CLAMP,
                                              scalar2=-1.0, op0=ALU.max,
                                              op1=ALU.mult), inc=True)
                ev_ak = b.evt("VE")
                b.wait_evt("scalar", ev_ak)
                sop(lambda e: e.activation(out=bbuf[:, B_EK:B_EK + 2048],
                                           in_=t2, func=AF.Exp), inc=True)
                ev_ek = b.evt("SE")
                vop(lambda e: e.tensor_scalar(out=t1, in0=t1, scalar1=CLAMP,
                                              scalar2=None, op0=ALU.min))
                vop(lambda e: e.tensor_tensor(out=t1, in0=t1, in1=uu,
                                              op=ALU.add), inc=True)
                ev_ar = b.evt("VE")
                b.wait_evt("scalar", ev_ar)
                sop(lambda e: e.activation(out=bbuf[:, B_RA:B_RA + 2048],
                                           in_=t1, func=AF.Exp), inc=True)
                ev_ra = b.evt("SE")
                vop(lambda e: e.tensor_tensor(out=t2, in0=lam,
                                              in1=bcast_ap(ends, 32, 64),
                                              op=ALU.subtract))
                vop(lambda e: e.tensor_tensor(out=t2, in0=t2, in1=uu,
                                              op=ALU.add), inc=True)
                ev_ai = b.evt("VE")
                b.wait_evt("scalar", ev_ai)
                sop(lambda e: e.activation(out=bbuf[:, B_RI:B_RI + 2048],
                                           in_=t2, func=AF.Exp), inc=True)
                ev_ri = b.evt("SE")
                if "tp" not in p5:
                    continue
                kvp = (g & 1) * 2048
                if g + 1 < NG:
                    ev_ev_next = emit_proj(g + 1)
                if g + 1 == NG - 1:
                    pegate = b.evt("PE")
                    b.wait_evt("vector", pegate)
                    ev_sel_early = None
                    for k in range(NG - 2):
                        t0 = mstg[:, 0:1024]
                        t1b = ztok[:, 0:1024]
                        b.wait("sync", "VE")
                        b.wait_evt("sync", b.evt("LD"))
                        el0 = b.dma("sync", t0,
                                    outss[k * 128:(k + 1) * 128, 0:1024],
                                    "LD")
                        el1 = b.dma("sync", t1b,
                                    outss[k * 128:(k + 1) * 128, 1024:2048],
                                    "LD")
                        b.wait_evt("vector", el1)
                        vop(lambda e, t0=t0: e.tensor_scalar(
                            out=t0, in0=t0, scalar1=mcols[:, 0:1],
                            scalar2=None, op0=ALU.mult))
                        vop(lambda e, t0=t0, t1b=t1b, k=k:
                            e.scalar_tensor_tensor(
                                out=actf[:, k * 2048:k * 2048 + 1024],
                                in0=t1b, scalar=mcols[:, 1:2], in1=t0,
                                op0=ALU.mult, op1=ALU.add), inc=True)
                    b.wait_evt("gpsimd", pegate)
                    ev_wo0 = b.dma(
                        "gpsimd",
                        bass.AP(tensor=actf[:, :].tensor,
                                offset=actf[:, :].offset + 1024,
                                ap=[actf[:, :].ap[0], [2048, 16], [1, 1024]]),
                        woT[:, 0:1024].rearrange("(k p) n -> p k n", p=128),
                        "LG")
                for bk in range(4):
                    b.wait_evt("tensor", ev_ev[("v", bk)])
                for j0 in (0, 8):
                    srcs = [bbuf[:, B_V + kvp + j * 128:
                                 B_V + kvp + (j + 1) * 128]
                            for j in range(j0, j0 + 8)]
                    dst = bbuf[:, B_VT + j0 * 128:B_VT + j0 * 128 + 1024]
                    transpose_batch8(
                        srcs, dst.rearrange("p (k n) -> p k n", n=128))
                ev_vt_done = b.evt("VE")
                for bk in range(4):
                    b.wait_evt("vector", ev_ev[("k", bk)])
                b.wait_evt("vector", ev_ek)
                b.wait_evt("vector", ev_small)
                b.wait("vector", "PE")
                vop(lambda e, kvp=kvp: e.tensor_tensor(
                    out=bbuf[:, B_V + kvp:B_V + kvp + 2048],
                    in0=bbuf[:, B_K + kvp:B_K + kvp + 2048],
                    in1=bbuf[:, B_EK:B_EK + 2048], op=ALU.mult))
                vop(lambda e, kvp=kvp: e.tensor_copy(
                    out=bass.AP(tensor=bbuf[:, :].tensor,
                                offset=bbuf[:, :].offset + B_KAP,
                                ap=[[22528, 64], [128, 32], [1, 64]]),
                    in_=bass.AP(tensor=bbuf[:, :].tensor,
                                offset=bbuf[:, :].offset + B_V + kvp,
                                ap=[[22528, 64], [64, 32], [1, 64]])))
                vop(lambda e, kvp=kvp: e.tensor_copy(
                    out=bass.AP(tensor=bbuf[:, :].tensor,
                                offset=bbuf[:, :].offset + 64 * 22528
                                + B_KAP + 64,
                                ap=[[22528, 64], [128, 32], [1, 64]]),
                    in_=bass.AP(tensor=bbuf[:, :].tensor,
                                offset=bbuf[:, :].offset + 64 * 22528
                                + B_V + kvp,
                                ap=[[22528, 64], [64, 32], [1, 64]])))
                vop(lambda e, kvp=kvp: e.tensor_tensor(
                    out=bbuf[:, B_K + kvp:B_K + kvp + 2048],
                    in0=bbuf[:, B_V + kvp:B_V + kvp + 2048],
                    in1=bcast_ap(eksb, 32, 64), op=ALU.mult), inc=True)
                ev_ks = b.evt("VE")
                b.wait_evt("tensor", ev_ks)
                for j0 in (0, 8):
                    srcs = [bbuf[:, B_K + kvp + j * 128:
                                 B_K + kvp + (j + 1) * 128]
                            for j in range(j0, j0 + 8)]
                    dst = bbuf[:, B_KT + j0 * 128:B_KT + j0 * 128 + 1024]
                    transpose_batch8(
                        srcs, dst.rearrange("p (k n) -> p k n", n=128))
                ev_tp_done = b.evt("VE")

                b.wait("vector", "PE")
                vop(lambda e: e.memset(s0b[:, :], 0.0))
                vop(lambda e: e.memset(s1b[:, :], 0.0))
                if g == 0:
                    vop(lambda e: e.memset(pa[:, :], 0.0))
                    vop(lambda e: e.memset(
                        bass.AP(tensor=bbuf[:, :].tensor,
                                offset=bbuf[:, :].offset + B_KAP + 64,
                                ap=[[22528, 64], [128, 32], [1, 64]]), 0.0))
                    vop(lambda e: e.memset(
                        bass.AP(tensor=bbuf[:, :].tensor,
                                offset=bbuf[:, :].offset + 64 * 22528 + B_KAP,
                                ap=[[22528, 64], [128, 32], [1, 64]]), 0.0))
                vop(lambda e: e.memset(ogb[:, 0:1], 0.0), inc=True)
                ev_s0 = b.evt("VE")
                b.wait_evt("tensor", ev_tp_done)
                b.wait_evt("tensor", ev_s0)
                b.wait_evt("tensor", ev_ra)
                b.wait_evt("tensor", ev_ri)

                if "chunks" not in p5:
                    continue
                ev_stt_prev = None
                ev_yev_prev = None
                ev_as_prev = None
                ev_ogb_store = None
                for bat in range(8):
                    c0 = bat * 4
                    b.wait_evt("tensor", ev_as_prev)
                    b.wait_evt("tensor", ev_stt_prev)
                    for c in range(c0, c0 + 4):
                        par = (c & 1) * 64
                        for hh in (0, 1):
                            mm(pa[par:par + 64,
                                  ((c & 3) * 2 + hh) * 64:
                                  ((c & 3) * 2 + hh) * 64 + 64],
                               bbuf[:, B_KAP + c * 128 + hh * 64:
                                    B_KAP + c * 128 + hh * 64 + 64],
                               bbuf[:, B_RA + c * 64:B_RA + c * 64 + 64],
                               start=True, stop=True,
                               inc=(c == c0 + 3 and hh == 1),
                               tp=(0, par))
                    ev_ap = b.evt("PE")
                    b.wait_evt("vector", ev_ap)
                    vop(lambda e: e.tensor_tensor(
                        out=asb[:, :], in0=pa[:, :], in1=amask[:, :],
                        op=ALU.mult), inc=True)
                    ev_as = b.evt("VE")
                    ev_as_prev = ev_as
                    b.wait_evt("tensor", ev_as)
                    if "cy" not in p5:
                        continue
                    if c0 % 8 == 0 and ev_yev_prev is not None:
                        b.wait_evt("tensor", ev_yev_prev)
                    for c in range(c0, c0 + 4):
                        par = (c & 1) * 64
                        pair = c >> 1
                        scur = s0b if c % 2 == 0 else s1b
                        snxt = s1b if c % 2 == 0 else s0b
                        ypos = (c & 7) * 64
                        upsum = pu if c % 2 == 0 else pa
                        for hh, hb in ((0, 0), (1, 64)):
                            mm(upsum[hb:hb + 64, hb:hb + 64],
                               bbuf[par:par + 64,
                                    B_KT + pair * 128 + hh * 64:
                                    B_KT + pair * 128 + hh * 64 + 64],
                               bbuf[par:par + 64,
                                    B_VT + pair * 128 + hh * 64:
                                    B_VT + pair * 128 + hh * 64 + 64],
                               start=True, stop=True, inc=(hh == 1),
                               tp=(par, hb))
                        ev_upd = b.evt("PE")
                        if ev_stt_prev is not None:
                            b.wait_evt("tensor", ev_stt_prev)
                        for hh, hb in ((0, 0), (1, 64)):
                            mm(py[hb:hb + 64, ypos:ypos + 64],
                               bbuf[:, B_VT + pair * 128 + hh * 64:
                                    B_VT + pair * 128 + hh * 64 + 64],
                               asb[:, ((c & 3) * 2 + hh) * 64:
                                   ((c & 3) * 2 + hh) * 64 + 64],
                               start=True, stop=False, tp=(0, hb))
                            mm(py[hb:hb + 64, ypos:ypos + 64],
                               scur[:, hh * 64:hh * 64 + 64],
                               bbuf[:, B_RI + c * 64:B_RI + c * 64 + 64],
                               start=False, stop=True,
                               inc=(hh == 1), tp=(0, hb))
                        ev_y = b.evt("PE")
                        b.wait_evt("vector", ev_upd)
                        vop(lambda e, sc=scur, sn=snxt, c=c, up=upsum:
                            e.scalar_tensor_tensor(
                                out=sn[0:64, 0:64], in0=sc[0:64, 0:64],
                                scalar=edec[0:64, c:c + 1],
                                in1=up[0:64, 0:64],
                                op0=ALU.mult, op1=ALU.add))
                        vop(lambda e, sc=scur, sn=snxt, c=c, up=upsum:
                            e.scalar_tensor_tensor(
                                out=sn[64:128, 64:128], in0=sc[64:128, 64:128],
                                scalar=edec[64:128, c:c + 1],
                                in1=up[64:128, 64:128],
                                op0=ALU.mult, op1=ALU.add), inc=True)
                        ev_stt_prev = b.evt("VE")
                    if "cs" not in p5:
                        continue
                    if bat % 2 == 1:
                        b.wait_evt("vector", ev_y)
                        b.wait_evt("vector", ev_ogb_store)
                        vop(lambda e: e.tensor_copy(out=ogb[:, :],
                                                    in_=py[:, :]), inc=True)
                        ev_yev_prev = b.evt("VE")
                        b.wait_evt("sync", ev_yev_prev)
                        ev_ogb_store = b.dma(
                            "sync", outss[g * 128:(g + 1) * 128,
                                          (c0 - 4) * 64:(c0 - 4) * 64 + 512],
                            ogb[:, :], "LD")
            ev_scan_done = b.evt("LD")

        if upto >= 6:
            b.wait("vector", "PE")
            b.wait("sync", "PE")
            b.wait("sync", "VE")
            b.wait("sync", "SE")
            for k in range(NG - 2, KB):
                t0 = mstg[:, 0:1024]
                t1b = ztok[:, 0:1024]
                b.wait("sync", "VE")
                b.wait_evt("sync", ev_scan_done)
                el0 = b.dma("sync", t0, outss[k * 128:(k + 1) * 128, 0:1024],
                            "LD")
                el1 = b.dma("sync", t1b,
                            outss[k * 128:(k + 1) * 128, 1024:2048], "LD")
                b.wait_evt("vector", el1)
                vop(lambda e, t0=t0: e.tensor_scalar(
                    out=t0, in0=t0, scalar1=mcols[:, 0:1], scalar2=None,
                    op0=ALU.mult))
                vop(lambda e, t0=t0, t1b=t1b, k=k: e.scalar_tensor_tensor(
                    out=actf[:, k * 2048:k * 2048 + 1024], in0=t1b,
                    scalar=mcols[:, 1:2], in1=t0, op0=ALU.mult, op1=ALU.add),
                    inc=True)
            ev_sel = b.evt("VE")
            dbg_store("omf", actf[:, 0:1024])
            reg_evac = {0: None, 1: None}
            ev_sq_prev = None
            ev_store_prev = {0: None, 1: None}
            for half in range(2):
                if half == 1:
                    b.wait("gpsimd", "PE")
                    b.wait_evt("gpsimd", ev_sel)
                    ewo = b.dma("gpsimd",
                                bbuf[:, 0:16384]
                                .rearrange("p (k n) -> p k n", n=1024),
                                woT[:, 1024:2048]
                                .rearrange("(k p) n -> p k n", p=128), "LG")
                    b.wait_evt("tensor", ewo)
                else:
                    b.wait_evt("tensor", ev_wo0)
                    b.wait_evt("tensor", ev_sel)
                for i in range(WT):
                    reg = (i % 2) * 1024
                    b.wait_evt("tensor", reg_evac[i % 2])
                    for nb in range(2):
                        po = reg + nb * 512
                        for K in range(KB):
                            wsrc_ap = (actf[:, K * 2048 + 1024 + nb * 512:
                                            K * 2048 + 1024 + nb * 512 + 512]
                                       if half == 0 else
                                       bbuf[:, K * 1024 + nb * 512:
                                            K * 1024 + nb * 512 + 512])
                            mm(pp[:, po:po + 512],
                               actf[:, K * 2048 + i * 128:
                                    K * 2048 + i * 128 + 128],
                               wsrc_ap,
                               start=(K == 0), stop=(K == KB - 1),
                               inc=(K == KB - 1))
                    ev_ymm = b.evt("PE")
                    xl = xstg[:, (i % 2) * 2048:(i % 2) * 2048 + 1024]
                    xh = xstg[:, (i % 2) * 2048 + 1024:(i % 2) * 2048 + 2048]
                    b.wait("sync", "VE")
                    b.wait("sync", "SE")
                    b.wait_evt("sync", ev_store_prev[i % 2])
                    exlo = b.dma("sync", xl,
                                 x1s[i * 128:(i + 1) * 128,
                                     half * 1024:(half + 1) * 1024], "LD")
                    exhi = b.dma("sync", xh,
                                 x1s[1024 + i * 128:1024 + (i + 1) * 128,
                                     half * 1024:(half + 1) * 1024], "LD")
                    b.wait_evt("vector", exhi)
                    b.wait_evt("vector", ev_ymm)
                    vop(lambda e, xl=xl: e.tensor_scalar(
                        out=xl, in0=xl, scalar1=mcols[:, 0:1], scalar2=None,
                        op0=ALU.mult))
                    vop(lambda e, xl=xl, xh=xh: e.scalar_tensor_tensor(
                        out=xl, in0=xh, scalar=mcols[:, 1:2], in1=xl,
                        op0=ALU.mult, op1=ALU.add))
                    vop(lambda e, xl=xl, reg=reg:
                        e.scalar_tensor_tensor(
                            out=xl, in0=pp[:, reg:reg + 1024], scalar=1.0,
                            in1=xl, op0=ALU.mult, op1=ALU.add), inc=True)
                    ev_x2 = b.evt("VE")
                    reg_evac[i % 2] = ev_x2
                    b.wait_evt("scalar", ev_x2)
                    sop(lambda e, xl=xl, half=half, i=i: e.activation(
                        out=f32p[:, S_T1:S_T1 + 1024], in_=xl, func=AF.Square,
                        accum_out=colap(stat, 32 + 16 * half + i)))
                    sop(lambda e, xl=xl, half=half, i=i: e.activation(
                        out=f32p[:, S_T1:S_T1 + 1024], in_=xl, func=AF.Copy,
                        accum_out=colap(stat, 16 * half + i)), inc=True)
                    b.wait_evt("sync", ev_x2)
                    ev_store_prev[i % 2] = b.dma(
                        "sync", x1s[i * 128:(i + 1) * 128,
                                    half * 1024:(half + 1) * 1024], xl, "LD")
                ev_pe_wo = b.evt("PE")
                b.wait_evt("gpsimd", ev_pe_wo)
            ev_x2_stored = b.evt("LD")
            b.wait("vector", "SE")
            ev_ln3 = ln_batch_math(0, 16, 32, 48, 64, 80, WT)
            for i in range(WT):
                xs = xstg[:, (i % 2) * 2048:(i % 2) * 2048 + 2048]
                b.wait("gpsimd", "VE")
                b.wait_evt("gpsimd", ev_x2_stored)
                ex2 = b.dma("gpsimd", xs, x1s[i * 128:(i + 1) * 128, :], "LG")
                b.wait_evt("vector", ex2)
                vop(lambda e, xs=xs, i=i: e.tensor_scalar(
                    out=ztok[:, :], in0=xs, scalar1=colap(stat, 80 + i),
                    scalar2=colap(stat, 64 + i), op0=ALU.mult, op1=ALU.add),
                    inc=True)
                ev_z3 = b.evt("VE")
                b.wait_evt("tensor", ev_z3)
                transpose_tile_set(ztok, i, 1024, 2048)
            ev_z3f = b.evt("VE")
            dbg_store("z3f", actf[:, 1024:2048])

        if upto >= 7:
            reg_evac = {0: None, 1: None}
            for half in range(2):
                b.wait("gpsimd", "PE")
                ewf = b.dma("gpsimd",
                            bbuf[:, 0:16384].rearrange("p (k n) -> p k n", n=1024),
                            wfT[:, half * 1024:(half + 1) * 1024]
                            .rearrange("(k p) n -> p k n", p=128), "LG")
                b.wait_evt("tensor", ewf)
                for i in range(WT):
                    reg = (i % 2) * 1024
                    xs = xstg[:, (i % 2) * 2048 + half * 1024:
                              (i % 2) * 2048 + half * 1024 + 1024]
                    b.wait("sync", "VE")
                    exl = b.dma("sync", xs,
                                x1s[i * 128:(i + 1) * 128,
                                    half * 1024:(half + 1) * 1024], "LD")
                    if half == 0:
                        ev_tl = emit_tproj(i, 1024)
                        ev_tg_pend = emit_gating(ev_tl)
                    b.wait_evt("tensor", reg_evac[i % 2])
                    for nb in range(2):
                        po = reg + nb * 512
                        for K in range(KB):
                            mm(pp[:, po:po + 512],
                               actf[:, K * 2048 + 1024 + i * 128:
                                    K * 2048 + 1024 + i * 128 + 128],
                               bbuf[:, K * 1024 + nb * 512:
                                    K * 1024 + nb * 512 + 512],
                               start=(K == 0), stop=False)
                        if half == 0 and nb == 0:
                            emit_ttile(i, ev_tg_pend)
                        mm(pp[:, po:po + 512], ttf[:, i * 128:(i + 1) * 128],
                           btb[:, half * 1024 + nb * 512:
                               half * 1024 + nb * 512 + 512],
                           start=False, stop=True, inc=True)
                    ev_h = b.evt("PE")
                    b.wait_evt("vector", exl)
                    b.wait_evt("vector", ev_h)
                    vop(lambda e, xs=xs, reg=reg: e.scalar_tensor_tensor(
                        out=xs, in0=pp[:, reg:reg + 1024], scalar=0.5, in1=xs,
                        op0=ALU.mult, op1=ALU.add), inc=True)
                    ev_x3 = b.evt("VE")
                    reg_evac[i % 2] = ev_x3
                    b.wait_evt("sync", ev_x3)
                    b.dma("sync", out_t[i * 128:(i + 1) * 128,
                                        half * 1024:(half + 1) * 1024], xs,
                          "LD")
            b.wait("sync", "LD")
            b.wait("gpsimd", "LG")

        with nc.Block() as blk:
            for eng_name in KBd.ENGINES:
                opslist = b.ops[eng_name]
                def mk(opslist):
                    def run(e):
                        for f in opslist:
                            f(e)
                    return run
                getattr(blk, eng_name)(mk(opslist))

    return nc


def _layout_inputs(inputs, core):
    bidx = core // 2
    half = core % 2
    cat = np.ascontiguousarray
    A_w = inputs["L0_ffn_A_w"]
    gate = inputs["L0_ffn_gate_w"]
    ATg = np.concatenate([A_w.transpose(2, 0, 1).reshape(C, NE * NR),
                          np.asarray(gate).T], axis=1)
    amask = np.zeros((128, 512), np.float32)
    tri = np.triu(np.ones((64, 64), np.float32))
    for cl in range(4):
        par = (cl & 1) * 64
        for hh in range(2):
            blkc = (cl * 2 + hh) * 64
            amask[par:par + 64, blkc:blkc + 64] = tri
    colsify = lambda v: cat(np.asarray(v).reshape(16, 128).T.astype(np.float32))
    m0 = np.full((128, 1), 1.0 if half == 0 else 0.0, np.float32)
    m1 = np.full((128, 1), 0.0 if half == 0 else 1.0, np.float32)
    f32c = lambda a: cat(np.asarray(a).astype(np.float32, copy=False))
    return {
        "x": f32c(inputs["L0_attn_x"][bidx]),
        "WkT": f32c(np.asarray(inputs["L0_attn_Wk"]).T),
        "WvT": f32c(np.asarray(inputs["L0_attn_Wv"]).T),
        "WrT": f32c(np.asarray(inputs["L0_attn_Wr"]).T),
        "WoT": f32c(np.asarray(inputs["L0_attn_Wo"]).T),
        "WfT": f32c(np.asarray(inputs["L0_ffn_weight"]).T),
        "WwT": f32c(np.asarray(inputs["L0_attn_Ww"]).T),
        "Wx": f32c(inputs["L0_attn_Wx"]),
        "ATg": f32c(ATg),
        "BT": f32c(np.asarray(inputs["L0_ffn_B_w"]).transpose(0, 2, 1)
                   .reshape(NE * NR, C)),
        "lns": colsify(inputs["L0_attn_ln_scale"]),
        "lnb": colsify(inputs["L0_attn_ln_bias"]),
        "bw": colsify(inputs["L0_attn_bw"]),
        "m0": m0, "m1": m1,
        "eye": np.eye(128, dtype=np.float32),
        "amask": amask,
    }


def kernel(**inputs):
    nc = build_program()
    in_maps = [_layout_inputs(inputs, c) for c in range(8)]
    res = run_bass_kernel_spmd(nc, in_maps, core_ids=list(range(8)))
    out = np.empty((4, T, C), np.float32)
    for c in range(8):
        bidx, half = c // 2, c % 2
        out[bidx, half * WTOK:(half + 1) * WTOK, :] = res.results[c]["out"]
    return out
